# version 12
# baseline (speedup 1.0000x reference)
import math
import numpy as np
import ml_dtypes
import concourse.bass as bass
import concourse.mybir as mybir
from concourse.bass_utils import run_bass_kernel_spmd

F32 = mybir.dt.float32
BF16 = mybir.dt.bfloat16
AF = mybir.ActivationFunctionType
ALU = mybir.AluOpType
AX = mybir.AxisListType
BF = ml_dtypes.bfloat16

D = 1024
NCH = 8
HEADS = 16
RLAT = 128
KR = 32
EE = 8
II = 512
OHD = 128
QB = 512
KT = 128
EPS = 1e-5
NEG = -1e30

ENGS = ("tensor", "vector", "scalar", "gpsimd", "sync")


class _Item:
    __slots__ = ("idx", "eng", "emit", "deps", "target", "label", "amt",
                 "semname", "use")

    def __init__(self, idx, eng, emit, label, amt):
        self.idx, self.eng, self.emit, self.label, self.amt = idx, eng, emit, label, amt
        self.deps = []
        self.target = None


class Buf:
    __slots__ = ("name", "last_write", "readers")

    def __init__(self, name):
        self.name = name
        self.last_write = None
        self.readers = []


class Prog:
    def __init__(self):
        self.items = []
        self.fence_items = None
        self.fenced_engines = set()

    EXCL = frozenset(("pA0", "pA1", "pS0", "pS1", "pS2", "pS3", "pY0", "pY1"))

    def add(self, eng, emit, reads=(), writes=(), label="", dma=False):
        it = _Item(len(self.items), eng, emit, label, 16 if dma else 1)
        deps = []
        if self.fence_items is not None and eng not in self.fenced_engines:
            deps.extend(self.fence_items)
            self.fenced_engines.add(eng)
        excl = [b for b in reads if b.name in self.EXCL]
        reads = [b for b in reads if b.name not in self.EXCL]
        writes = tuple(writes) + tuple(excl)
        for b in reads:
            if b.last_write is not None:
                deps.append(b.last_write)
            b.readers.append(it)
        for b in writes:
            if b.last_write is not None:
                deps.append(b.last_write)
            deps.extend(b.readers)
            b.readers = []
            b.last_write = it
        seen = set()
        for d in deps:
            if d is it or d.idx in seen:
                continue
            seen.add(d.idx)
            if d.eng == eng:
                if d.amt == 1 and it.amt == 1 and eng == "tensor":
                    continue
                if d.amt == 1 and it.amt == 16:
                    continue
            it.deps.append(d)
        self.items.append(it)
        return it

    def barrier(self, bufs):
        fence = []
        fb = Buf("_fence")
        for e in ENGS:
            fence.append(self.add(e, lambda h: h.nop(), reads=(), writes=tuple(bufs) + (fb,),
                                  label=f"fence_{e}", dma=False))
        self.fence_items = fence
        self.fenced_engines = set()

    NPOOL = 10

    def emit_all(self, nc):
        dma_cnt = {}
        for it in self.items:
            if it.amt == 16:
                q = it.eng
                idx = dma_cnt.get(q, 0)
                dma_cnt[q] = idx + 1
                it.semname = f"dpool_{q}_{idx % self.NPOOL}"
                it.use = idx // self.NPOOL
                it.target = 16 * (it.use + 1)
            else:
                it.semname = f"sem_{it.eng}"
                it.use = None
        needed = set()
        for it in self.items:
            for d in it.deps:
                needed.add(d.idx)
        running = {}
        for it in self.items:
            if it.amt == 1:
                if it.idx in needed:
                    running[it.eng] = running.get(it.eng, 0) + 1
                    it.target = running[it.eng]
                else:
                    it.target = None
        engs = [e for e in ENGS if any(it.eng == e for it in self.items)]
        semnames = sorted(set(it.semname for it in self.items))
        sem_cms = {n: nc.semaphore(n) for n in semnames}
        sems = {n: cm.__enter__() for n, cm in sem_cms.items()}
        blk_cm = nc.Block()
        block = blk_cm.__enter__()
        by_eng = {e: [it for it in self.items if it.eng == e] for e in engs}

        def mk(eng_items):
            def body(h):
                seen = {}
                for it in eng_items:
                    wmax = {}
                    for d in it.deps:
                        assert d.idx < it.idx and d.target is not None, (d.label, it.label)
                        wmax[d.semname] = max(wmax.get(d.semname, 0), d.target)
                    if it.amt == 16 and it.use > 0:
                        v = 16 * it.use
                        if wmax.get(it.semname, 0) < v:
                            wmax[it.semname] = v
                    for sn in sorted(wmax):
                        if seen.get(sn, -1) >= wmax[sn]:
                            continue
                        h.wait_ge(sems[sn], wmax[sn])
                        seen[sn] = wmax[sn]
                    ins = it.emit(h)
                    if it.target is not None:
                        ins.then_inc(sems[it.semname], it.amt)
            return body

        dec = {"tensor": block.tensor, "vector": block.vector, "scalar": block.scalar,
               "gpsimd": block.gpsimd, "sync": block.sync}
        for e in engs:
            dec[e](mk(by_eng[e]))
        blk_cm.__exit__(None, None, None)
        for n in semnames:
            sem_cms[n].__exit__(None, None, None)


def fe(shape):
    n = 1
    for s in shape[1:]:
        n *= s
    return n


def outer_tables(S):
    j = np.arange(64)
    invf = (10000.0 ** (-(2.0 * j) / 128.0))[:, None]
    ang = (np.arange(S)[None, :] - 1) * invf
    c = np.cos(ang)
    s = np.sin(ang)
    c[:, 0] = 1.0
    s[:, 0] = 0.0
    return (np.concatenate([c, c], 0).astype(np.float32),
            np.concatenate([-s, s], 0).astype(np.float32))


def mla_tables(positions, nblk):
    j = np.arange(16)
    freqs = (100000.0 ** (-(2.0 * j) / 32.0))[:, None]
    ang = positions[None, :].astype(np.float64) * freqs
    c = np.cos(ang)
    s = np.sin(ang)
    blkA = np.concatenate([c, c], 0)
    blkB = np.concatenate([-s, s], 0)
    A = np.concatenate([blkA] * nblk, 0).astype(np.float32)
    Bm = np.concatenate([blkB] * nblk, 0).astype(np.float32)
    return A, Bm


class Builder:
    def __init__(self, S, debug=False):
        self.S = S
        self.debug = debug
        self.NQB = S // QB
        self.NKT = S // KT
        self.NSLOT = self.NQB // 2
        self.OWN_T = self.NSLOT * QB
        self.NOS = self.OWN_T // 128
        self.Ls = [8 * (s + 1) for s in range(self.NSLOT)]
        self.nc = bass.Bass("TRN2", target_bir_lowering=False)
        self.p = Prog()
        self._bufs = {}
        self._arena = []

    def T(self, emit, r=(), w=(), l=""):
        return self.p.add("tensor", emit, r, w, l)

    def V(self, emit, r=(), w=(), l=""):
        return self.p.add("vector", emit, r, w, l)

    def S_(self, emit, r=(), w=(), l=""):
        return self.p.add("scalar", emit, r, w, l)

    def G(self, emit, r=(), w=(), l=""):
        return self.p.add("gpsimd", emit, r, w, l)

    def DS(self, emit, r=(), w=(), l=""):
        return self.p.add("sync", emit, r, w, l, dma=True)

    def DG(self, emit, r=(), w=(), l=""):
        return self.p.add("gpsimd", emit, r, w, l, dma=True)

    def B(self, name):
        if name not in self._bufs:
            self._bufs[name] = Buf(name)
        return self._bufs[name]

    def sb(self, name, shape, dtype):
        cm = self.nc.sbuf_tensor(name, shape, dtype)
        t = cm.__enter__()
        self._arena.append((name, cm))
        return t

    def free_arena(self, names, bufnames=None):
        self.p.barrier(list(self._bufs.values()))
        keep = []
        for name, cm in reversed(self._arena):
            if name in names:
                cm.__exit__(None, None, None)
            else:
                keep.append((name, cm))
        self._arena = list(reversed(keep))

    def ap(self, tile, shape, part0, nparts, free_off, dims):
        return bass.AP(tile, part0 * fe(shape) + free_off, [[fe(shape), nparts]] + list(dims))

    def declare_io(self):
        nc = self.nc
        S = self.S
        d = {}
        def di(name, shape, dtype=F32):
            d[name] = nc.dram_tensor(name, shape, dtype, kind="ExternalInput").ap()
        di("x", [S, D]); di("wq", [OHD, D]); di("wk", [OHD, D]); di("wv", [OHD, D])
        di("wo", [D, OHD]); di("w_ln1", [D]); di("Wq", [D, D]); di("Wkva", [D, RLAT + KR])
        di("Wkvb", [RLAT, HEADS * 32]); di("Wo", [HEADS * 64, D]); di("w_ln2", [D])
        di("Wgate", [D, EE]); di("ebias", [EE]); di("Wg", [EE, D, II]); di("Wu", [EE, D, II])
        di("Wd", [EE, II, D]); di("sWg", [1, D, II]); di("sWu", [1, D, II]); di("sWd", [1, II, D])
        di("Aout", [128, S]); di("Bout", [128, S])
        di("Aq", [128, self.OWN_T]); di("Bq", [128, self.OWN_T])
        di("Ak", [32, S]); di("Bk", [32, S]); di("iden", [128, 128])
        di("orow", [1, self.NQB]); di("padrow", [1, sum(self.Ls)])
        di("masks", [128, 8, QB], BF16)
        self.inp = d
        self.out = nc.dram_tensor("out", [self.OWN_T, D], F32, kind="ExternalOutput").ap()
        k = "ExternalOutput" if self.debug else "Internal"
        self.x1_sp = nc.dram_tensor("x1_sp", [S, D], F32, kind=k).ap()
        self.xn1_sp = nc.dram_tensor("xn1_sp", [S, D], BF16, kind=k).ap()
        self.g_sp = nc.dram_tensor("g_sp", [self.OWN_T, D], F32, kind=k).ap()
        self.comb_dram = nc.dram_tensor("comb_dram", [EE, self.OWN_T], F32, kind=k).ap()
        self.xn2_sp = nc.dram_tensor("xn2_sp", [self.OWN_T, D], BF16, kind=k).ap()

    def alloc_persist(self):
        sb = self.sb
        S = self.S
        self.biasKsl = sb("biasKsl", [128, sum(self.Ls)], F32)
        self.wln1_c = sb("wln1_c", [128, NCH], F32)
        self.wln2_c = sb("wln2_c", [128, NCH], F32)
        self.iden_f = sb("iden_f", [128, 128], F32)
        self.ones_b = sb("ones_b", [128, 2], BF16)
        self.ones_f = sb("ones_f", [128, 128], F32)
        self.negh = sb("negh", [128, 1], BF16)
        self.econst = sb("econst", [128, 1], F32)
        self.ocol = sb("ocol", [128, self.NQB], F32)
        self.ebias_b = sb("ebias_b", [128, EE], F32)
        self.Wgate_s = sb("Wgate_s", [128, NCH, EE], BF16)
        self.vscr = sb("vscr", [128, D], F32)
        self.sqscr = sb("sqscr", [128, D], BF16)
        self.stc = sb("stc", [128, 24], F32)
        self.xto = sb("xto", [128, D], F32)
        self.xnb = sb("xnb", [128, 2, D], BF16)
        self.sqX = sb("sqX", [128, 2, 512], BF16)
        self.nkrow = sb("nkrow", [1, 512], F32)
        self.mxk = sb("mxk", [1, 8], F32)
        self.t1s = sb("t1s", [128, QB], F32)
        self.t2s = sb("t2s", [128, QB], F32)
        self.kvpack = sb("kvpack", [128, 4, S], BF16)
        self.kvt = sb("kvt", [128, self.NKT, HEADS * 33], BF16)
        self.krass = sb("krass", [128, S], BF16)
        self.PAB = self.nc.psum_tensor("PAB", [128, 1024], F32).__enter__()
        self.S4 = self.nc.psum_tensor("S4", [128, 2048], F32).__enter__()
        self.Y2 = self.nc.psum_tensor("Y2", [128, 1024], F32).__enter__()

    def alloc_outer(self):
        sb = self.sb
        S = self.S
        self.WkvaA = sb("WkvaA", [128, NCH, 192], BF16)
        self.WkvbT = sb("WkvbT", [128, 4, 128], BF16)
        self.Wkvb_n = sb("Wkvb_n", [128, HEADS * 32], BF16)
        self.wqX = sb("wqX", [128, NCH, 128], BF16)
        self.wqXs = sb("wqXs", [128, NCH, 128], BF16)
        self.wkX = sb("wkX", [128, NCH, 128], BF16)
        self.wkXs = sb("wkXs", [128, NCH, 128], BF16)
        self.wvT = sb("wvT", [128, NCH, 128], BF16)
        self.woT = sb("woT", [128, NCH, 128], BF16)
        self.qA = sb("qA", [128, S], BF16)
        self.kA = sb("kA", [128, S], BF16)
        self.vscr2 = sb("vscr2", [128, D], F32)
        self.biasG = sb("biasG", [128, 1], F32)
        self.vt = sb("vt", [128, self.NKT, 129], BF16)
        self.x0f = sb("x0f", [128, 2, NCH, 512], BF16)
        self.x0tb = sb("x0tb", [128, 2, D], BF16)
        self.x0tf = sb("x0tf", [128, 1, D], F32)
        self.AB_blk = sb("AB_blk", [128, 2, 512], F32)
        self.ABk_blk = sb("ABk_blk", [32, 2, 512], F32)
        self.wld = sb("wld", [128, D], F32)
        self.wld2 = sb("wld2", [128, 128], F32)
        self.wrow = sb("wrow", [8, 128], F32)
        self.lat = sb("lat", [128, 512], BF16)
        self.expO = sb("expO", [128, 4, 512], BF16)

    OUTER_NAMES = ["WkvaA", "WkvbT", "Wkvb_n",
                   "wqX", "wqXs", "wkX", "wkXs", "wvT", "woT", "qA", "kA",
                   "vscr2", "biasG", "vt", "x0f", "x0tb", "x0tf", "AB_blk",
                   "ABk_blk", "wld", "wld2", "wrow", "lat", "expO"]
    OUTER_BUFS = ["wq0", "wq1", "wk0", "wk1", "wvT", "woT", "qA", "qB", "kA", "kB",
                  "vt", "biasKo", "x0f0", "x0f1", "x0tb0", "x0tb1", "x0tf0", "x0tf1",
                  "ABb", "ABk", "wld", "wld2", "wrow", "lat",
                  "psXq", "psXqs", "psXk", "psXks", "psv", "psQn", "psKc", "psKc2",
                  "psS0", "psS512", "psY1", "psY2", "psA", "exp0", "exp1",
                  "t1", "t2", "sqX", "nkr", "vscr", "sqscr", "stc", "xto", "xsc",
                  "xnb0", "xnb1", "xnsc", "x1sp", "xn1sp", "krass", "kvp", "kvt",
                  "biasKm", "ocol"]

    def alloc_mla(self):
        sb = self.sb
        self.WqA = sb("WqA", [128, NCH, HEADS, 96], BF16)
        self.WoE = sb("WoE", [128, 4, D], BF16)
        self.Qnope = sb("Qnope", [128, 4, QB], BF16)
        self.Qra = sb("Qra", [128, 8, QB], BF16)
        self.expT = sb("expT", [128, 2, 4, QB], BF16)
        self.yasm = sb("yasm", [128, 4, QB], BF16)
        self.rDr = sb("rDr", [1, 4, QB], F32)
        self.Bbc4 = sb("Bbc4", [128, QB], F32)
        self.xn1f = sb("xn1f", [128, 2, 4, NCH, 128], BF16)
        self.x1s2 = sb("x1s2", [128, D], F32)
        self.x1su = sb("x1su", [128, D], F32)
        self.xns2 = sb("xns2", [128, D], BF16)
        self.xnsu = sb("xnsu", [128, 2, D], BF16)
        self.masks_s = sb("masks_s", [128, 8, QB], BF16)
        self.ABq_sl = sb("ABq_sl", [128, 2, QB], F32)
        self.wldM = self.x1su

    MLA_NAMES = ["WqA", "WoE", "Qnope", "Qra", "expT", "yasm", "rDr", "Bbc4",
                 "xn1f", "x1s2", "x1su", "xns2", "xnsu", "masks_s", "ABq_sl",
                 "kvpack", "kvt", "krass"]
    MLA_BUFS = ["WqA", "WoE", "Qnope", "Qra", "ex0", "ex1", "ex2", "ex3", "yasm",
                "rD", "Bbc", "xn1f", "x1qb", "x1qb2", "xnqb", "xnqb2", "masks",
                "ABq", "wldM", "psQ0", "psQ1", "psS0", "psS512", "psS1024",
                "psS1536", "psYm0", "psYm1", "psA", "psKc", "gsp", "kvp", "kvt",
                "krass", "biasKsl", "t1", "t2", "sqX", "vscr", "sqscr", "stc",
                "xto", "xnb0", "xnb1", "xn2f"]

    def alloc_moe(self):
        sb = self.sb
        self.acc = sb("acc", [128, self.NOS, D], F32)
        self.Wg_s = sb("Wg_s", [128, 2, NCH, II], BF16)
        self.Wu_s = sb("Wu_s", [128, 2, NCH, II], BF16)
        self.Wd_s = sb("Wd_s", [128, 2, 4, D], BF16)
        self.comb_t = sb("comb_t", [128, self.NOS, 8], F32)
        self.comb_st = sb("comb_st", [8, 128], F32)
        self.comb_row = sb("comb_row", [1, 512], F32)
        self.gpr = sb("gpr", [128, 4, 8], F32)
        self.gcol = sb("gcol", [128, 8], F32)
        self.sg = sb("sg", [128, 4, 512], BF16)
        self.hu = sb("hu", [128, 4, 512], BF16)
        self.gu = sb("gu", [128, 4, 512], BF16)
        self.Be = sb("Be", [128, 512], F32)
        self.xn2f = sb("xn2f", [128, self.NOS, NCH, 128], BF16)
        self.xe = sb("xe", [128, 2, D], BF16)

    def phase_prep(self):
        B, T, V, S_, G, DS, DG = self.B, self.T, self.V, self.S_, self.G, self.DS, self.DG
        i = self.inp
        DS(lambda e: e.dma_start(self.iden_f[:], i["iden"][:]), w=[B("iden")])
        V(lambda e: e.memset(self.ones_b[:], 1.0), w=[B("ones")])
        V(lambda e: e.memset(self.ones_f[:], 1.0), w=[B("onesf")])
        V(lambda e: e.memset(self.negh[:], -0.5), w=[B("negh")])
        V(lambda e: e.memset(self.econst[:], EPS), w=[B("econst")])
        V(lambda e: e.memset(self.mxk[:], 0.0), w=[B("mxk")])
        V(lambda e: e.memset(
            self.ap(self.kvt, [128, self.NKT, HEADS * 33], 0, 128, 32,
                    [[33, HEADS * self.NKT]]), 1.0), w=[B("kvt")])
        V(lambda e: e.memset(
            self.ap(self.vt, [128, self.NKT, 129], 0, 128, 64,
                    [[129, self.NKT]]), 1.0), w=[B("vt")])
        V(lambda e: e.memset(self.krass[32:64, :], 0.0), w=[B("krass")])
        V(lambda e: e.memset(self.krass[32:33, :], 1.0), r=[B("krass")], w=[B("krass")])
        V(lambda e: e.memset(self.krass[96:128, :], 0.0), r=[B("krass")], w=[B("krass")])
        V(lambda e: e.memset(self.krass[96:97, :], 1.0), r=[B("krass")], w=[B("krass")])

        for nm, dst, bn in (("w_ln1", self.wln1_c, "w_ln1"), ("w_ln2", self.wln2_c, "w_ln2")):
            DS(lambda e, nm=nm: e.dma_start(self.wrow[:], i[nm].rearrange("(c p) -> c p", p=128)),
               r=[B("wrow")], w=[B("wrow")], l=f"ld{nm}")
            T(lambda e: e.transpose(self.PAB[0:128, 0:8], self.wrow[:], self.iden_f[0:8, 0:8]),
              r=[B("wrow"), B("iden"), B("pA0")], w=[B("pA0")], l=f"t{nm}")
            V(lambda e, dst=dst: e.tensor_copy(dst[:, 0:NCH], self.PAB[0:128, 0:8]),
              r=[B("pA0")], w=[B(bn)], l=f"ev{nm}")
        DS(lambda e: e.dma_start(self.wrow[0:1, 0:EE], i["ebias"][None, :]),
           r=[B("wrow")], w=[B("wrow")], l="ldeb")
        T(lambda e: e.matmul(self.PAB[0:128, 256:256 + EE], self.ones_f[0:1, :],
                             self.wrow[0:1, 0:EE], start=True, stop=True),
          r=[B("wrow"), B("onesf"), B("pA0")], w=[B("pA0")], l="mmEb")
        V(lambda e: e.tensor_copy(self.ebias_b[:], self.PAB[0:128, 256:256 + EE]),
          r=[B("pA0")], w=[B("ebias")], l="evEb")
        DS(lambda e: e.dma_start(self.wrow[0:1, 0:self.NQB], i["orow"][:]),
           r=[B("wrow")], w=[B("wrow")], l="ldor")
        T(lambda e: e.matmul(self.PAB[0:128, 256:256 + self.NQB], self.ones_f[0:1, :],
                             self.wrow[0:1, 0:self.NQB], start=True, stop=True),
          r=[B("wrow"), B("onesf"), B("pA0")], w=[B("pA0")], l="mmOr")
        V(lambda e: e.tensor_copy(self.ocol[:], self.PAB[0:128, 256:256 + self.NQB]),
          r=[B("pA0")], w=[B("ocol")], l="evOr")

        def outer_w(name, dstX, dstXs, tag):
            wview = i[name].rearrange("(j two) d -> j two d", two=2)
            for swap in (0, 1):
                dst = dstXs if swap else dstX
                DS(lambda e, swap=swap, wview=wview: e.dma_start(self.wld[0:64, :], wview[:, swap, :]),
                   r=[B("wld")], w=[B("wld")], l=f"{tag}a{swap}")
                DS(lambda e, swap=swap, wview=wview: e.dma_start(self.wld[64:128, :], wview[:, 1 - swap, :]),
                   r=[B("wld")], w=[B("wld")], l=f"{tag}b{swap}")
                for c in range(NCH):
                    T(lambda e, c=c: e.transpose(self.PAB[0:128, 0:128],
                                                 self.wld[:, c * 128:(c + 1) * 128], self.iden_f[:]),
                      r=[B("wld"), B("iden"), B("pA0")], w=[B("pA0")], l=f"t{tag}{swap}{c}")
                    V(lambda e, c=c, dst=dst, tag=tag, swap=swap: e.tensor_copy(
                        dst[:, c, :], self.PAB[0:128, 0:128]),
                      r=[B("pA0")], w=[B(f"{tag}{swap}")], l=f"e{tag}{swap}{c}")
        outer_w("wq", self.wqX, self.wqXs, "wq")
        outer_w("wk", self.wkX, self.wkXs, "wk")
        DS(lambda e: e.dma_start(self.wld[:, :], i["wv"][:]), r=[B("wld")], w=[B("wld")], l="ldwv")
        for c in range(NCH):
            T(lambda e, c=c: e.transpose(self.PAB[0:128, 0:128],
                                         self.wld[:, c * 128:(c + 1) * 128], self.iden_f[:]),
              r=[B("wld"), B("iden"), B("pA0")], w=[B("pA0")], l=f"twv{c}")
            V(lambda e, c=c: e.tensor_copy(self.wvT[:, c, :], self.PAB[0:128, 0:128]),
              r=[B("pA0")], w=[B("wvT")], l=f"evwv{c}")
        for c in range(NCH):
            DS(lambda e, c=c: e.dma_start(self.wld2[:, :], i["wo"][c * 128:(c + 1) * 128, :]),
               r=[B("wld2")], w=[B("wld2")], l=f"ldwo{c}")
            T(lambda e: e.transpose(self.PAB[0:128, 128:256], self.wld2[:, :], self.iden_f[:]),
              r=[B("wld2"), B("iden"), B("pA0")], w=[B("pA0")], l=f"two{c}")
            V(lambda e, c=c: e.tensor_copy(self.woT[:, c, :], self.PAB[0:128, 128:256]),
              r=[B("pA0")], w=[B("woT")], l=f"evwo{c}")
        for c in range(NCH):
            DS(lambda e, c=c: e.dma_start(self.wld[:, 0:RLAT + KR], i["Wkva"][c * 128:(c + 1) * 128, :]),
               r=[B("wld"), B("WkvaA")], w=[B("wld")], l=f"ldkva{c}")
            for (o_d, o_s, dims) in ((0, 0, [[1, 128]]), (128, 128, [[1, 2], [2, 16]]),
                                     (160, 128 + 33, [[2, 16]]), (176, 128 + 32, [[2, 16]])):
                V(lambda e, c=c, o_d=o_d, o_s=o_s, dims=dims: e.tensor_scalar(
                    self.WkvaA[:, c, o_d:o_d + _dims_n(dims)],
                    self.ap(self.wld, [128, D], 0, 128, o_s, dims),
                    self.wln1_c[:, c:c + 1], None, ALU.mult),
                  r=[B("wld"), B("w_ln1")], w=[B("WkvaA")], l=f"gkva{c}_{o_d}")
        DG(lambda e: e.dma_start(self.Wkvb_n[:], i["Wkvb"][:]), w=[B("Wkvb_n")], l="ldKvbN")
        DS(lambda e: e.dma_start(self.wld[:, 0:512], i["Wkvb"][:]),
           r=[B("wld"), B("WkvaA")], w=[B("wld")], l="ldKvbF")
        for c in range(4):
            T(lambda e, c=c: e.transpose(self.PAB[0:128, 0:128],
                                         self.wld[:, c * 128:(c + 1) * 128], self.iden_f[:]),
              r=[B("wld"), B("iden"), B("pA0")], w=[B("pA0")], l=f"tkvb{c}")
            V(lambda e, c=c: e.tensor_copy(self.WkvbT[:, c, :], self.PAB[0:128, 0:128]),
              r=[B("pA0")], w=[B("WkvbT")], l=f"ekvb{c}")
        DG(lambda e: e.dma_start(self.Wgate_s[:], i["Wgate"].rearrange("(c p) e -> p c e", p=128)),
           w=[B("Wgate")], l="ldWgate")

    def phase_outer(self):
        B, T, V, S_, G, DS, DG = self.B, self.T, self.V, self.S_, self.G, self.DS, self.DG
        i = self.inp
        for blk in range(self.NQB):
            pp = blk % 2
            t0 = blk * QB
            DS(lambda e, t0=t0: e.dma_start(self.AB_blk[:, 0, :], i["Aout"][:, t0:t0 + QB]),
               r=[B("ABb")], w=[B("ABb")], l=f"ldA{blk}")
            DS(lambda e, t0=t0: e.dma_start(self.AB_blk[:, 1, :], i["Bout"][:, t0:t0 + QB]),
               r=[B("ABb")], w=[B("ABb")], l=f"ldB{blk}")
            for s_ in range(4):
                ts_ = t0 + s_ * 128
                DG(lambda e, ts_=ts_, pp=pp: e.dma_start(self.x0tb[:, pp, :], i["x"][ts_:ts_ + 128, :]),
                   r=[B(f"x0tb{pp}")], w=[B(f"x0tb{pp}")], l=f"ldx0{blk}_{s_}")
                DS(lambda e, pp=pp, s_=s_: e.dma_start_transpose(
                    self.ap(self.x0f, [128, 2, NCH, 512], 0, 128, pp * NCH * 512 + s_ * 128,
                            [[512, NCH], [1, 128]]),
                    self.x0tb[:, pp, :]),
                   r=[B(f"x0tb{pp}"), B(f"x0f{pp}")], w=[B(f"x0f{pp}")], l=f"tx0{blk}_{s_}")

            def proj(off, wgt, wbuf, tag):
                for c in range(NCH):
                    T(lambda e, c=c, off=off, wgt=wgt, pp=pp: e.matmul(
                        self.S4[0:128, off:off + 512], wgt[:, c, :],
                        self.ap(self.x0f, [128, 2, NCH, 512], 0, 128, pp * NCH * 512 + c * 512,
                                [[1, 512]]),
                        start=(c == 0), stop=(c == NCH - 1)),
                      r=[B(f"x0f{pp}"), B(wbuf)] + ([B(tag)] if c == 0 else []),
                      w=[B(tag)], l=f"{tag}{blk}_{c}")
            proj(0, self.wqX, "wq0", "pS0")
            proj(512, self.wqXs, "wq1", "pS1")
            proj(1024, self.wkX, "wk0", "pS2")
            proj(1536, self.wkXs, "wk1", "pS3")
            for s_ in range(4):
                for c in range(NCH):
                    T(lambda e, s_=s_, c=c, pp=pp: e.matmul(
                        self.Y2[0:128, s_ * 128:(s_ + 1) * 128],
                        self.ap(self.x0f, [128, 2, NCH, 512], 0, 128,
                                pp * NCH * 512 + c * 512 + s_ * 128, [[1, 128]]),
                        self.wvT[:, c, :], start=(c == 0), stop=(c == NCH - 1)),
                      r=[B(f"x0f{pp}"), B("wvT")] + ([B("pY0")] if c == 0 else []),
                      w=[B("pY0")], l=f"v{blk}_{s_}_{c}")
                V(lambda e, s_=s_, blk=blk: e.tensor_copy(
                    self.ap(self.vt, [128, self.NKT, 129], 0, 128, (blk * 4 + s_) * 129,
                            [[65, 2], [1, 64]]),
                    self.Y2[0:128, s_ * 128:(s_ + 1) * 128]),
                  r=[B("pY0"), B("vt")], w=[B("vt")], l=f"ev{blk}_{s_}")

            def rope(psA, psAs, dstA, dstB, bA, bAs, tag):
                V(lambda e, psA=psA: e.tensor_tensor(
                    self.t1s[:, :], self.S4[0:128, psA:psA + 512], self.AB_blk[:, 0, :],
                    ALU.mult),
                  r=[B(bA), B("ABb"), B("t1")], w=[B("t1")], l=f"r1{tag}{blk}")
                V(lambda e, psAs=psAs: e.tensor_tensor(
                    self.t2s[:, :], self.S4[0:128, psAs:psAs + 512], self.AB_blk[:, 1, :],
                    ALU.mult),
                  r=[B(bAs), B("ABb"), B("t2")], w=[B("t2")], l=f"r2{tag}{blk}")
                V(lambda e, dstA=dstA, t0=t0, tag=tag: e.tensor_tensor(
                    dstA[0:64, t0:t0 + 512], self.t1s[0:64, :], self.t2s[0:64, :], ALU.add),
                  r=[B("t1"), B("t2"), B(f"{tag}A")], w=[B(f"{tag}A")], l=f"r3{tag}{blk}")
                V(lambda e, dstA=dstA, t0=t0, tag=tag: e.tensor_tensor(
                    dstA[64:128, t0:t0 + 512], self.t1s[64:128, :], self.t2s[64:128, :], ALU.add),
                  r=[B("t1"), B("t2"), B(f"{tag}A")], w=[B(f"{tag}A")], l=f"r4{tag}{blk}")
            rope(0, 512, self.qA, None, "pS0", "pS1", "q")
            rope(1024, 1536, self.kA, None, "pS2", "pS3", "k")

            S_(lambda e: e.activation(self.sqX[:, 0, :], self.S4[0:128, 0:512], AF.Square),
               r=[B("pS0"), B("sqX")], w=[B("sqX")], l=f"sqQ{blk}")
            T(lambda e: e.matmul(self.PAB[0:1, 0:512], self.ones_b[:, 0:1], self.sqX[:, 0, :],
                                 start=True, stop=True),
              r=[B("sqX"), B("ones"), B("pA0")], w=[B("pA0")], l=f"mmQn{blk}")
            V(lambda e: e.reduce_max(self.mxk[0:1, 6:7], self.PAB[0:1, 0:512], axis=AX.X),
              r=[B("pA0"), B("mxk")], w=[B("mxk")], l=f"qmx{blk}")
            V(lambda e: e.tensor_tensor(self.mxk[0:1, 3:4], self.mxk[0:1, 3:4],
                                        self.mxk[0:1, 6:7], ALU.max),
              r=[B("mxk")], w=[B("mxk")], l=f"qmr{blk}")
            S_(lambda e: e.activation(self.sqX[:, 0, :], self.S4[0:128, 1024:1536], AF.Square),
               r=[B("pS2"), B("sqX")], w=[B("sqX")], l=f"sqK{blk}")
            T(lambda e: e.matmul(self.PAB[0:1, 0:512], self.ones_b[:, 0:1], self.sqX[:, 0, :],
                                 start=True, stop=True),
              r=[B("sqX"), B("ones"), B("pA0")], w=[B("pA0")], l=f"mmKn{blk}")
            V(lambda e: e.reduce_max(self.mxk[0:1, 1:2], self.PAB[0:1, 0:512], axis=AX.X),
              r=[B("pA0"), B("mxk")], w=[B("mxk")], l=f"kmx{blk}")
            V(lambda e: e.tensor_tensor(self.mxk[0:1, 0:1], self.mxk[0:1, 0:1],
                                        self.mxk[0:1, 1:2], ALU.max),
              r=[B("mxk")], w=[B("mxk")], l=f"kmr{blk}")

        V(lambda e: e.tensor_tensor(self.mxk[0:1, 2:3], self.mxk[0:1, 0:1],
                                    self.mxk[0:1, 3:4], ALU.add),
          r=[B("mxk")], w=[B("mxk")], l="gb1")
        V(lambda e: e.tensor_scalar(self.mxk[0:1, 2:3], self.mxk[0:1, 2:3],
                                    -0.5 / math.sqrt(OHD), None, ALU.mult),
          r=[B("mxk")], w=[B("mxk")], l="gb2")
        T(lambda e: e.matmul(self.PAB[0:128, 512:513], self.ones_f[0:1, :],
                             self.mxk[0:1, 2:3], start=True, stop=True),
          r=[B("mxk"), B("onesf"), B("pA1")], w=[B("pA1")], l="gb3")
        V(lambda e: e.tensor_copy(self.biasG[:, :], self.PAB[0:128, 512:513]),
          r=[B("pA1"), B("bG")], w=[B("bG")], l="gb4")

        for qb in range(self.NQB):
            t0 = qb * QB
            def oav(kp2):
                for kt in (2 * kp2, 2 * kp2 + 1):
                    eb = f"expO{kt % 4}"
                    T(lambda e, kt=kt: e.matmul(
                        self.Y2[0:65, 0:512],
                        self.ap(self.vt, [128, self.NKT, 129], 0, 128, kt * 129, [[1, 65]]),
                        self.expO[:, kt % 4, :], start=(kt == 0), stop=(kt == self.NKT - 1)),
                      r=[B(eb), B("vt")] + ([B("pY0")] if kt == 0 else []),
                      w=[B("pY0")], l=f"a1_{qb}_{kt}")
                    T(lambda e, kt=kt: e.matmul(
                        self.Y2[0:64, 512:1024],
                        self.ap(self.vt, [128, self.NKT, 129], 0, 128, kt * 129 + 65, [[1, 64]]),
                        self.expO[:, kt % 4, :], start=(kt == 0), stop=(kt == self.NKT - 1)),
                      r=[B(eb), B("vt")] + ([B("pY1")] if kt == 0 else []),
                      w=[B("pY1")], l=f"a2_{qb}_{kt}")

            for kp in range(self.NKT // 2):
                for kt in (2 * kp, 2 * kp + 1):
                    so = 512 * (kt % 4)
                    pbank = f"pS{kt % 4}"
                    T(lambda e, kt=kt, t0=t0, so=so: e.matmul(
                        self.S4[0:128, so:so + 512], self.kA[:, kt * KT:(kt + 1) * KT],
                        self.qA[:, t0:t0 + 512], start=True, stop=True),
                      r=[B("qA"), B("kA"), B(pbank)], w=[B(pbank)], l=f"s1_{qb}_{kt}")
                b0 = (2 * kp) % 4
                S_(lambda e, b0=b0: e.activation(
                    self.ap(self.expO, [128, 4, 512], 0, 128, b0 * 512, [[1, 1024]]),
                    self.S4[0:128, b0 * 512:(b0 + 2) * 512], AF.Exp,
                    bias=self.biasG[:, :], scale=1.0 / math.sqrt(OHD)),
                   r=[B(f"pS{b0}"), B(f"pS{b0 + 1}"), B("bG"),
                      B(f"expO{b0}"), B(f"expO{b0 + 1}")],
                   w=[B(f"expO{b0}"), B(f"expO{b0 + 1}")], l=f"e_{qb}_{kp}")
                if kp > 0:
                    oav(kp - 1)
            oav(self.NKT // 2 - 1)
            V(lambda e: e.tensor_copy(self.sqX[0:64, 0, :], self.Y2[0:64, 0:512]),
              r=[B("pY0"), B("sqX")], w=[B("sqX")], l=f"ey1_{qb}")
            V(lambda e: e.tensor_copy(self.sqX[64:128, 0, :], self.Y2[0:64, 512:1024]),
              r=[B("pY1"), B("sqX")], w=[B("sqX")], l=f"ey2_{qb}")
            V(lambda e: e.reciprocal(self.nkrow[:], self.Y2[64:65, 0:512]),
              r=[B("pY0"), B("nkr")], w=[B("nkr")], l=f"rD_{qb}")
            for s_ in range(4):
                T(lambda e, s_=s_: e.matmul(
                    self.PAB[0:128, 512 + s_:513 + s_],
                    self.nkrow[0:1, s_ * 128:(s_ + 1) * 128], self.ones_f[0:1, 0:1],
                    start=True, stop=True),
                  r=[B("nkr"), B("onesf")] + ([B("pA1")] if s_ == 0 else []),
                  w=[B("pA1")], l=f"rdc_{qb}_{s_}")
            V(lambda e: e.tensor_copy(self.stc[:, 0:4], self.PAB[0:128, 512:516]),
              r=[B("pA1"), B("stc")], w=[B("stc")], l=f"erdc_{qb}")

            for s_ in range(4):
                sub = qb * 4 + s_
                ts_ = t0 + s_ * 128
                xp = s_ % 2
                DS(lambda e, ts_=ts_: e.dma_start(self.x0tf[:, 0, :], i["x"][ts_:ts_ + 128, :]),
                   r=[B("x0tf")], w=[B("x0tf")], l=f"ldxf{sub}")
                wps = self.PAB
                wb0 = "pA0"
                wb1 = "pA1"
                for half in range(2):
                    T(lambda e, s_=s_, half=half, wps=wps: e.matmul(
                        wps[0:128, half * 512:(half + 1) * 512],
                        self.sqX[:, 0, s_ * 128:(s_ + 1) * 128],
                        self.ap(self.woT, [128, NCH, 128], 0, 128, half * 512, [[1, 512]]),
                        start=True, stop=True),
                      r=[B("sqX"), B("woT"), B(wb0 if half == 0 else wb1)],
                      w=[B(wb0 if half == 0 else wb1)], l=f"wo_{sub}_{half}")
                xp2 = s_ % 2
                vsx = self.vscr if xp2 == 0 else self.vscr2
                vbx = "vscr" if xp2 == 0 else "vscr2"
                ca = 8 if xp2 == 0 else 16
                V(lambda e, s_=s_, wps=wps, vsx=vsx, ca=ca: e.scalar_tensor_tensor(
                    vsx[:, :], wps[0:128, 0:1024], self.stc[:, s_:s_ + 1],
                    self.x0tf[:, 0, :], ALU.mult, ALU.add,
                    accum_out=self.stc[:, ca:ca + 1]),
                  r=[B(wb0), B(wb1), B("stc"), B("x0tf"), B(vbx), B(f"stc{xp2}")],
                  w=[B(vbx), B(f"stc{xp2}")], l=f"vres_{sub}")
                self._ln1(qb, sub)
            self._mla_kv(qb)

    def _ln1(self, qb, sub):
        B, T, V, S_, DS, DG = self.B, self.T, self.V, self.S_, self.DS, self.DG
        st = self.stc
        xp = sub % 2
        cb_ = 8 if xp == 0 else 16
        sb_ = f"stc{xp}"
        vs = self.vscr if xp == 0 else self.vscr2
        vb = "vscr" if xp == 0 else "vscr2"
        c0, c1, c2, c3, c4, c5 = cb_, cb_ + 1, cb_ + 2, cb_ + 3, cb_ + 4, cb_ + 5
        c6, c7 = cb_ + 6, cb_ + 7
        V(lambda e: e.scalar_tensor_tensor(self.sqscr[:, :], vs[:, :], 1.0,
                                           vs[:, :], ALU.mult, ALU.mult,
                                           accum_out=st[:, c1:c1 + 1]),
          r=[B(vb), B("sqscr"), B(sb_)], w=[B("sqscr"), B(sb_)], l=f"v2_{sub}")
        V(lambda e: e.tensor_scalar(st[:, c2:c2 + 1], st[:, c0:c0 + 1], 1.0 / D, None, ALU.mult),
          r=[B(sb_)], w=[B(sb_)], l=f"mu_{sub}")
        V(lambda e: e.tensor_tensor(st[:, c3:c3 + 1], st[:, c2:c2 + 1], st[:, c2:c2 + 1], ALU.mult),
          r=[B(sb_)], w=[B(sb_)], l=f"mu2_{sub}")
        V(lambda e: e.scalar_tensor_tensor(st[:, c3:c3 + 1], st[:, c1:c1 + 1], 1.0 / D,
                                           st[:, c3:c3 + 1], ALU.mult, ALU.subtract),
          r=[B(sb_)], w=[B(sb_)], l=f"var_{sub}")
        S_(lambda e: e.activation(st[:, c4:c4 + 1], st[:, c3:c3 + 1], AF.Ln, bias=EPS),
           r=[B(sb_), B("econst")], w=[B(sb_)], l=f"ln_{sub}")
        S_(lambda e: e.activation(st[:, c4:c4 + 1], st[:, c4:c4 + 1], AF.Exp, scale=-0.5),
           r=[B(sb_)], w=[B(sb_)], l=f"rstd_{sub}")
        V(lambda e: e.tensor_scalar(st[:, c5:c5 + 1], st[:, c3:c3 + 1], EPS, None, ALU.add),
          r=[B(sb_)], w=[B(sb_)], l=f"ve_{sub}")
        V(lambda e: e.reciprocal(st[:, c5:c5 + 1], st[:, c5:c5 + 1]),
          r=[B(sb_)], w=[B(sb_)], l=f"vr_{sub}")
        V(lambda e: e.tensor_tensor(st[:, c5:c5 + 1], st[:, c3:c3 + 1], st[:, c5:c5 + 1], ALU.mult),
          r=[B(sb_)], w=[B(sb_)], l=f"m2_{sub}")
        S_(lambda e: e.activation(st[:, c6:c6 + 1], st[:, c5:c5 + 1], AF.Ln, bias=EPS),
           r=[B(sb_), B("econst")], w=[B(sb_)], l=f"lnr_{sub}")
        S_(lambda e: e.activation(st[:, c6:c6 + 1], st[:, c6:c6 + 1], AF.Exp, scale=-0.5),
           r=[B(sb_)], w=[B(sb_)], l=f"rr_{sub}")
        V(lambda e: e.scalar_tensor_tensor(st[:, c7:c7 + 1], st[:, c2:c2 + 1], -1.0,
                                           st[:, c4:c4 + 1], ALU.mult, ALU.mult),
          r=[B(sb_)], w=[B(sb_)], l=f"nm_{sub}")
        S_(lambda e: e.activation(self.xto[:, :], vs[:, :], AF.Identity,
                                  bias=st[:, c7:c7 + 1], scale=st[:, c4:c4 + 1]),
           r=[B(vb), B(sb_), B("xto")], w=[B("xto")], l=f"x1_{sub}")
        V(lambda e, xp=xp: e.tensor_scalar(self.xnb[:, xp, :], self.xto[:, :],
                                           st[:, c6:c6 + 1], None, ALU.mult),
          r=[B("xto"), B(sb_), B(f"xnb{xp}")], w=[B(f"xnb{xp}")], l=f"xn1_{sub}")
        V(lambda e, qb=qb, vs=vs: e.tensor_scalar(vs[:, :], self.xto[:, :],
                                                  self.ocol[:, qb:qb + 1], None, ALU.mult),
          r=[B("xto"), B("ocol"), B(vb)], w=[B(vb)], l=f"scx_{sub}")
        DS(lambda e, sub=sub, vs=vs: e.dma_start(self.x1_sp[sub * 128:(sub + 1) * 128, :], vs[:, :]),
           r=[B(vb)], w=[B("x1sp")], l=f"spx_{sub}")
        V(lambda e, qb=qb, xp=xp: e.tensor_scalar(self.sqscr[:, 0:D], self.xnb[:, xp, :],
                                                  self.ocol[:, qb:qb + 1], None, ALU.mult),
          r=[B(f"xnb{xp}"), B("ocol"), B("sqscr")], w=[B("sqscr")], l=f"scn_{sub}")
        DS(lambda e, sub=sub: e.dma_start(self.xn1_sp[sub * 128:(sub + 1) * 128, :],
                                          self.sqscr[:, 0:D]),
           r=[B("sqscr")], w=[B("xn1sp")], l=f"spn_{sub}")
        s_ = sub % 4
        DS(lambda e, s_=s_, xp=xp: e.dma_start_transpose(
            self.ap(self.xn1f_oa, [128, 4, NCH, 128], 0, 128, s_ * NCH * 128,
                    [[128, NCH], [1, 128]]),
            self.xnb[:, xp, :]),
           r=[B(f"xnb{xp}"), B("xn1f")], w=[B("xn1f")], l=f"txn_{sub}")

    def _mla_kv(self, qb):
        B, T, V, S_, DS = self.B, self.T, self.V, self.S_, self.DS
        i = self.inp
        t0 = qb * QB
        pp = qb % 2
        DS(lambda e, t0=t0: e.dma_start(self.ABk_blk[:, 0, :], i["Ak"][:, t0:t0 + QB]),
           r=[B("ABk")], w=[B("ABk")], l=f"ldAk{qb}")
        DS(lambda e, t0=t0: e.dma_start(self.ABk_blk[:, 1, :], i["Bk"][:, t0:t0 + QB]),
           r=[B("ABk")], w=[B("ABk")], l=f"ldBk{qb}")
        for c in range(NCH):
            rhs = lambda c=c: self.ap(self.xn1f_oa, [128, 4, NCH, 128], 0, 128, c * 128,
                                      [[NCH * 128, 4], [1, 128]])
            T(lambda e, c=c, rhs=rhs: e.matmul(
                self.PAB[0:128, 0:512],
                self.ap(self.WkvaA, [128, NCH, 192], 0, 128, c * 192, [[1, 128]]),
                rhs(), start=(c == 0), stop=(c == NCH - 1)),
              r=[B("xn1f"), B("WkvaA")] + ([B("pA0")] if c == 0 else []),
              w=[B("pA0")], l=f"lat{qb}_{c}")
            T(lambda e, c=c, rhs=rhs: e.matmul(
                self.PAB[0:32, 512:1024],
                self.ap(self.WkvaA, [128, NCH, 192], 0, 128, c * 192 + 128, [[1, 32]]),
                rhs(), start=(c == 0), stop=(c == NCH - 1)),
              r=[B("xn1f"), B("WkvaA")] + ([B("pA1")] if c == 0 else []),
              w=[B("pA1")], l=f"krX{qb}_{c}")
        for c in range(NCH):
            rhs = lambda c=c: self.ap(self.xn1f_oa, [128, 4, NCH, 128], 0, 128, c * 128,
                                      [[NCH * 128, 4], [1, 128]])
            T(lambda e, c=c, rhs=rhs: e.matmul(
                self.PAB[64:96, 512:1024],
                self.ap(self.WkvaA, [128, NCH, 192], 0, 128, c * 192 + 160, [[1, 32]]),
                rhs(), start=(c == 0), stop=(c == NCH - 1)),
              r=[B("xn1f"), B("WkvaA")], w=[B("pA1")], l=f"krXs{qb}_{c}")
        V(lambda e: e.tensor_copy(self.lat[:, :], self.PAB[0:128, 0:512]),
          r=[B("pA0"), B("lat")], w=[B("lat")], l=f"elat{qb}")
        V(lambda e: e.tensor_tensor(self.t1s[0:32, :], self.PAB[0:32, 512:1024],
                                    self.ABk_blk[:, 0, :], ALU.mult),
          r=[B("pA1"), B("ABk"), B("t1")], w=[B("t1")], l=f"kr1{qb}")
        V(lambda e: e.tensor_tensor(self.t2s[0:32, :], self.PAB[64:96, 512:1024],
                                    self.ABk_blk[:, 1, :], ALU.mult),
          r=[B("pA1"), B("ABk"), B("t2")], w=[B("t2")], l=f"kr2{qb}")
        V(lambda e, t0=t0: e.tensor_tensor(self.krass[0:32, t0:t0 + 512],
                                           self.t1s[0:32, :], self.t2s[0:32, :], ALU.add),
          r=[B("t1"), B("t2"), B("krass")], w=[B("krass")], l=f"kr3{qb}")
        V(lambda e, t0=t0: e.tensor_copy(self.krass[64:96, t0:t0 + 512],
                                         self.krass[0:32, t0:t0 + 512]),
          r=[B("krass")], w=[B("krass")], l=f"kr4{qb}")
        for c in range(4):
            yo = 512 * (c % 2)
            yb = f"pY{c % 2}"
            T(lambda e, c=c, yo=yo: e.matmul(self.Y2[0:128, yo:yo + 512],
                                             self.WkvbT[:, c, :],
                                             self.lat[:, :], start=True, stop=True),
              r=[B("lat"), B("WkvbT"), B(yb)], w=[B(yb)], l=f"kvf{qb}_{c}")
            V(lambda e, c=c, t0=t0, yo=yo: e.tensor_copy(self.kvpack[:, c, t0:t0 + 512],
                                                         self.Y2[0:128, yo:yo + 512]),
              r=[B(yb), B("kvp")], w=[B("kvp")], l=f"ekvf{qb}_{c}")
        for s_ in range(4):
            kt = qb * 4 + s_
            yo = 512 * (s_ % 2)
            yb = f"pY{s_ % 2}"
            T(lambda e, s_=s_, yo=yo: e.matmul(self.Y2[0:128, yo:yo + 512],
                                               self.lat[:, s_ * 128:(s_ + 1) * 128],
                                               self.Wkvb_n[:, :], start=True, stop=True),
              r=[B("lat"), B("Wkvb_n"), B(yb)], w=[B(yb)], l=f"kvt{kt}")
            V(lambda e, kt=kt, yo=yo: e.tensor_copy(
                self.ap(self.kvt, [128, self.NKT, HEADS * 33], 0, 128, kt * HEADS * 33,
                        [[33, 16], [1, 32]]),
                self.Y2[0:128, yo:yo + 512]),
              r=[B(yb), B("kvt")], w=[B("kvt")], l=f"ekvt{kt}")


    def phase_mla(self, mstop=None):
        B, T, V, S_, G, DS, DG = self.B, self.T, self.V, self.S_, self.G, self.DS, self.DG
        i = self.inp
        for c in range(NCH):
            wt = self.x1su if c % 2 == 0 else self.x1s2
            wtb = "x1su" if c % 2 == 0 else "x1s2"
            DS(lambda e, c=c, wt=wt: e.dma_start(wt[:, :], i["Wq"][c * 128:(c + 1) * 128, :]),
               r=[B(wtb), B("WqA")], w=[B(wtb)], l=f"ldWq{c}")
            for (o_d, o_s, dims, n_) in (
                    (0, 32, [[64, HEADS], [1, 2], [2, 16]], 32),
                    (32, 0, [[64, HEADS], [1, 32]], 32),
                    (64, 33, [[64, HEADS], [2, 16]], 16),
                    (80, 32, [[64, HEADS], [2, 16]], 16)):
                V(lambda e, c=c, o_d=o_d, o_s=o_s, dims=dims, n_=n_, wt=wt: e.tensor_scalar(
                    self.ap(self.WqA, [128, NCH, HEADS, 96], 0, 128,
                            c * HEADS * 96 + o_d, [[96, HEADS], [1, n_]]),
                    self.ap(wt, [128, D], 0, 128, o_s, dims),
                    self.wln1_c[:, c:c + 1], None, ALU.mult),
                  r=[B(wtb), B("w_ln1")], w=[B("WqA")], l=f"g{c}_{o_d}")
        for j in range(4):
            DG(lambda e, j=j: e.dma_start(
                self.WoE[:, j, :],
                bass.AP(i["Wo"].tensor, (256 * j) * D, [[64 * D, 4], [D, 32], [1, D]])),
               w=[B("WoE")], l=f"ldWoE{j}")
        DS(lambda e: e.dma_start(self.masks_s[:], i["masks"][:]), w=[B("masks")], l="ldmask")
        DS(lambda e: e.dma_start(self.nkrow[0:1, 0:sum(self.Ls)], i["padrow"][:]),
           r=[B("nkr")], w=[B("nkr")], l="ldpad")
        T(lambda e: e.matmul(self.PAB[0:128, 0:sum(self.Ls)], self.ones_f[0:1, :],
                             self.nkrow[0:1, 0:sum(self.Ls)], start=True, stop=True),
          r=[B("nkr"), B("onesf"), B("pA0")], w=[B("pA0")], l="mmpad")
        V(lambda e: e.tensor_copy(self.biasKsl[:, :], self.PAB[0:128, 0:sum(self.Ls)]),
          r=[B("pA0"), B("biasKsl")], w=[B("biasKsl")], l="bsl")
        V(lambda e: e.memset(self.mxk[0:1, 4:5], 0.0), r=[B("mxk")], w=[B("mxk")], l="zmxm")
        for qb in range(self.NQB):
            t0 = qb * QB
            for c in range(4):
                S_(lambda e, c=c, t0=t0: e.activation(
                    self.sqX[:, 0, :], self.kvpack[:, c, t0:t0 + 512], AF.Square),
                   r=[B("kvp"), B("sqX0")], w=[B("sqX0")], l=f"mks{qb}_{c}")
                T(lambda e, c=c: e.matmul(self.S4[0:1, 0:512], self.ones_b[:, 0:1],
                                          self.sqX[:, 0, :], start=(c == 0), stop=False),
                  r=[B("sqX0"), B("ones")] + ([B("pS0")] if c == 0 else []),
                  w=[B("pS0")], l=f"mkm{qb}_{c}")
            S_(lambda e, t0=t0: e.activation(self.sqX[0:32, 0, :],
                                             self.krass[0:32, t0:t0 + 512], AF.Square),
               r=[B("krass"), B("sqX0")], w=[B("sqX0")], l=f"mksr{qb}")
            T(lambda e: e.matmul(self.S4[0:1, 0:512], self.ones_b[0:32, 0:1],
                                 self.sqX[0:32, 0, :], start=False, stop=True),
              r=[B("sqX0"), B("ones")], w=[B("pS0")], l=f"mkmr{qb}")
            V(lambda e: e.reduce_max(self.mxk[0:1, 5:6], self.S4[0:1, 0:512], axis=AX.X),
              r=[B("pS0"), B("mxk")], w=[B("mxk")], l=f"mkx{qb}")
            V(lambda e: e.tensor_tensor(self.mxk[0:1, 4:5], self.mxk[0:1, 4:5],
                                        self.mxk[0:1, 5:6], ALU.max),
              r=[B("mxk")], w=[B("mxk")], l=f"mkr2{qb}")

        V(lambda e: e.memset(self.Qra[:, :, :], 0.0), r=[B("Qra")], w=[B("Qra")], l="zQra")
        if mstop == "prep":
            return
        def emit_reloads(sn):
            if sn >= self.NSLOT:
                return
            pb = sn % 2
            q0n = sn * QB
            for s_ in range(4):
                xp = s_ % 2
                t0_ = 2 * sn * QB + s_ * 128
                t1_ = (2 * sn + 1) * QB + s_ * 128
                DG(lambda e, t0_=t0_, xp=xp: e.dma_start(
                    self.xnsu[:, xp, :], self.xn1_sp[t0_:t0_ + 128, :]),
                   r=[B("xn1sp"), B(f"xnsu{xp}")], w=[B(f"xnsu{xp}")], l=f"lxnA{sn}_{s_}")
                DG(lambda e, t1_=t1_: e.dma_start(
                    self.xns2[:, :], self.xn1_sp[t1_:t1_ + 128, :]),
                   r=[B("xn1sp"), B("xns2")], w=[B("xns2")], l=f"lxnB{sn}_{s_}")
                V(lambda e, xp=xp: e.tensor_tensor(self.xnsu[:, xp, :], self.xnsu[:, xp, :],
                                                   self.xns2[:, :], ALU.add),
                  r=[B("xns2"), B(f"xnsu{xp}")], w=[B(f"xnsu{xp}")], l=f"sxn{sn}_{s_}")
                DS(lambda e, s_=s_, xp=xp, pb=pb: e.dma_start_transpose(
                    self.ap(self.xn1f, [128, 2, 4, NCH, 128], 0, 128,
                            (pb * 4 + s_) * NCH * 128, [[128, NCH], [1, 128]]),
                    self.xnsu[:, xp, :]),
                   r=[B(f"xnsu{xp}"), B(f"xn1fM{pb}")], w=[B(f"xn1fM{pb}")],
                   l=f"txq{sn}_{s_}")
            DS(lambda e, q0n=q0n: e.dma_start(self.ABq_sl[:, 0, :], i["Aq"][:, q0n:q0n + QB]),
               r=[B("ABq")], w=[B("ABq")], l=f"ldAq{sn}")
            DS(lambda e, q0n=q0n: e.dma_start(self.ABq_sl[:, 1, :], i["Bq"][:, q0n:q0n + QB]),
               r=[B("ABq")], w=[B("ABq")], l=f"ldBq{sn}")

        emit_reloads(0)
        sl_off = 0
        for s in range(self.NSLOT):
            if mstop in ("rel1", "q1", "att1", "slot1") and s > 0:
                break
            L = self.Ls[s]
            pb = s % 2

            if mstop == "rel1":
                continue
            import os
            qheads = int(os.environ.get("K_QHEADS", HEADS))
            qskip = os.environ.get("K_QSKIP", "")
            aug_q = []

            def flush_aug():
                while aug_q:
                    fn = aug_q.pop(0)
                    fn()

            for h in range(qheads):
                hr = h % 4
                qps = (self.PAB, self.PAB, self.S4, self.S4)[hr]
                po = (0, 512, 512, 1024)[hr]
                qbn = ("pA0", "pA1", "pS1", "pS2")[hr]
                hp = h % 2
                for c in range(NCH):
                    T(lambda e, c=c, h=h, po=po, qps=qps, pb=pb: e.matmul(
                        qps[0:96, po:po + 512],
                        self.ap(self.WqA, [128, NCH, HEADS, 96], 0, 128, (c * HEADS + h) * 96,
                                [[1, 96]]),
                        self.ap(self.xn1f, [128, 2, 4, NCH, 128], 0, 128,
                                pb * 4 * NCH * 128 + c * 128,
                                [[NCH * 128, 4], [1, 128]]),
                        start=(c == 0), stop=(c == NCH - 1)),
                      r=[B(f"xn1fM{pb}"), B("WqA")] + ([B(qbn)] if c == 0 else []),
                      w=[B(qbn)], l=f"q{s}_{h}_{c}")
                m = h % 4
                j = h // 4
                if "n" not in qskip:
                    V(lambda e, m=m, j=j, po=po, qps=qps: e.tensor_copy(
                        self.Qnope[32 * m:32 * m + 32, j, :], qps[32:64, po:po + 512]),
                      r=[B(qbn), B("Qnope")], w=[B("Qnope")], l=f"eqn{s}_{h}")
                rb = 64 * (h % 2)
                cb = h // 2
                if "r" not in qskip:
                    V(lambda e, po=po, qps=qps: e.tensor_tensor(
                        self.t1s[0:32, :], qps[0:32, po:po + 512], self.ABq_sl[0:32, 0, :],
                        ALU.mult),
                      r=[B(qbn), B("ABq"), B("t1")], w=[B("t1")], l=f"qr1{s}_{h}")
                    V(lambda e, po=po, qps=qps: e.tensor_tensor(
                        self.t2s[0:32, :], qps[64:96, po:po + 512], self.ABq_sl[0:32, 1, :],
                        ALU.mult),
                      r=[B(qbn), B("ABq"), B("t2")], w=[B("t2")], l=f"qr2{s}_{h}")
                    V(lambda e, rb=rb, cb=cb: e.tensor_tensor(
                        self.Qra[rb:rb + 32, cb, :], self.t1s[0:32, :], self.t2s[0:32, :], ALU.add),
                      r=[B("t1"), B("t2"), B("Qra")], w=[B("Qra")], l=f"qr3{s}_{h}")
                if "a" not in qskip:
                    S_(lambda e, po=po, hp=hp, qps=qps: e.activation(
                        self.sqX[0:64, hp, :], qps[0:64, po:po + 512], AF.Square),
                       r=[B(qbn), B(f"sqX{hp}")], w=[B(f"sqX{hp}")], l=f"qs{s}_{h}")

                    def aug(hp=hp, rb=rb, cb=cb, s=s, h=h):
                        T(lambda e, hp=hp: e.matmul(
                            self.S4[0:1, 1536:2048], self.ones_b[0:64, 0:1],
                            self.sqX[0:64, hp, :], start=True, stop=True),
                          r=[B(f"sqX{hp}"), B("ones"), B("pS3")], w=[B("pS3")],
                          l=f"qn{s}_{h}")
                        V(lambda e, rb=rb, cb=cb: e.tensor_scalar(
                            self.Qra[rb + 32:rb + 33, cb, :], self.S4[0:1, 1536:2048],
                            self.mxk[0:1, 4:5], -0.5, ALU.add, ALU.mult),
                          r=[B("pS3"), B("mxk"), B("Qra")], w=[B("Qra")], l=f"eqa{s}_{h}")
                    aug_q.append(aug)
                    if len(aug_q) > 1:
                        aug_q.pop(0)()
            flush_aug()
            emit_reloads(s + 1)

            if mstop == "q1":
                continue
            for j in range(4):
                def avs(kt, first, last):
                    ep = kt % 2
                    for m in range(4):
                        h = 4 * j + m
                        ypt = (self.PAB, self.PAB, self.Y2, self.Y2)[m]
                        yb = 512 * (m % 2)
                        ybn = ("pA0", "pA1", "pY0", "pY1")[m]
                        T(lambda e, m=m, kt=kt, h=h, ypt=ypt, yb=yb, ep=ep,
                          first=first, last=last: e.matmul(
                              ypt[0:33, yb:yb + 512],
                              self.ap(self.kvt, [128, self.NKT, HEADS * 33], 0, 128,
                                      kt * HEADS * 33 + 33 * h, [[1, 33]]),
                              self.expT[:, ep, m, :], start=first, stop=last),
                          r=[B(f"ex{ep}{m}"), B("kvt")] + ([B(ybn)] if first else []),
                          w=[B(ybn)], l=f"a{s}_{j}_{kt}_{m}")
                for kt in range(L):
                    mskd = (kt >= 8 * s)
                    ep = kt % 2
                    for m in range(4):
                        h = 4 * j + m
                        so = 512 * m
                        psn = f"pS{m}"
                        T(lambda e, m=m, j=j, kt=kt, so=so: e.matmul(
                            self.S4[0:128, so:so + 512],
                            self.kvpack[32 * m:32 * m + 32, j, kt * KT:(kt + 1) * KT],
                            self.Qnope[32 * m:32 * m + 32, j, :],
                            start=True, stop=False, tile_position=(32 * m, 0)),
                          r=[B("kvp"), B("Qnope"), B(psn)], w=[B(psn)], l=f"n{s}_{j}_{kt}_{m}")
                        rb = 64 * (m % 2)
                        cb = 2 * j + m // 2
                        T(lambda e, rb=rb, cb=cb, kt=kt, so=so: e.matmul(
                            self.S4[0:128, so:so + 512],
                            self.krass[rb:rb + 64, kt * KT:(kt + 1) * KT],
                            self.Qra[rb:rb + 64, cb, :],
                            start=False, stop=True, tile_position=(rb, 0)),
                          r=[B("krass"), B("Qra")], w=[B(psn)], l=f"r{s}_{j}_{kt}_{m}")
                        if m % 2 == 1:
                            S_(lambda e, m=m, kt=kt, sl_off=sl_off, ep=ep: e.activation(
                                self.ap(self.expT, [128, 2, 4, QB], 0, 128,
                                        (ep * 4 + m - 1) * QB, [[1, 2 * QB]]),
                                self.S4[0:128, (m - 1) * 512:(m + 1) * 512], AF.Exp,
                                bias=self.biasKsl[:, sl_off + kt:sl_off + kt + 1], scale=0.125),
                               r=[B(f"pS{m-1}"), B(psn), B("biasKsl"),
                                  B(f"ex{ep}{m-1}"), B(f"ex{ep}{m}")],
                               w=[B(f"ex{ep}{m-1}"), B(f"ex{ep}{m}")],
                               l=f"x{s}_{j}_{kt}_{m}")
                    if mskd:
                        md = kt - 8 * s
                        V(lambda e, md=md, ep=ep: e.tensor_tensor(
                            self.ap(self.expT, [128, 2, 4, QB], 0, 128, ep * 4 * QB,
                                    [[1, 4 * QB]]),
                            self.ap(self.expT, [128, 2, 4, QB], 0, 128, ep * 4 * QB,
                                    [[1, 4 * QB]]),
                            self.ap(self.masks_s, [128, 8, QB], 0, 128, md * QB,
                                    [[0, 4], [1, QB]]),
                            ALU.mult),
                          r=[B("masks")] + [B(f"ex{ep}{m}") for m in range(4)],
                          w=[B(f"ex{ep}{m}") for m in range(4)], l=f"m{s}_{j}_{kt}")
                    if kt > 0:
                        avs(kt - 1, kt - 1 == 0, False)
                avs(L - 1, L == 1, True)
                ypts = (self.PAB, self.PAB, self.Y2, self.Y2)
                ybs = (0, 512, 0, 512)
                ybns = ("pA0", "pA1", "pY0", "pY1")
                for m in range(4):
                    V(lambda e, m=m: e.reciprocal(
                        self.rDr[0:1, m, :], ypts[m][32:33, ybs[m]:ybs[m] + 512]),
                      r=[B(ybns[m]), B("rD")], w=[B("rD")], l=f"rd{s}_{j}_{m}")
                for m in range(4):
                    T(lambda e, m=m: e.matmul(
                        self.S4[0:32, 512 * m:512 * m + 512], self.ones_f[0:1, 0:32],
                        self.rDr[0:1, m, :], start=True, stop=True),
                      r=[B("rD"), B("onesf"), B(f"pS{m}")], w=[B(f"pS{m}")],
                      l=f"bc{s}_{j}_{m}")
                for m in range(4):
                    V(lambda e, m=m: e.tensor_copy(self.Bbc4[32 * m:32 * m + 32, :],
                                                   self.S4[0:32, 512 * m:512 * m + 512]),
                      r=[B(f"pS{m}"), B("Bbc")], w=[B("Bbc")], l=f"ebc{s}_{j}_{m}")
                for m in range(4):
                    V(lambda e, m=m, j=j: e.tensor_tensor(
                        self.yasm[32 * m:32 * m + 32, j, :],
                        ypts[m][0:32, ybs[m]:ybs[m] + 512],
                        self.Bbc4[32 * m:32 * m + 32, :], ALU.mult),
                      r=[B(ybns[m]), B("Bbc"), B("yasm")], w=[B("yasm")],
                      l=f"ny{s}_{j}_{m}")

            if mstop == "att1":
                continue
            for s_ in range(4):
                t0_ = 2 * s * QB + s_ * 128
                t1_ = (2 * s + 1) * QB + s_ * 128
                DS(lambda e, t0_=t0_: e.dma_start(
                    self.x1su[:, :], self.x1_sp[t0_:t0_ + 128, :]),
                   r=[B("x1sp"), B("x1su")], w=[B("x1su")], l=f"lx1A{s}_{s_}")
                DS(lambda e, t1_=t1_: e.dma_start(
                    self.x1s2[:, :], self.x1_sp[t1_:t1_ + 128, :]),
                   r=[B("x1sp"), B("x1s2")], w=[B("x1s2")], l=f"lx1B{s}_{s_}")
                V(lambda e: e.tensor_tensor(self.x1su[:, :], self.x1su[:, :],
                                            self.x1s2[:, :], ALU.add),
                  r=[B("x1s2"), B("x1su")], w=[B("x1su")], l=f"sx1{s}_{s_}")
                for j in range(4):
                    for half in range(2):
                        T(lambda e, s_=s_, j=j, half=half: e.matmul(
                            self.PAB[0:128, half * 512:(half + 1) * 512],
                            self.yasm[:, j, s_ * 128:(s_ + 1) * 128],
                            self.WoE[:, j, half * 512:(half + 1) * 512],
                            start=(j == 0), stop=(j == 3)),
                          r=[B("yasm"), B("WoE")] + ([B(f"pA{half}")] if j == 0 else []),
                          w=[B(f"pA{half}")], l=f"mo{s}_{s_}_{j}_{half}")
                V(lambda e: e.scalar_tensor_tensor(
                    self.vscr[:, :], self.PAB[0:128, 0:1024], 1.0, self.x1su[:, :],
                    ALU.mult, ALU.add),
                  r=[B("pA0"), B("pA1"), B("x1su"), B("vscr")], w=[B("vscr")], l=f"y1{s}_{s_}")
                V(lambda e: e.tensor_tensor(self.xto[:, :], self.vscr[:, :],
                                            self.x1su[:, :], ALU.add),
                  r=[B("vscr"), B("x1su"), B("xto")], w=[B("xto")], l=f"g{s}_{s_}")
                DS(lambda e, s=s, s_=s_: e.dma_start(
                    self.g_sp[(s * 4 + s_) * 128:(s * 4 + s_ + 1) * 128, :], self.xto[:, :]),
                   r=[B("xto")], w=[B("gsp")], l=f"spg{s}_{s_}")
                V(lambda e: e.scalar_tensor_tensor(self.sqscr[:, :], self.vscr[:, :], 1.0,
                                                   self.vscr[:, :], ALU.mult, ALU.mult,
                                                   accum_out=self.stc[:, 9:10]),
                  r=[B("vscr"), B("sqscr"), B("stc")], w=[B("sqscr"), B("stc")], l=f"y2{s}_{s_}")
                V(lambda e: e.tensor_scalar(self.stc[:, 10:11], self.stc[:, 9:10], 1.0 / D,
                                            None, ALU.mult),
                  r=[B("stc")], w=[B("stc")], l=f"ms{s}_{s_}")
                S_(lambda e: e.activation(self.stc[:, 11:12], self.stc[:, 10:11], AF.Ln,
                                          bias=EPS),
                   r=[B("stc"), B("econst")], w=[B("stc")], l=f"l2{s}_{s_}")
                S_(lambda e: e.activation(self.stc[:, 11:12], self.stc[:, 11:12], AF.Exp,
                                          scale=-0.5),
                   r=[B("stc")], w=[B("stc")], l=f"r2{s}_{s_}")
                xp = s_ % 2
                V(lambda e, xp=xp: e.tensor_scalar(self.xnb[:, xp, :], self.vscr[:, :],
                                                   self.stc[:, 11:12], None, ALU.mult),
                  r=[B("vscr"), B("stc"), B(f"xnb{xp}")], w=[B(f"xnb{xp}")], l=f"x2{s}_{s_}")
                so_ = s * 4 + s_
                DG(lambda e, so_=so_, xp=xp: e.dma_start(
                    self.xn2_sp[so_ * 128:(so_ + 1) * 128, :], self.xnb[:, xp, :]),
                   r=[B(f"xnb{xp}")], w=[B("xn2sp")], l=f"sp2{s}_{s_}")
            sl_off += L

    def phase_moe(self):
        B, T, V, S_, G, DS, DG = self.B, self.T, self.V, self.S_, self.G, self.DS, self.DG
        i = self.inp
        NOS = self.NOS
        for ss in range(NOS):
            xp = ss % 2
            DG(lambda e, ss=ss, xp=xp: e.dma_start(
                self.xe[:, xp, :], self.xn2_sp[ss * 128:(ss + 1) * 128, :]),
               r=[B("xn2sp"), B(f"xe{xp}")], w=[B(f"xe{xp}")], l=f"lx2{ss}")
            DS(lambda e, ss=ss, xp=xp: e.dma_start_transpose(
                self.ap(self.xn2f, [128, self.NOS, NCH, 128], 0, 128, ss * NCH * 128,
                        [[128, NCH], [1, 128]]),
                self.xe[:, xp, :]),
               r=[B(f"xe{xp}"), B("xn2f")], w=[B("xn2f")], l=f"tx2{ss}")
            for c in range(NCH):
                V(lambda e, ss=ss, c=c: e.tensor_scalar(
                    self.xn2f[:, ss, c, :], self.xn2f[:, ss, c, :],
                    self.wln2_c[:, c:c + 1], None, ALU.mult),
                  r=[B("xn2f"), B("w_ln2")], w=[B("xn2f")], l=f"w2{ss}_{c}")
        for ss in range(NOS):
            for c in range(NCH):
                T(lambda e, ss=ss, c=c: e.matmul(
                    self.S4[0:128, 1536:1536 + EE],
                    self.ap(self.xn2f, [128, NOS, NCH, 128], 0, 128, (ss * NCH + c) * 128,
                            [[1, 128]]),
                    self.Wgate_s[:, c, :], start=(c == 0), stop=(c == NCH - 1)),
                  r=[B("xn2f"), B("Wgate")] + ([B("pS3")] if c == 0 else []),
                  w=[B("pS3")], l=f"gm{ss}_{c}")
            gp = self.gpr
            V(lambda e: e.tensor_tensor(gp[:, 0, :], self.S4[0:128, 1536:1536 + EE],
                                        self.ebias_b[:, :], ALU.add),
              r=[B("pS3"), B("ebias"), B("gp")], w=[B("gp")], l=f"gb{ss}")
            V(lambda e: e.reduce_max(self.gcol[:, 0:1], gp[:, 0, :], axis=AX.X, negate=True),
              r=[B("gp"), B("gc")], w=[B("gc")], l=f"gx{ss}")
            S_(lambda e: e.activation(gp[:, 1, :], gp[:, 0, :], AF.Exp, bias=self.gcol[:, 0:1]),
               r=[B("gp"), B("gc")], w=[B("gp")], l=f"ge{ss}")
            V(lambda e: e.reduce_max(self.gcol[:, 1:2], gp[:, 1, :], axis=AX.X),
              r=[B("gp"), B("gc")], w=[B("gc")], l=f"m1{ss}")
            V(lambda e: e.tensor_scalar(gp[:, 2, :], gp[:, 1, :], self.gcol[:, 1:2], None,
                                        ALU.is_ge),
              r=[B("gp"), B("gc")], w=[B("gp")], l=f"q1{ss}")
            V(lambda e: e.tensor_tensor(gp[:, 3, :], gp[:, 1, :], gp[:, 2, :], ALU.mult),
              r=[B("gp")], w=[B("gp")], l=f"tt{ss}")
            V(lambda e: e.tensor_tensor(gp[:, 3, :], gp[:, 1, :], gp[:, 3, :], ALU.subtract),
              r=[B("gp")], w=[B("gp")], l=f"e2{ss}")
            V(lambda e: e.reduce_max(self.gcol[:, 2:3], gp[:, 3, :], axis=AX.X),
              r=[B("gp"), B("gc")], w=[B("gc")], l=f"m2{ss}")
            V(lambda e: e.tensor_scalar(gp[:, 3, :], gp[:, 3, :], self.gcol[:, 2:3], None,
                                        ALU.is_ge),
              r=[B("gp"), B("gc")], w=[B("gp")], l=f"q2{ss}")
            V(lambda e: e.tensor_tensor(gp[:, 2, :], gp[:, 2, :], gp[:, 3, :], ALU.add),
              r=[B("gp")], w=[B("gp")], l=f"qs{ss}")
            V(lambda e: e.tensor_tensor(gp[:, 2, :], gp[:, 1, :], gp[:, 2, :], ALU.mult),
              r=[B("gp")], w=[B("gp")], l=f"nu{ss}")
            V(lambda e: e.tensor_tensor(self.gcol[:, 3:4], self.gcol[:, 1:2],
                                        self.gcol[:, 2:3], ALU.add),
              r=[B("gc")], w=[B("gc")], l=f"de{ss}")
            V(lambda e: e.reciprocal(self.gcol[:, 4:5], self.gcol[:, 3:4]),
              r=[B("gc")], w=[B("gc")], l=f"rr{ss}")
            V(lambda e, ss=ss: e.tensor_scalar(self.comb_t[:, ss, :], gp[:, 2, :],
                                               self.gcol[:, 4:5], None, ALU.mult),
              r=[B("gp"), B("gc"), B("cmb")], w=[B("cmb")], l=f"cm{ss}")
            T(lambda e, ss=ss: e.transpose(self.Y2[0:8, 0:128], self.comb_t[:, ss, :],
                                           self.iden_f[:]),
              r=[B("cmb"), B("iden"), B("pY0")], w=[B("pY0")], l=f"ct{ss}")
            V(lambda e: e.tensor_copy(self.comb_st[:, :], self.Y2[0:8, 0:128]),
              r=[B("pY0"), B("cst")], w=[B("cst")], l=f"ec{ss}")
            DS(lambda e, ss=ss: e.dma_start(self.comb_dram[:, ss * 128:(ss + 1) * 128],
                                            self.comb_st[:, :]),
               r=[B("cst")], w=[B("cfd")], l=f"wcf{ss}")

        DS(lambda e: e.dma_start(self.acc[:, :, :],
                                 self.g_sp[:].rearrange("(k p) d -> p k d", p=128)),
           r=[B("gsp"), B("acc")], w=[B("acc")], l="ldG")

        experts = [(None, i["sWg"], i["sWu"], i["sWd"])]
        for ex in range(EE):
            experts.append((ex, i["Wg"], i["Wu"], i["Wd"]))
        for ei, (eidx, wg_src, wu_src, wd_src) in enumerate(experts):
            pp = ei % 2
            sl = 0 if eidx is None else eidx
            DG(lambda e, wg_src=wg_src, sl=sl, pp=pp: e.dma_start(
                self.Wg_s[:, pp, :, :], wg_src[sl].rearrange("(c p) i -> p c i", p=128)),
               r=[B(f"wg{pp}")], w=[B(f"wg{pp}")], l=f"lg{ei}")
            DG(lambda e, wu_src=wu_src, sl=sl, pp=pp: e.dma_start(
                self.Wu_s[:, pp, :, :], wu_src[sl].rearrange("(c p) i -> p c i", p=128)),
               r=[B(f"wu{pp}")], w=[B(f"wu{pp}")], l=f"lu{ei}")
            DG(lambda e, wd_src=wd_src, sl=sl, pp=pp: e.dma_start(
                self.Wd_s[:, pp, :, :], wd_src[sl].rearrange("(c p) d -> p c d", p=128)),
               r=[B(f"wd{pp}")], w=[B(f"wd{pp}")], l=f"ld{ei}")
            for blk in range(self.OWN_T // QB):
                if eidx is not None:
                    DS(lambda e, eidx=eidx, blk=blk: e.dma_start(
                        self.comb_row[:, :],
                        self.comb_dram[eidx, blk * 512:(blk + 1) * 512][None, :]),
                       r=[B("cfd"), B("crow")], w=[B("crow")], l=f"lcf{ei}_{blk}")
                    T(lambda e: e.matmul(
                        self.Y2[0:128, 0:512], self.ones_f[0:1, :],
                        self.comb_row[:, :], start=True, stop=True),
                      r=[B("crow"), B("onesf"), B("pY0")], w=[B("pY0")], l=f"be{ei}_{blk}")
                    V(lambda e: e.tensor_copy(self.Be[:, :], self.Y2[0:128, 0:512]),
                      r=[B("pY0"), B("Be")], w=[B("Be")], l=f"eb{ei}_{blk}")
                for ic in range(4):
                    gso = 1024 * (ic % 2)
                    gb = f"pS{2 * (ic % 2)}"
                    ub = f"pS{2 * (ic % 2) + 1}"
                    for c in range(NCH):
                        T(lambda e, ic=ic, c=c, pp=pp, blk=blk, gso=gso: e.matmul(
                            self.S4[0:128, gso:gso + 512],
                            self.ap(self.Wg_s, [128, 2, NCH, II], 0, 128,
                                    (pp * NCH + c) * II + ic * 128, [[1, 128]]),
                            self.ap(self.xn2f, [128, self.NOS, NCH, 128], 0, 128,
                                    blk * 4 * NCH * 128 + c * 128, [[NCH * 128, 4], [1, 128]]),
                            start=(c == 0), stop=(c == NCH - 1)),
                          r=[B(f"wg{pp}"), B("xn2f")] + ([B(gb)] if c == 0 else []),
                          w=[B(gb)], l=f"hg{ei}_{blk}_{ic}_{c}")
                    S_(lambda e, ic=ic, gso=gso: e.activation(
                        self.sg[:, ic, :], self.S4[0:128, gso:gso + 512], AF.Silu),
                       r=[B(gb), B("sg")], w=[B("sg")], l=f"si{ei}_{blk}_{ic}")
                    for c in range(NCH):
                        T(lambda e, ic=ic, c=c, pp=pp, blk=blk, gso=gso: e.matmul(
                            self.S4[0:128, gso + 512:gso + 1024],
                            self.ap(self.Wu_s, [128, 2, NCH, II], 0, 128,
                                    (pp * NCH + c) * II + ic * 128, [[1, 128]]),
                            self.ap(self.xn2f, [128, self.NOS, NCH, 128], 0, 128,
                                    blk * 4 * NCH * 128 + c * 128, [[NCH * 128, 4], [1, 128]]),
                            start=(c == 0), stop=(c == NCH - 1)),
                          r=[B(f"wu{pp}"), B("xn2f")] + ([B(ub)] if c == 0 else []),
                          w=[B(ub)], l=f"hu{ei}_{blk}_{ic}_{c}")
                    if eidx is None:
                        V(lambda e, ic=ic, gso=gso: e.tensor_copy(
                            self.hu[:, ic, :], self.S4[0:128, gso + 512:gso + 1024]),
                          r=[B(ub), B("hu")], w=[B("hu")], l=f"eu{ei}_{blk}_{ic}")
                    else:
                        V(lambda e, ic=ic, gso=gso: e.tensor_tensor(
                            self.hu[:, ic, :], self.S4[0:128, gso + 512:gso + 1024],
                            self.Be[:, :], ALU.mult),
                          r=[B(ub), B("Be"), B("hu")], w=[B("hu")], l=f"eu{ei}_{blk}_{ic}")
                    V(lambda e, ic=ic: e.tensor_tensor(self.gu[:, ic, :], self.sg[:, ic, :],
                                                       self.hu[:, ic, :], ALU.mult),
                      r=[B("sg"), B("hu"), B("gu")], w=[B("gu")], l=f"gu{ei}_{blk}_{ic}")
                for s4 in range(4):
                    sub = blk * 4 + s4
                    dps = self.PAB if s4 % 2 == 0 else self.Y2
                    db0 = "pA0" if s4 % 2 == 0 else "pY0"
                    db1 = "pA1" if s4 % 2 == 0 else "pY1"
                    for ic in range(4):
                        for half in range(2):
                            T(lambda e, ic=ic, s4=s4, half=half, pp=pp, dps=dps: e.matmul(
                                dps[0:128, half * 512:(half + 1) * 512],
                                self.gu[:, ic, s4 * 128:(s4 + 1) * 128],
                                self.ap(self.Wd_s, [128, 2, 4, D], 0, 128,
                                        (pp * 4 + ic) * D + half * 512, [[1, 512]]),
                                start=(ic == 0), stop=(ic == 3)),
                              r=[B("gu"), B(f"wd{pp}")] + ([B(db0 if half == 0 else db1)] if ic == 0 else []),
                              w=[B(db0 if half == 0 else db1)], l=f"wd{ei}_{blk}_{s4}_{ic}_{half}")
                    V(lambda e, sub=sub, dps=dps: e.tensor_tensor(self.acc[:, sub, :],
                                                                  dps[0:128, 0:1024],
                                                                  self.acc[:, sub, :], ALU.add),
                      r=[B(db0), B(db1), B("acc")], w=[B("acc")], l=f"ac{ei}_{blk}_{s4}")

    def phase_final(self):
        B, T, V, S_, DS = self.B, self.T, self.V, self.S_, self.DS
        st = self.stc
        for sub in range(self.NOS):
            V(lambda e, sub=sub: e.reduce_sum(st[:, 8:9], self.acc[:, sub, :], axis=AX.X),
              r=[B("acc"), B("stc")], w=[B("stc")], l=f"fs{sub}")
            V(lambda e, sub=sub: e.scalar_tensor_tensor(
                self.sqscr[:, :], self.acc[:, sub, :], 1.0, self.acc[:, sub, :],
                ALU.mult, ALU.mult, accum_out=st[:, 9:10]),
              r=[B("acc"), B("sqscr"), B("stc")], w=[B("sqscr"), B("stc")], l=f"f2{sub}")
            V(lambda e: e.tensor_scalar(st[:, 10:11], st[:, 8:9], 1.0 / D, None, ALU.mult),
              r=[B("stc")], w=[B("stc")], l=f"fm{sub}")
            V(lambda e: e.tensor_tensor(st[:, 11:12], st[:, 10:11], st[:, 10:11], ALU.mult),
              r=[B("stc")], w=[B("stc")], l=f"f3{sub}")
            V(lambda e: e.scalar_tensor_tensor(st[:, 11:12], st[:, 9:10], 1.0 / D,
                                               st[:, 11:12], ALU.mult, ALU.subtract),
              r=[B("stc")], w=[B("stc")], l=f"fv{sub}")
            S_(lambda e: e.activation(st[:, 12:13], st[:, 11:12], AF.Ln, bias=EPS),
               r=[B("stc"), B("econst")], w=[B("stc")], l=f"fl{sub}")
            S_(lambda e: e.activation(st[:, 12:13], st[:, 12:13], AF.Exp, scale=-0.5),
               r=[B("stc")], w=[B("stc")], l=f"fr{sub}")
            V(lambda e: e.scalar_tensor_tensor(st[:, 15:16], st[:, 10:11], -1.0,
                                               st[:, 12:13], ALU.mult, ALU.mult),
              r=[B("stc")], w=[B("stc")], l=f"fn{sub}")
            S_(lambda e, sub=sub: e.activation(self.xto[:, :], self.acc[:, sub, :], AF.Identity,
                                               bias=st[:, 15:16], scale=st[:, 12:13]),
               r=[B("acc"), B("stc"), B("xto")], w=[B("xto")], l=f"fo{sub}")
            DS(lambda e, sub=sub: e.dma_start(self.out[sub * 128:(sub + 1) * 128, :],
                                              self.xto[:, :]),
               r=[B("xto")], w=[B("outd")], l=f"o{sub}")
        self.p.barrier(list(self._bufs.values()))

    def build(self, stop_after=None):
        self.declare_io()
        self.alloc_persist()
        self.nc.const_aps.register(F32, EPS, self.econst[:, 0:1])
        self.alloc_outer()
        self.xn1f_oa = self.sb("xn1f_oa", [128, 4, NCH, 128], BF16)
        self.phase_prep()
        if stop_after != "prep":
            self.phase_outer()
            if self.debug:
                for nm, bufn, tile, shape, dt in (
                        ("qA", "qA", self.qA, [128, self.S], BF16),
                        ("kA", "kA", self.kA, [128, self.S], BF16),
                        ("vt", "vt", self.vt, [128, self.NKT, 129], BF16),
                        ("krass", "krass", self.krass, [128, self.S], BF16),
                        ("kvpack", "kvp", self.kvpack, [128, 4, self.S], BF16),
                        ("kvt", "kvt", self.kvt, [128, self.NKT, HEADS * 33], BF16)):
                    t = self.nc.dram_tensor("dbg_" + nm, shape, dt,
                                            kind="ExternalOutput").ap()
                    self.p.add("sync", lambda e, t=t, tile=tile: e.dma_start(t[:], tile[:]),
                               reads=(self.B(bufn),), writes=(self.B("dbg"),),
                               label=f"dump_{nm}", dma=True)
        if stop_after in (None, "moe") or (stop_after or "").startswith("mla"):
            self.free_arena(set(self.OUTER_NAMES + ["xn1f_oa"]))
            self.alloc_mla()
            ms = stop_after[4:] if (stop_after or "").startswith("mla_") else None
            self.phase_mla(mstop=ms)
        if stop_after in (None, "moe"):
            self.free_arena(set(self.MLA_NAMES))
            self.alloc_moe()
            self.phase_moe()
        if stop_after is None:
            self.phase_final()
        else:
            self.p.barrier(list(self._bufs.values()))
            self.p.add("vector", lambda e: e.memset(self.xto[:, :], 0.0),
                       reads=(), writes=(self.B("xto"),), label="dumz")
            for sub in range(self.NOS):
                self.p.add("sync", lambda e, sub=sub: e.dma_start(
                    self.out[sub * 128:(sub + 1) * 128, :], self.xto[:, :]),
                    reads=(self.B("xto"),), writes=(self.B("outd"),),
                    label=f"do{sub}", dma=True)
            self.p.barrier([self.B("outd")])
        self.p.emit_all(self.nc)
        for name, cm in reversed(self._arena):
            cm.__exit__(None, None, None)
        return self.nc


def _dims_n(dims):
    n = 1
    for d in dims:
        n *= d[1]
    return n



def host_tables(S, half):
    NQB = S // QB
    NSLOT = NQB // 2
    Ls = [8 * (s + 1) for s in range(NSLOT)]
    A, Bm = outer_tables(S)
    own_qbs = [2 * s + half for s in range(NSLOT)]
    pos = np.concatenate([np.arange(qb * QB, (qb + 1) * QB) for qb in own_qbs])
    Aq, Bq = mla_tables(pos, 4)
    Ak, Bk = mla_tables(np.arange(S), 1)
    orow = np.zeros((1, NQB), np.float32)
    for qb in own_qbs:
        orow[0, qb] = 1.0
    padrow = np.zeros((1, sum(Ls)), np.float32)
    off = 0
    for s in range(NSLOT):
        lim = 4 * (2 * s + half) + 4
        for j in range(Ls[s]):
            if j >= lim:
                padrow[0, off + j] = NEG
        off += Ls[s]
    masks = np.ones((128, 8, QB), np.float32)
    f = np.arange(QB)[None, :]
    pp = np.arange(128)[:, None]
    for j in range(8):
        if half == 0 and j < 4:
            masks[:, j, :] = (f - pp - 128 * j >= 0)
        if half == 1 and j >= 4:
            masks[:, j, :] = (f - pp - 128 * (j - 4) >= 0)
    return {
        "Aout": A, "Bout": Bm, "Aq": Aq, "Bq": Bq, "Ak": Ak, "Bk": Bk,
        "iden": np.eye(128, dtype=np.float32), "orow": orow, "padrow": padrow,
        "masks": masks.astype(BF),
    }


_nc_cache = {}


def kernel(L0_attn_x, L0_attn_wq, L0_attn_wk, L0_attn_wv, L0_attn_wo,
           L0_ffn_w_ln1, L0_ffn_Wq, L0_ffn_Wkva, L0_ffn_Wkvb, L0_ffn_Wo,
           L0_ffn_w_ln2, L0_ffn_Wgate, L0_ffn_expert_bias, L0_ffn_Wg,
           L0_ffn_Wu, L0_ffn_Wd, L0_ffn_sWg, L0_ffn_sWu, L0_ffn_sWd):
    x = np.ascontiguousarray(np.asarray(L0_attn_x, dtype=np.float32))
    nb, S, _ = x.shape
    weights = {
        "wq": L0_attn_wq, "wk": L0_attn_wk, "wv": L0_attn_wv, "wo": L0_attn_wo,
        "w_ln1": L0_ffn_w_ln1, "Wq": L0_ffn_Wq, "Wkva": L0_ffn_Wkva,
        "Wkvb": L0_ffn_Wkvb, "Wo": L0_ffn_Wo, "w_ln2": L0_ffn_w_ln2,
        "Wgate": L0_ffn_Wgate, "ebias": L0_ffn_expert_bias, "Wg": L0_ffn_Wg,
        "Wu": L0_ffn_Wu, "Wd": L0_ffn_Wd, "sWg": L0_ffn_sWg, "sWu": L0_ffn_sWu,
        "sWd": L0_ffn_sWd,
    }
    weights = {k: np.ascontiguousarray(np.asarray(v, dtype=np.float32))
               for k, v in weights.items()}
    if S not in _nc_cache:
        _nc_cache[S] = Builder(S).build()
    nc = _nc_cache[S]
    tabs = [host_tables(S, h) for h in range(2)]
    n_cores = 2 * nb
    in_maps = []
    for b in range(nb):
        for h in range(2):
            m = dict(weights)
            m["x"] = x[b]
            m.update(tabs[h])
            in_maps.append(m)
    res = run_bass_kernel_spmd(nc, in_maps, core_ids=list(range(n_cores)))
    out = np.empty((nb, S, D), np.float32)
    NSLOT = (S // QB) // 2
    for b in range(nb):
        for h in range(2):
            o = res.results[2 * b + h]["out"]
            for s in range(NSLOT):
                qb = 2 * s + h
                out[b, qb * QB:(qb + 1) * QB, :] = o[s * QB:(s + 1) * QB, :]
    return out


# version 13
# speedup vs baseline: 1.2142x; 1.2142x over previous
import math
import numpy as np
import ml_dtypes
import concourse.bass as bass
import concourse.mybir as mybir
from concourse.bass_utils import run_bass_kernel_spmd

F32 = mybir.dt.float32
BF16 = mybir.dt.bfloat16
AF = mybir.ActivationFunctionType
ALU = mybir.AluOpType
AX = mybir.AxisListType
BF = ml_dtypes.bfloat16

D = 1024
NCH = 8
HEADS = 16
RLAT = 128
KR = 32
EE = 8
II = 512
OHD = 128
QB = 512
KT = 128
EPS = 1e-5
NEG = -1e30

ENGS = ("tensor", "vector", "scalar", "gpsimd", "sync")


class _Item:
    __slots__ = ("idx", "eng", "emit", "deps", "target", "label", "amt",
                 "semname", "use")

    def __init__(self, idx, eng, emit, label, amt):
        self.idx, self.eng, self.emit, self.label, self.amt = idx, eng, emit, label, amt
        self.deps = []
        self.target = None


class Buf:
    __slots__ = ("name", "last_write", "readers")

    def __init__(self, name):
        self.name = name
        self.last_write = None
        self.readers = []


class Prog:
    def __init__(self):
        self.items = []
        self.fence_items = None
        self.fenced_engines = set()

    EXCL = frozenset(("pA0", "pA1", "pS0", "pS1", "pS2", "pS3", "pY0", "pY1"))

    def add(self, eng, emit, reads=(), writes=(), label="", dma=False):
        it = _Item(len(self.items), eng, emit, label, 16 if dma else 1)
        deps = []
        if self.fence_items is not None and eng not in self.fenced_engines:
            deps.extend(self.fence_items)
            self.fenced_engines.add(eng)
        excl = [b for b in reads if b.name in self.EXCL]
        reads = [b for b in reads if b.name not in self.EXCL]
        writes = tuple(writes) + tuple(excl)
        for b in reads:
            if b.last_write is not None:
                deps.append(b.last_write)
            b.readers.append(it)
        for b in writes:
            if b.last_write is not None:
                deps.append(b.last_write)
            deps.extend(b.readers)
            b.readers = []
            b.last_write = it
        seen = set()
        for d in deps:
            if d is it or d.idx in seen:
                continue
            seen.add(d.idx)
            if d.eng == eng:
                if d.amt == 1 and it.amt == 1 and eng == "tensor":
                    continue
                if d.amt == 1 and it.amt == 16:
                    continue
            it.deps.append(d)
        self.items.append(it)
        return it

    def barrier(self, bufs):
        fence = []
        fb = Buf("_fence")
        for e in ENGS:
            fence.append(self.add(e, lambda h: h.nop(), reads=(), writes=tuple(bufs) + (fb,),
                                  label=f"fence_{e}", dma=False))
        self.fence_items = fence
        self.fenced_engines = set()

    NPOOL = 10

    def emit_all(self, nc):
        dma_cnt = {}
        for it in self.items:
            if it.amt == 16:
                q = it.eng
                idx = dma_cnt.get(q, 0)
                dma_cnt[q] = idx + 1
                it.semname = f"dpool_{q}_{idx % self.NPOOL}"
                it.use = idx // self.NPOOL
                it.target = 16 * (it.use + 1)
            else:
                it.semname = f"sem_{it.eng}"
                it.use = None
        needed = set()
        for it in self.items:
            for d in it.deps:
                needed.add(d.idx)
        running = {}
        for it in self.items:
            if it.amt == 1:
                if it.idx in needed:
                    running[it.eng] = running.get(it.eng, 0) + 1
                    it.target = running[it.eng]
                else:
                    it.target = None
        engs = [e for e in ENGS if any(it.eng == e for it in self.items)]
        semnames = sorted(set(it.semname for it in self.items))
        sem_cms = {n: nc.semaphore(n) for n in semnames}
        sems = {n: cm.__enter__() for n, cm in sem_cms.items()}
        blk_cm = nc.Block()
        block = blk_cm.__enter__()
        by_eng = {e: [it for it in self.items if it.eng == e] for e in engs}

        def mk(eng_items):
            def body(h):
                seen = {}
                for it in eng_items:
                    wmax = {}
                    for d in it.deps:
                        assert d.idx < it.idx and d.target is not None, (d.label, it.label)
                        wmax[d.semname] = max(wmax.get(d.semname, 0), d.target)
                    if it.amt == 16 and it.use > 0:
                        v = 16 * it.use
                        if wmax.get(it.semname, 0) < v:
                            wmax[it.semname] = v
                    for sn in sorted(wmax):
                        if seen.get(sn, -1) >= wmax[sn]:
                            continue
                        h.wait_ge(sems[sn], wmax[sn])
                        seen[sn] = wmax[sn]
                    ins = it.emit(h)
                    if it.target is not None:
                        ins.then_inc(sems[it.semname], it.amt)
            return body

        dec = {"tensor": block.tensor, "vector": block.vector, "scalar": block.scalar,
               "gpsimd": block.gpsimd, "sync": block.sync}
        for e in engs:
            dec[e](mk(by_eng[e]))
        blk_cm.__exit__(None, None, None)
        for n in semnames:
            sem_cms[n].__exit__(None, None, None)


def fe(shape):
    n = 1
    for s in shape[1:]:
        n *= s
    return n


def outer_tables(S):
    j = np.arange(64)
    invf = (10000.0 ** (-(2.0 * j) / 128.0))[:, None]
    ang = (np.arange(S)[None, :] - 1) * invf
    c = np.cos(ang)
    s = np.sin(ang)
    c[:, 0] = 1.0
    s[:, 0] = 0.0
    return (np.concatenate([c, c], 0).astype(np.float32),
            np.concatenate([-s, s], 0).astype(np.float32))


def mla_tables(positions, nblk):
    j = np.arange(16)
    freqs = (100000.0 ** (-(2.0 * j) / 32.0))[:, None]
    ang = positions[None, :].astype(np.float64) * freqs
    c = np.cos(ang)
    s = np.sin(ang)
    blkA = np.concatenate([c, c], 0)
    blkB = np.concatenate([-s, s], 0)
    A = np.concatenate([blkA] * nblk, 0).astype(np.float32)
    Bm = np.concatenate([blkB] * nblk, 0).astype(np.float32)
    return A, Bm


class Builder:
    def __init__(self, S, debug=False):
        self.S = S
        self.debug = debug
        self.NQB = S // QB
        self.NKT = S // KT
        self.NSLOT = self.NQB // 2
        self.OWN_T = self.NSLOT * QB
        self.NOS = self.OWN_T // 128
        self.Ls = [8 * (s + 1) for s in range(self.NSLOT)]
        self.nc = bass.Bass("TRN2", target_bir_lowering=False)
        self.p = Prog()
        self._bufs = {}
        self._arena = []

    def T(self, emit, r=(), w=(), l=""):
        return self.p.add("tensor", emit, r, w, l)

    def V(self, emit, r=(), w=(), l=""):
        return self.p.add("vector", emit, r, w, l)

    def S_(self, emit, r=(), w=(), l=""):
        return self.p.add("scalar", emit, r, w, l)

    def G(self, emit, r=(), w=(), l=""):
        return self.p.add("gpsimd", emit, r, w, l)

    def DS(self, emit, r=(), w=(), l=""):
        return self.p.add("sync", emit, r, w, l, dma=True)

    def DG(self, emit, r=(), w=(), l=""):
        return self.p.add("gpsimd", emit, r, w, l, dma=True)

    def B(self, name):
        if name not in self._bufs:
            self._bufs[name] = Buf(name)
        return self._bufs[name]

    def sb(self, name, shape, dtype):
        cm = self.nc.sbuf_tensor(name, shape, dtype)
        t = cm.__enter__()
        self._arena.append((name, cm))
        return t

    def free_arena(self, names, bufnames=None):
        self.p.barrier(list(self._bufs.values()))
        keep = []
        for name, cm in reversed(self._arena):
            if name in names:
                cm.__exit__(None, None, None)
            else:
                keep.append((name, cm))
        self._arena = list(reversed(keep))

    def ap(self, tile, shape, part0, nparts, free_off, dims):
        return bass.AP(tile, part0 * fe(shape) + free_off, [[fe(shape), nparts]] + list(dims))

    def declare_io(self):
        nc = self.nc
        S = self.S
        d = {}
        def di(name, shape, dtype=F32):
            d[name] = nc.dram_tensor(name, shape, dtype, kind="ExternalInput").ap()
        di("x", [S, D]); di("wq", [OHD, D]); di("wk", [OHD, D]); di("wv", [OHD, D])
        di("wo", [D, OHD]); di("w_ln1", [D]); di("Wq", [D, D]); di("Wkva", [D, RLAT + KR])
        di("Wkvb", [RLAT, HEADS * 32]); di("Wo", [HEADS * 64, D]); di("w_ln2", [D])
        di("Wgate", [D, EE]); di("ebias", [EE]); di("Wg", [EE, D, II]); di("Wu", [EE, D, II])
        di("Wd", [EE, II, D]); di("sWg", [1, D, II]); di("sWu", [1, D, II]); di("sWd", [1, II, D])
        di("Aout", [128, S]); di("Bout", [128, S])
        di("Aq", [128, self.OWN_T]); di("Bq", [128, self.OWN_T])
        di("Ak", [32, S]); di("Bk", [32, S]); di("iden", [128, 128])
        di("orow", [1, self.NQB]); di("padrow", [1, sum(self.Ls)])
        di("masks", [128, 8, QB], BF16)
        self.inp = d
        self.out = nc.dram_tensor("out", [self.OWN_T, D], F32, kind="ExternalOutput").ap()
        k = "ExternalOutput" if self.debug else "Internal"
        self.x1_sp = nc.dram_tensor("x1_sp", [S, D], F32, kind=k).ap()
        self.xn1_sp = nc.dram_tensor("xn1_sp", [S, D], BF16, kind=k).ap()
        self.g_sp = nc.dram_tensor("g_sp", [self.OWN_T, D], F32, kind=k).ap()
        self.comb_dram = nc.dram_tensor("comb_dram", [EE, self.OWN_T], F32, kind=k).ap()
        self.xn2_sp = nc.dram_tensor("xn2_sp", [self.OWN_T, D], BF16, kind=k).ap()

    def alloc_persist(self):
        sb = self.sb
        S = self.S
        self.biasKsl = sb("biasKsl", [128, sum(self.Ls)], F32)
        self.wln1_c = sb("wln1_c", [128, NCH], F32)
        self.wln2_c = sb("wln2_c", [128, NCH], F32)
        self.iden_f = sb("iden_f", [128, 128], F32)
        self.ones_b = sb("ones_b", [128, 2], BF16)
        self.ones_f = sb("ones_f", [128, 128], F32)
        self.negh = sb("negh", [128, 1], BF16)
        self.econst = sb("econst", [128, 1], F32)
        self.ocol = sb("ocol", [128, self.NQB], F32)
        self.ebias_b = sb("ebias_b", [128, EE], F32)
        self.Wgate_s = sb("Wgate_s", [128, NCH, EE], BF16)
        self.vscr = sb("vscr", [128, D], F32)
        self.sqscr = sb("sqscr", [128, D], BF16)
        self.stc = sb("stc", [128, 24], F32)
        self.xto = sb("xto", [128, D], F32)
        self.xnb = sb("xnb", [128, 2, D], BF16)
        self.sqX = sb("sqX", [128, 2, 512], BF16)
        self.nkrow = sb("nkrow", [1, 512], F32)
        self.mxk = sb("mxk", [1, 8], F32)
        self.t1s = sb("t1s", [128, QB], F32)
        self.t2s = sb("t2s", [128, QB], F32)
        self.kvpack = sb("kvpack", [128, 4, S], BF16)
        self.kvt = sb("kvt", [128, self.NKT, HEADS * 33], BF16)
        self.krass = sb("krass", [128, S], BF16)
        self.PAB = self.nc.psum_tensor("PAB", [128, 1024], F32).__enter__()
        self.S4 = self.nc.psum_tensor("S4", [128, 2048], F32).__enter__()
        self.Y2 = self.nc.psum_tensor("Y2", [128, 1024], F32).__enter__()

    def alloc_outer(self):
        sb = self.sb
        S = self.S
        self.WkvaA = sb("WkvaA", [128, NCH, 192], BF16)
        self.WkvbT = sb("WkvbT", [128, 4, 128], BF16)
        self.Wkvb_n = sb("Wkvb_n", [128, HEADS * 32], BF16)
        self.wqX = sb("wqX", [128, NCH, 128], BF16)
        self.wqXs = sb("wqXs", [128, NCH, 128], BF16)
        self.wkX = sb("wkX", [128, NCH, 128], BF16)
        self.wkXs = sb("wkXs", [128, NCH, 128], BF16)
        self.wvT = sb("wvT", [128, NCH, 128], BF16)
        self.woT = sb("woT", [128, NCH, 128], BF16)
        self.qA = sb("qA", [128, S], BF16)
        self.kA = sb("kA", [128, S], BF16)
        self.vscr2 = sb("vscr2", [128, D], F32)
        self.biasG = sb("biasG", [128, 1], F32)
        self.vt = sb("vt", [128, self.NKT, 129], BF16)
        self.x0f = sb("x0f", [128, 2, NCH, 512], BF16)
        self.x0tb = sb("x0tb", [128, 2, D], BF16)
        self.x0tf = sb("x0tf", [128, 2, D], F32)
        self.AB_blk = sb("AB_blk", [128, 2, 512], F32)
        self.ABk_blk = sb("ABk_blk", [32, 2, 512], F32)
        self.wld = sb("wld", [128, D], F32)
        self.wld2 = sb("wld2", [128, 128], F32)
        self.wrow = sb("wrow", [8, 128], F32)
        self.lat = sb("lat", [128, 512], BF16)
        self.expO = sb("expO", [128, 4, 512], BF16)

    OUTER_NAMES = ["WkvaA", "WkvbT", "Wkvb_n",
                   "wqX", "wqXs", "wkX", "wkXs", "wvT", "woT", "qA", "kA",
                   "vscr2", "biasG", "vt", "x0f", "x0tb", "x0tf", "AB_blk",
                   "ABk_blk", "wld", "wld2", "wrow", "lat", "expO"]
    OUTER_BUFS = ["wq0", "wq1", "wk0", "wk1", "wvT", "woT", "qA", "qB", "kA", "kB",
                  "vt", "biasKo", "x0f0", "x0f1", "x0tb0", "x0tb1", "x0tf0", "x0tf1",
                  "ABb", "ABk", "wld", "wld2", "wrow", "lat",
                  "psXq", "psXqs", "psXk", "psXks", "psv", "psQn", "psKc", "psKc2",
                  "psS0", "psS512", "psY1", "psY2", "psA", "exp0", "exp1",
                  "t1", "t2", "sqX", "nkr", "vscr", "sqscr", "stc", "xto", "xsc",
                  "xnb0", "xnb1", "xnsc", "x1sp", "xn1sp", "krass", "kvp", "kvt",
                  "biasKm", "ocol"]

    def alloc_mla(self):
        sb = self.sb
        self.WqA = sb("WqA", [128, NCH, HEADS, 96], BF16)
        self.WoE = sb("WoE", [128, 4, D], BF16)
        self.Qnope = sb("Qnope", [128, 4, QB], BF16)
        self.Qra = sb("Qra", [128, 8, QB], BF16)
        self.expT = sb("expT", [128, 2, 4, QB], BF16)
        self.yasm = sb("yasm", [128, 4, QB], BF16)
        self.rDr = sb("rDr", [1, 4, QB], F32)
        self.sel32 = sb("sel32", [128, 128], F32)
        self.rDc = sb("rDc", [128, QB], F32)
        self.Bbc4 = sb("Bbc4", [128, QB], F32)
        self.xn1f = sb("xn1f", [128, 2, 4, NCH, 128], BF16)
        self.x1s2 = sb("x1s2", [128, D], F32)
        self.x1su = sb("x1su", [128, D], F32)
        self.xns2 = sb("xns2", [128, D], BF16)
        self.xnsu = sb("xnsu", [128, 2, D], BF16)
        self.masks_s = sb("masks_s", [128, 8, QB], BF16)
        self.ABq_sl = sb("ABq_sl", [128, 2, QB], F32)
        self.wldM = self.x1su

    MLA_NAMES = ["WqA", "WoE", "Qnope", "Qra", "expT", "yasm", "rDr", "sel32",
                 "rDc", "Bbc4",
                 "xn1f", "x1s2", "x1su", "xns2", "xnsu", "masks_s", "ABq_sl",
                 "kvpack", "kvt", "krass"]
    MLA_BUFS = ["WqA", "WoE", "Qnope", "Qra", "ex0", "ex1", "ex2", "ex3", "yasm",
                "rD", "Bbc", "xn1f", "x1qb", "x1qb2", "xnqb", "xnqb2", "masks",
                "ABq", "wldM", "psQ0", "psQ1", "psS0", "psS512", "psS1024",
                "psS1536", "psYm0", "psYm1", "psA", "psKc", "gsp", "kvp", "kvt",
                "krass", "biasKsl", "t1", "t2", "sqX", "vscr", "sqscr", "stc",
                "xto", "xnb0", "xnb1", "xn2f"]

    def alloc_moe(self):
        sb = self.sb
        self.acc = sb("acc", [128, self.NOS, D], F32)
        self.Wg_s = sb("Wg_s", [128, 2, NCH, II], BF16)
        self.Wu_s = sb("Wu_s", [128, 2, NCH, II], BF16)
        self.Wd_s = sb("Wd_s", [128, 2, 4, D], BF16)
        self.comb_t = sb("comb_t", [128, self.NOS, 8], F32)
        self.comb_st = sb("comb_st", [8, 128], F32)
        self.comb_row = sb("comb_row", [1, 512], F32)
        self.gpr = sb("gpr", [128, 4, 8], F32)
        self.gcol = sb("gcol", [128, 8], F32)
        self.sg = sb("sg", [128, 4, 512], BF16)
        self.hu = sb("hu", [128, 4, 512], BF16)
        self.gu = sb("gu", [128, 4, 512], BF16)
        self.Be = sb("Be", [128, 512], F32)
        self.xn2f = sb("xn2f", [128, self.NOS, NCH, 128], BF16)
        self.xe = sb("xe", [128, 4, D], BF16)

    def phase_prep(self):
        B, T, V, S_, G, DS, DG = self.B, self.T, self.V, self.S_, self.G, self.DS, self.DG
        i = self.inp
        DS(lambda e: e.dma_start(self.iden_f[:], i["iden"][:]), w=[B("iden")])
        V(lambda e: e.memset(self.ones_b[:], 1.0), w=[B("ones")])
        V(lambda e: e.memset(self.ones_f[:], 1.0), w=[B("onesf")])
        V(lambda e: e.memset(self.negh[:], -0.5), w=[B("negh")])
        V(lambda e: e.memset(self.econst[:], EPS), w=[B("econst")])
        V(lambda e: e.memset(self.mxk[:], 0.0), w=[B("mxk")])
        V(lambda e: e.memset(
            self.ap(self.kvt, [128, self.NKT, HEADS * 33], 0, 128, 32,
                    [[33, HEADS * self.NKT]]), 1.0), w=[B("kvt")])
        V(lambda e: e.memset(
            self.ap(self.vt, [128, self.NKT, 129], 0, 128, 64,
                    [[129, self.NKT]]), 1.0), w=[B("vt")])
        V(lambda e: e.memset(self.krass[32:64, :], 0.0), w=[B("krass")])
        V(lambda e: e.memset(self.krass[32:33, :], 1.0), r=[B("krass")], w=[B("krass")])
        V(lambda e: e.memset(self.krass[96:128, :], 0.0), r=[B("krass")], w=[B("krass")])
        V(lambda e: e.memset(self.krass[96:97, :], 1.0), r=[B("krass")], w=[B("krass")])

        for nm, dst, bn in (("w_ln1", self.wln1_c, "w_ln1"), ("w_ln2", self.wln2_c, "w_ln2")):
            DS(lambda e, nm=nm: e.dma_start(self.wrow[:], i[nm].rearrange("(c p) -> c p", p=128)),
               r=[B("wrow")], w=[B("wrow")], l=f"ld{nm}")
            T(lambda e: e.transpose(self.PAB[0:128, 0:8], self.wrow[:], self.iden_f[0:8, 0:8]),
              r=[B("wrow"), B("iden"), B("pA0")], w=[B("pA0")], l=f"t{nm}")
            V(lambda e, dst=dst: e.tensor_copy(dst[:, 0:NCH], self.PAB[0:128, 0:8]),
              r=[B("pA0")], w=[B(bn)], l=f"ev{nm}")
        DS(lambda e: e.dma_start(self.wrow[0:1, 0:EE], i["ebias"][None, :]),
           r=[B("wrow")], w=[B("wrow")], l="ldeb")
        T(lambda e: e.matmul(self.PAB[0:128, 256:256 + EE], self.ones_f[0:1, :],
                             self.wrow[0:1, 0:EE], start=True, stop=True),
          r=[B("wrow"), B("onesf"), B("pA0")], w=[B("pA0")], l="mmEb")
        V(lambda e: e.tensor_copy(self.ebias_b[:], self.PAB[0:128, 256:256 + EE]),
          r=[B("pA0")], w=[B("ebias")], l="evEb")
        DS(lambda e: e.dma_start(self.wrow[0:1, 0:self.NQB], i["orow"][:]),
           r=[B("wrow")], w=[B("wrow")], l="ldor")
        T(lambda e: e.matmul(self.PAB[0:128, 256:256 + self.NQB], self.ones_f[0:1, :],
                             self.wrow[0:1, 0:self.NQB], start=True, stop=True),
          r=[B("wrow"), B("onesf"), B("pA0")], w=[B("pA0")], l="mmOr")
        V(lambda e: e.tensor_copy(self.ocol[:], self.PAB[0:128, 256:256 + self.NQB]),
          r=[B("pA0")], w=[B("ocol")], l="evOr")

        def outer_w(name, dstX, dstXs, tag):
            wview = i[name].rearrange("(j two) d -> j two d", two=2)
            for swap in (0, 1):
                dst = dstXs if swap else dstX
                DS(lambda e, swap=swap, wview=wview: e.dma_start(self.wld[0:64, :], wview[:, swap, :]),
                   r=[B("wld")], w=[B("wld")], l=f"{tag}a{swap}")
                DS(lambda e, swap=swap, wview=wview: e.dma_start(self.wld[64:128, :], wview[:, 1 - swap, :]),
                   r=[B("wld")], w=[B("wld")], l=f"{tag}b{swap}")
                for c in range(NCH):
                    T(lambda e, c=c: e.transpose(self.PAB[0:128, 0:128],
                                                 self.wld[:, c * 128:(c + 1) * 128], self.iden_f[:]),
                      r=[B("wld"), B("iden"), B("pA0")], w=[B("pA0")], l=f"t{tag}{swap}{c}")
                    V(lambda e, c=c, dst=dst, tag=tag, swap=swap: e.tensor_copy(
                        dst[:, c, :], self.PAB[0:128, 0:128]),
                      r=[B("pA0")], w=[B(f"{tag}{swap}")], l=f"e{tag}{swap}{c}")
        outer_w("wq", self.wqX, self.wqXs, "wq")
        outer_w("wk", self.wkX, self.wkXs, "wk")
        DS(lambda e: e.dma_start(self.wld[:, :], i["wv"][:]), r=[B("wld")], w=[B("wld")], l="ldwv")
        for c in range(NCH):
            T(lambda e, c=c: e.transpose(self.PAB[0:128, 0:128],
                                         self.wld[:, c * 128:(c + 1) * 128], self.iden_f[:]),
              r=[B("wld"), B("iden"), B("pA0")], w=[B("pA0")], l=f"twv{c}")
            V(lambda e, c=c: e.tensor_copy(self.wvT[:, c, :], self.PAB[0:128, 0:128]),
              r=[B("pA0")], w=[B("wvT")], l=f"evwv{c}")
        for c in range(NCH):
            DS(lambda e, c=c: e.dma_start(self.wld2[:, :], i["wo"][c * 128:(c + 1) * 128, :]),
               r=[B("wld2")], w=[B("wld2")], l=f"ldwo{c}")
            T(lambda e: e.transpose(self.PAB[0:128, 128:256], self.wld2[:, :], self.iden_f[:]),
              r=[B("wld2"), B("iden"), B("pA0")], w=[B("pA0")], l=f"two{c}")
            V(lambda e, c=c: e.tensor_copy(self.woT[:, c, :], self.PAB[0:128, 128:256]),
              r=[B("pA0")], w=[B("woT")], l=f"evwo{c}")
        for c in range(NCH):
            DS(lambda e, c=c: e.dma_start(self.wld[:, 0:RLAT + KR], i["Wkva"][c * 128:(c + 1) * 128, :]),
               r=[B("wld"), B("WkvaA")], w=[B("wld")], l=f"ldkva{c}")
            for (o_d, o_s, dims) in ((0, 0, [[1, 128]]), (128, 128, [[1, 2], [2, 16]]),
                                     (160, 128 + 33, [[2, 16]]), (176, 128 + 32, [[2, 16]])):
                V(lambda e, c=c, o_d=o_d, o_s=o_s, dims=dims: e.tensor_scalar(
                    self.WkvaA[:, c, o_d:o_d + _dims_n(dims)],
                    self.ap(self.wld, [128, D], 0, 128, o_s, dims),
                    self.wln1_c[:, c:c + 1], None, ALU.mult),
                  r=[B("wld"), B("w_ln1")], w=[B("WkvaA")], l=f"gkva{c}_{o_d}")
        DG(lambda e: e.dma_start(self.Wkvb_n[:], i["Wkvb"][:]), w=[B("Wkvb_n")], l="ldKvbN")
        DS(lambda e: e.dma_start(self.wld[:, 0:512], i["Wkvb"][:]),
           r=[B("wld"), B("WkvaA")], w=[B("wld")], l="ldKvbF")
        for c in range(4):
            T(lambda e, c=c: e.transpose(self.PAB[0:128, 0:128],
                                         self.wld[:, c * 128:(c + 1) * 128], self.iden_f[:]),
              r=[B("wld"), B("iden"), B("pA0")], w=[B("pA0")], l=f"tkvb{c}")
            V(lambda e, c=c: e.tensor_copy(self.WkvbT[:, c, :], self.PAB[0:128, 0:128]),
              r=[B("pA0")], w=[B("WkvbT")], l=f"ekvb{c}")
        DG(lambda e: e.dma_start(self.Wgate_s[:], i["Wgate"].rearrange("(c p) e -> p c e", p=128)),
           w=[B("Wgate")], l="ldWgate")

    def phase_outer(self):
        B, T, V, S_, G, DS, DG = self.B, self.T, self.V, self.S_, self.G, self.DS, self.DG
        i = self.inp
        for blk in range(self.NQB):
            pp = blk % 2
            t0 = blk * QB
            DS(lambda e, t0=t0: e.dma_start(self.AB_blk[:, 0, :], i["Aout"][:, t0:t0 + QB]),
               r=[B("ABb")], w=[B("ABb")], l=f"ldA{blk}")
            DS(lambda e, t0=t0: e.dma_start(self.AB_blk[:, 1, :], i["Bout"][:, t0:t0 + QB]),
               r=[B("ABb")], w=[B("ABb")], l=f"ldB{blk}")
            for s_ in range(4):
                ts_ = t0 + s_ * 128
                DG(lambda e, ts_=ts_, pp=pp: e.dma_start(self.x0tb[:, pp, :], i["x"][ts_:ts_ + 128, :]),
                   r=[B(f"x0tb{pp}")], w=[B(f"x0tb{pp}")], l=f"ldx0{blk}_{s_}")
                DS(lambda e, pp=pp, s_=s_: e.dma_start_transpose(
                    self.ap(self.x0f, [128, 2, NCH, 512], 0, 128, pp * NCH * 512 + s_ * 128,
                            [[512, NCH], [1, 128]]),
                    self.x0tb[:, pp, :]),
                   r=[B(f"x0tb{pp}"), B(f"x0f{pp}")], w=[B(f"x0f{pp}")], l=f"tx0{blk}_{s_}")

            def proj(off, wgt, wbuf, tag):
                for c in range(NCH):
                    T(lambda e, c=c, off=off, wgt=wgt, pp=pp: e.matmul(
                        self.S4[0:128, off:off + 512], wgt[:, c, :],
                        self.ap(self.x0f, [128, 2, NCH, 512], 0, 128, pp * NCH * 512 + c * 512,
                                [[1, 512]]),
                        start=(c == 0), stop=(c == NCH - 1)),
                      r=[B(f"x0f{pp}"), B(wbuf)] + ([B(tag)] if c == 0 else []),
                      w=[B(tag)], l=f"{tag}{blk}_{c}")
            proj(0, self.wqX, "wq0", "pS0")
            proj(512, self.wqXs, "wq1", "pS1")
            proj(1024, self.wkX, "wk0", "pS2")
            proj(1536, self.wkXs, "wk1", "pS3")
            for s_ in range(4):
                for c in range(NCH):
                    T(lambda e, s_=s_, c=c, pp=pp: e.matmul(
                        self.Y2[0:128, s_ * 128:(s_ + 1) * 128],
                        self.ap(self.x0f, [128, 2, NCH, 512], 0, 128,
                                pp * NCH * 512 + c * 512 + s_ * 128, [[1, 128]]),
                        self.wvT[:, c, :], start=(c == 0), stop=(c == NCH - 1)),
                      r=[B(f"x0f{pp}"), B("wvT")] + ([B("pY0")] if c == 0 else []),
                      w=[B("pY0")], l=f"v{blk}_{s_}_{c}")
                V(lambda e, s_=s_, blk=blk: e.tensor_copy(
                    self.ap(self.vt, [128, self.NKT, 129], 0, 128, (blk * 4 + s_) * 129,
                            [[65, 2], [1, 64]]),
                    self.Y2[0:128, s_ * 128:(s_ + 1) * 128]),
                  r=[B("pY0"), B("vt")], w=[B("vt")], l=f"ev{blk}_{s_}")

            def rope(psA, psAs, dstA, dstB, bA, bAs, tag):
                V(lambda e, psA=psA: e.tensor_tensor(
                    self.t1s[:, :], self.S4[0:128, psA:psA + 512], self.AB_blk[:, 0, :],
                    ALU.mult),
                  r=[B(bA), B("ABb"), B("t1")], w=[B("t1")], l=f"r1{tag}{blk}")
                V(lambda e, psAs=psAs: e.tensor_tensor(
                    self.t2s[:, :], self.S4[0:128, psAs:psAs + 512], self.AB_blk[:, 1, :],
                    ALU.mult),
                  r=[B(bAs), B("ABb"), B("t2")], w=[B("t2")], l=f"r2{tag}{blk}")
                V(lambda e, dstA=dstA, t0=t0, tag=tag: e.tensor_tensor(
                    dstA[0:64, t0:t0 + 512], self.t1s[0:64, :], self.t2s[0:64, :], ALU.add),
                  r=[B("t1"), B("t2"), B(f"{tag}A")], w=[B(f"{tag}A")], l=f"r3{tag}{blk}")
                V(lambda e, dstA=dstA, t0=t0, tag=tag: e.tensor_tensor(
                    dstA[64:128, t0:t0 + 512], self.t1s[64:128, :], self.t2s[64:128, :], ALU.add),
                  r=[B("t1"), B("t2"), B(f"{tag}A")], w=[B(f"{tag}A")], l=f"r4{tag}{blk}")
            rope(0, 512, self.qA, None, "pS0", "pS1", "q")
            rope(1024, 1536, self.kA, None, "pS2", "pS3", "k")

            S_(lambda e: e.activation(self.sqX[:, 0, :], self.S4[0:128, 0:512], AF.Square),
               r=[B("pS0"), B("sqX")], w=[B("sqX")], l=f"sqQ{blk}")
            T(lambda e: e.matmul(self.PAB[0:1, 0:512], self.ones_b[:, 0:1], self.sqX[:, 0, :],
                                 start=True, stop=True),
              r=[B("sqX"), B("ones"), B("pA0")], w=[B("pA0")], l=f"mmQn{blk}")
            V(lambda e: e.reduce_max(self.mxk[0:1, 6:7], self.PAB[0:1, 0:512], axis=AX.X),
              r=[B("pA0"), B("mxk")], w=[B("mxk")], l=f"qmx{blk}")
            V(lambda e: e.tensor_tensor(self.mxk[0:1, 3:4], self.mxk[0:1, 3:4],
                                        self.mxk[0:1, 6:7], ALU.max),
              r=[B("mxk")], w=[B("mxk")], l=f"qmr{blk}")
            S_(lambda e: e.activation(self.sqX[:, 0, :], self.S4[0:128, 1024:1536], AF.Square),
               r=[B("pS2"), B("sqX")], w=[B("sqX")], l=f"sqK{blk}")
            T(lambda e: e.matmul(self.PAB[0:1, 0:512], self.ones_b[:, 0:1], self.sqX[:, 0, :],
                                 start=True, stop=True),
              r=[B("sqX"), B("ones"), B("pA0")], w=[B("pA0")], l=f"mmKn{blk}")
            V(lambda e: e.reduce_max(self.mxk[0:1, 1:2], self.PAB[0:1, 0:512], axis=AX.X),
              r=[B("pA0"), B("mxk")], w=[B("mxk")], l=f"kmx{blk}")
            V(lambda e: e.tensor_tensor(self.mxk[0:1, 0:1], self.mxk[0:1, 0:1],
                                        self.mxk[0:1, 1:2], ALU.max),
              r=[B("mxk")], w=[B("mxk")], l=f"kmr{blk}")

        V(lambda e: e.tensor_tensor(self.mxk[0:1, 2:3], self.mxk[0:1, 0:1],
                                    self.mxk[0:1, 3:4], ALU.add),
          r=[B("mxk")], w=[B("mxk")], l="gb1")
        V(lambda e: e.tensor_scalar(self.mxk[0:1, 2:3], self.mxk[0:1, 2:3],
                                    -0.5 / math.sqrt(OHD), None, ALU.mult),
          r=[B("mxk")], w=[B("mxk")], l="gb2")
        T(lambda e: e.matmul(self.PAB[0:128, 512:513], self.ones_f[0:1, :],
                             self.mxk[0:1, 2:3], start=True, stop=True),
          r=[B("mxk"), B("onesf"), B("pA1")], w=[B("pA1")], l="gb3")
        V(lambda e: e.tensor_copy(self.biasG[:, :], self.PAB[0:128, 512:513]),
          r=[B("pA1"), B("bG")], w=[B("bG")], l="gb4")

        for qb in range(self.NQB):
            t0 = qb * QB
            def oav(kp2):
                for kt in (2 * kp2, 2 * kp2 + 1):
                    eb = f"expO{kt % 4}"
                    T(lambda e, kt=kt: e.matmul(
                        self.Y2[0:65, 0:512],
                        self.ap(self.vt, [128, self.NKT, 129], 0, 128, kt * 129, [[1, 65]]),
                        self.expO[:, kt % 4, :], start=(kt == 0), stop=(kt == self.NKT - 1)),
                      r=[B(eb), B("vt")] + ([B("pY0")] if kt == 0 else []),
                      w=[B("pY0")], l=f"a1_{qb}_{kt}")
                    T(lambda e, kt=kt: e.matmul(
                        self.Y2[0:64, 512:1024],
                        self.ap(self.vt, [128, self.NKT, 129], 0, 128, kt * 129 + 65, [[1, 64]]),
                        self.expO[:, kt % 4, :], start=(kt == 0), stop=(kt == self.NKT - 1)),
                      r=[B(eb), B("vt")] + ([B("pY1")] if kt == 0 else []),
                      w=[B("pY1")], l=f"a2_{qb}_{kt}")

            for kp in range(self.NKT // 2):
                for kt in (2 * kp, 2 * kp + 1):
                    so = 512 * (kt % 4)
                    pbank = f"pS{kt % 4}"
                    T(lambda e, kt=kt, t0=t0, so=so: e.matmul(
                        self.S4[0:128, so:so + 512], self.kA[:, kt * KT:(kt + 1) * KT],
                        self.qA[:, t0:t0 + 512], start=True, stop=True),
                      r=[B("qA"), B("kA"), B(pbank)], w=[B(pbank)], l=f"s1_{qb}_{kt}")
                b0 = (2 * kp) % 4
                S_(lambda e, b0=b0: e.activation(
                    self.ap(self.expO, [128, 4, 512], 0, 128, b0 * 512, [[1, 1024]]),
                    self.S4[0:128, b0 * 512:(b0 + 2) * 512], AF.Exp,
                    bias=self.biasG[:, :], scale=1.0 / math.sqrt(OHD)),
                   r=[B(f"pS{b0}"), B(f"pS{b0 + 1}"), B("bG"),
                      B(f"expO{b0}"), B(f"expO{b0 + 1}")],
                   w=[B(f"expO{b0}"), B(f"expO{b0 + 1}")], l=f"e_{qb}_{kp}")
                if kp > 0:
                    oav(kp - 1)
            oav(self.NKT // 2 - 1)
            V(lambda e: e.tensor_copy(self.sqX[0:64, 0, :], self.Y2[0:64, 0:512]),
              r=[B("pY0"), B("sqX")], w=[B("sqX")], l=f"ey1_{qb}")
            V(lambda e: e.tensor_copy(self.sqX[64:128, 0, :], self.Y2[0:64, 512:1024]),
              r=[B("pY1"), B("sqX")], w=[B("sqX")], l=f"ey2_{qb}")
            V(lambda e: e.reciprocal(self.nkrow[:], self.Y2[64:65, 0:512]),
              r=[B("pY0"), B("nkr")], w=[B("nkr")], l=f"rD_{qb}")
            for s_ in range(4):
                T(lambda e, s_=s_: e.matmul(
                    self.PAB[0:128, 512 + s_:513 + s_],
                    self.nkrow[0:1, s_ * 128:(s_ + 1) * 128], self.ones_f[0:1, 0:1],
                    start=True, stop=True),
                  r=[B("nkr"), B("onesf")] + ([B("pA1")] if s_ == 0 else []),
                  w=[B("pA1")], l=f"rdc_{qb}_{s_}")
            V(lambda e: e.tensor_copy(self.stc[:, 0:4], self.PAB[0:128, 512:516]),
              r=[B("pA1"), B("stc")], w=[B("stc")], l=f"erdc_{qb}")

            for s_ in range(4):
                sub = qb * 4 + s_
                ts_ = t0 + s_ * 128
                xp = s_ % 2
                DS(lambda e, ts_=ts_, xp=xp: e.dma_start(self.x0tf[:, xp, :],
                                                          i["x"][ts_:ts_ + 128, :]),
                   r=[B(f"x0tf{xp}")], w=[B(f"x0tf{xp}")], l=f"ldxf{sub}")
                wps = self.PAB
                wb0 = "pA0"
                wb1 = "pA1"
                for half in range(2):
                    T(lambda e, s_=s_, half=half, wps=wps: e.matmul(
                        wps[0:128, half * 512:(half + 1) * 512],
                        self.sqX[:, 0, s_ * 128:(s_ + 1) * 128],
                        self.ap(self.woT, [128, NCH, 128], 0, 128, half * 512, [[1, 512]]),
                        start=True, stop=True),
                      r=[B("sqX"), B("woT"), B(wb0 if half == 0 else wb1)],
                      w=[B(wb0 if half == 0 else wb1)], l=f"wo_{sub}_{half}")
                xp2 = s_ % 2
                vsx = self.vscr if xp2 == 0 else self.vscr2
                vbx = "vscr" if xp2 == 0 else "vscr2"
                ca = 8 if xp2 == 0 else 16
                V(lambda e, s_=s_, wps=wps, vsx=vsx, ca=ca, xp=xp: e.scalar_tensor_tensor(
                    vsx[:, :], wps[0:128, 0:1024], self.stc[:, s_:s_ + 1],
                    self.x0tf[:, xp, :], ALU.mult, ALU.add,
                    accum_out=self.stc[:, ca:ca + 1]),
                  r=[B(wb0), B(wb1), B("stc"), B(f"x0tf{xp}"), B(vbx), B(f"stc{xp2}")],
                  w=[B(vbx), B(f"stc{xp2}")], l=f"vres_{sub}")
                self._ln1(qb, sub)
            self._mla_kv(qb)

    def _ln1(self, qb, sub):
        B, T, V, S_, DS, DG = self.B, self.T, self.V, self.S_, self.DS, self.DG
        st = self.stc
        xp = sub % 2
        cb_ = 8 if xp == 0 else 16
        sb_ = f"stc{xp}"
        vs = self.vscr if xp == 0 else self.vscr2
        vb = "vscr" if xp == 0 else "vscr2"
        c0, c1, c2, c3, c4, c5 = cb_, cb_ + 1, cb_ + 2, cb_ + 3, cb_ + 4, cb_ + 5
        c6, c7 = cb_ + 6, cb_ + 7
        V(lambda e: e.scalar_tensor_tensor(self.sqscr[:, :], vs[:, :], 1.0,
                                           vs[:, :], ALU.mult, ALU.mult,
                                           accum_out=st[:, c1:c1 + 1]),
          r=[B(vb), B("sqscr"), B(sb_)], w=[B("sqscr"), B(sb_)], l=f"v2_{sub}")
        V(lambda e: e.tensor_scalar(st[:, c2:c2 + 1], st[:, c0:c0 + 1], 1.0 / D, None, ALU.mult),
          r=[B(sb_)], w=[B(sb_)], l=f"mu_{sub}")
        V(lambda e: e.tensor_tensor(st[:, c3:c3 + 1], st[:, c2:c2 + 1], st[:, c2:c2 + 1], ALU.mult),
          r=[B(sb_)], w=[B(sb_)], l=f"mu2_{sub}")
        V(lambda e: e.scalar_tensor_tensor(st[:, c3:c3 + 1], st[:, c1:c1 + 1], 1.0 / D,
                                           st[:, c3:c3 + 1], ALU.mult, ALU.subtract),
          r=[B(sb_)], w=[B(sb_)], l=f"var_{sub}")
        S_(lambda e: e.activation(st[:, c4:c4 + 1], st[:, c3:c3 + 1], AF.Ln, bias=EPS),
           r=[B(sb_), B("econst")], w=[B(sb_)], l=f"ln_{sub}")
        S_(lambda e: e.activation(st[:, c4:c4 + 1], st[:, c4:c4 + 1], AF.Exp, scale=-0.5),
           r=[B(sb_)], w=[B(sb_)], l=f"rstd_{sub}")
        V(lambda e: e.tensor_scalar(st[:, c5:c5 + 1], st[:, c3:c3 + 1], EPS, None, ALU.add),
          r=[B(sb_)], w=[B(sb_)], l=f"ve_{sub}")
        V(lambda e: e.reciprocal(st[:, c5:c5 + 1], st[:, c5:c5 + 1]),
          r=[B(sb_)], w=[B(sb_)], l=f"vr_{sub}")
        V(lambda e: e.tensor_tensor(st[:, c5:c5 + 1], st[:, c3:c3 + 1], st[:, c5:c5 + 1], ALU.mult),
          r=[B(sb_)], w=[B(sb_)], l=f"m2_{sub}")
        S_(lambda e: e.activation(st[:, c6:c6 + 1], st[:, c5:c5 + 1], AF.Ln, bias=EPS),
           r=[B(sb_), B("econst")], w=[B(sb_)], l=f"lnr_{sub}")
        S_(lambda e: e.activation(st[:, c6:c6 + 1], st[:, c6:c6 + 1], AF.Exp, scale=-0.5),
           r=[B(sb_)], w=[B(sb_)], l=f"rr_{sub}")
        V(lambda e: e.scalar_tensor_tensor(st[:, c7:c7 + 1], st[:, c2:c2 + 1], -1.0,
                                           st[:, c4:c4 + 1], ALU.mult, ALU.mult),
          r=[B(sb_)], w=[B(sb_)], l=f"nm_{sub}")
        S_(lambda e: e.activation(self.xto[:, :], vs[:, :], AF.Identity,
                                  bias=st[:, c7:c7 + 1], scale=st[:, c4:c4 + 1]),
           r=[B(vb), B(sb_), B("xto")], w=[B("xto")], l=f"x1_{sub}")
        V(lambda e, xp=xp: e.tensor_scalar(self.xnb[:, xp, :], self.xto[:, :],
                                           st[:, c6:c6 + 1], None, ALU.mult),
          r=[B("xto"), B(sb_), B(f"xnb{xp}")], w=[B(f"xnb{xp}")], l=f"xn1_{sub}")
        V(lambda e, qb=qb, vs=vs: e.tensor_scalar(vs[:, :], self.xto[:, :],
                                                  self.ocol[:, qb:qb + 1], None, ALU.mult),
          r=[B("xto"), B("ocol"), B(vb)], w=[B(vb)], l=f"scx_{sub}")
        DS(lambda e, sub=sub, vs=vs: e.dma_start(self.x1_sp[sub * 128:(sub + 1) * 128, :], vs[:, :]),
           r=[B(vb)], w=[B("x1sp")], l=f"spx_{sub}")
        V(lambda e, qb=qb, xp=xp: e.tensor_scalar(self.sqscr[:, 0:D], self.xnb[:, xp, :],
                                                  self.ocol[:, qb:qb + 1], None, ALU.mult),
          r=[B(f"xnb{xp}"), B("ocol"), B("sqscr")], w=[B("sqscr")], l=f"scn_{sub}")
        DS(lambda e, sub=sub: e.dma_start(self.xn1_sp[sub * 128:(sub + 1) * 128, :],
                                          self.sqscr[:, 0:D]),
           r=[B("sqscr")], w=[B("xn1sp")], l=f"spn_{sub}")
        s_ = sub % 4
        DS(lambda e, s_=s_, xp=xp: e.dma_start_transpose(
            self.ap(self.xn1f_oa, [128, 4, NCH, 128], 0, 128, s_ * NCH * 128,
                    [[128, NCH], [1, 128]]),
            self.xnb[:, xp, :]),
           r=[B(f"xnb{xp}"), B("xn1f")], w=[B("xn1f")], l=f"txn_{sub}")

    def _mla_kv(self, qb):
        B, T, V, S_, DS = self.B, self.T, self.V, self.S_, self.DS
        i = self.inp
        t0 = qb * QB
        pp = qb % 2
        DS(lambda e, t0=t0: e.dma_start(self.ABk_blk[:, 0, :], i["Ak"][:, t0:t0 + QB]),
           r=[B("ABk")], w=[B("ABk")], l=f"ldAk{qb}")
        DS(lambda e, t0=t0: e.dma_start(self.ABk_blk[:, 1, :], i["Bk"][:, t0:t0 + QB]),
           r=[B("ABk")], w=[B("ABk")], l=f"ldBk{qb}")
        for c in range(NCH):
            rhs = lambda c=c: self.ap(self.xn1f_oa, [128, 4, NCH, 128], 0, 128, c * 128,
                                      [[NCH * 128, 4], [1, 128]])
            T(lambda e, c=c, rhs=rhs: e.matmul(
                self.PAB[0:128, 0:512],
                self.ap(self.WkvaA, [128, NCH, 192], 0, 128, c * 192, [[1, 128]]),
                rhs(), start=(c == 0), stop=(c == NCH - 1)),
              r=[B("xn1f"), B("WkvaA")] + ([B("pA0")] if c == 0 else []),
              w=[B("pA0")], l=f"lat{qb}_{c}")
            T(lambda e, c=c, rhs=rhs: e.matmul(
                self.PAB[0:32, 512:1024],
                self.ap(self.WkvaA, [128, NCH, 192], 0, 128, c * 192 + 128, [[1, 32]]),
                rhs(), start=(c == 0), stop=(c == NCH - 1)),
              r=[B("xn1f"), B("WkvaA")] + ([B("pA1")] if c == 0 else []),
              w=[B("pA1")], l=f"krX{qb}_{c}")
        for c in range(NCH):
            rhs = lambda c=c: self.ap(self.xn1f_oa, [128, 4, NCH, 128], 0, 128, c * 128,
                                      [[NCH * 128, 4], [1, 128]])
            T(lambda e, c=c, rhs=rhs: e.matmul(
                self.PAB[64:96, 512:1024],
                self.ap(self.WkvaA, [128, NCH, 192], 0, 128, c * 192 + 160, [[1, 32]]),
                rhs(), start=(c == 0), stop=(c == NCH - 1)),
              r=[B("xn1f"), B("WkvaA")], w=[B("pA1")], l=f"krXs{qb}_{c}")
        V(lambda e: e.tensor_copy(self.lat[:, :], self.PAB[0:128, 0:512]),
          r=[B("pA0"), B("lat")], w=[B("lat")], l=f"elat{qb}")
        V(lambda e: e.tensor_tensor(self.t1s[0:32, :], self.PAB[0:32, 512:1024],
                                    self.ABk_blk[:, 0, :], ALU.mult),
          r=[B("pA1"), B("ABk"), B("t1")], w=[B("t1")], l=f"kr1{qb}")
        V(lambda e: e.tensor_tensor(self.t2s[0:32, :], self.PAB[64:96, 512:1024],
                                    self.ABk_blk[:, 1, :], ALU.mult),
          r=[B("pA1"), B("ABk"), B("t2")], w=[B("t2")], l=f"kr2{qb}")
        V(lambda e, t0=t0: e.tensor_tensor(self.krass[0:32, t0:t0 + 512],
                                           self.t1s[0:32, :], self.t2s[0:32, :], ALU.add),
          r=[B("t1"), B("t2"), B("krass")], w=[B("krass")], l=f"kr3{qb}")
        V(lambda e, t0=t0: e.tensor_copy(self.krass[64:96, t0:t0 + 512],
                                         self.krass[0:32, t0:t0 + 512]),
          r=[B("krass")], w=[B("krass")], l=f"kr4{qb}")
        for c in range(4):
            yo = 512 * (c % 2)
            yb = f"pY{c % 2}"
            T(lambda e, c=c, yo=yo: e.matmul(self.Y2[0:128, yo:yo + 512],
                                             self.WkvbT[:, c, :],
                                             self.lat[:, :], start=True, stop=True),
              r=[B("lat"), B("WkvbT"), B(yb)], w=[B(yb)], l=f"kvf{qb}_{c}")
            V(lambda e, c=c, t0=t0, yo=yo: e.tensor_copy(self.kvpack[:, c, t0:t0 + 512],
                                                         self.Y2[0:128, yo:yo + 512]),
              r=[B(yb), B("kvp")], w=[B("kvp")], l=f"ekvf{qb}_{c}")
        for s_ in range(4):
            kt = qb * 4 + s_
            yo = 512 * (s_ % 2)
            yb = f"pY{s_ % 2}"
            T(lambda e, s_=s_, yo=yo: e.matmul(self.Y2[0:128, yo:yo + 512],
                                               self.lat[:, s_ * 128:(s_ + 1) * 128],
                                               self.Wkvb_n[:, :], start=True, stop=True),
              r=[B("lat"), B("Wkvb_n"), B(yb)], w=[B(yb)], l=f"kvt{kt}")
            V(lambda e, kt=kt, yo=yo: e.tensor_copy(
                self.ap(self.kvt, [128, self.NKT, HEADS * 33], 0, 128, kt * HEADS * 33,
                        [[33, 16], [1, 32]]),
                self.Y2[0:128, yo:yo + 512]),
              r=[B(yb), B("kvt")], w=[B("kvt")], l=f"ekvt{kt}")


    def phase_mla(self, mstop=None):
        B, T, V, S_, G, DS, DG = self.B, self.T, self.V, self.S_, self.G, self.DS, self.DG
        i = self.inp
        for c in range(NCH):
            wt = self.x1su if c % 2 == 0 else self.x1s2
            wtb = "x1su" if c % 2 == 0 else "x1s2"
            DS(lambda e, c=c, wt=wt: e.dma_start(wt[:, :], i["Wq"][c * 128:(c + 1) * 128, :]),
               r=[B(wtb), B("WqA")], w=[B(wtb)], l=f"ldWq{c}")
            for (o_d, o_s, dims, n_) in (
                    (0, 32, [[64, HEADS], [1, 2], [2, 16]], 32),
                    (32, 0, [[64, HEADS], [1, 32]], 32),
                    (64, 33, [[64, HEADS], [2, 16]], 16),
                    (80, 32, [[64, HEADS], [2, 16]], 16)):
                V(lambda e, c=c, o_d=o_d, o_s=o_s, dims=dims, n_=n_, wt=wt: e.tensor_scalar(
                    self.ap(self.WqA, [128, NCH, HEADS, 96], 0, 128,
                            c * HEADS * 96 + o_d, [[96, HEADS], [1, n_]]),
                    self.ap(wt, [128, D], 0, 128, o_s, dims),
                    self.wln1_c[:, c:c + 1], None, ALU.mult),
                  r=[B(wtb), B("w_ln1")], w=[B("WqA")], l=f"g{c}_{o_d}")
        for j in range(4):
            DG(lambda e, j=j: e.dma_start(
                self.WoE[:, j, :],
                bass.AP(i["Wo"].tensor, (256 * j) * D, [[64 * D, 4], [D, 32], [1, D]])),
               w=[B("WoE")], l=f"ldWoE{j}")
        DS(lambda e: e.dma_start(self.masks_s[:], i["masks"][:]), w=[B("masks")], l="ldmask")
        DS(lambda e: e.dma_start(self.nkrow[0:1, 0:sum(self.Ls)], i["padrow"][:]),
           r=[B("nkr")], w=[B("nkr")], l="ldpad")
        T(lambda e: e.matmul(self.PAB[0:128, 0:sum(self.Ls)], self.ones_f[0:1, :],
                             self.nkrow[0:1, 0:sum(self.Ls)], start=True, stop=True),
          r=[B("nkr"), B("onesf"), B("pA0")], w=[B("pA0")], l="mmpad")
        V(lambda e: e.tensor_copy(self.biasKsl[:, :], self.PAB[0:128, 0:sum(self.Ls)]),
          r=[B("pA0"), B("biasKsl")], w=[B("biasKsl")], l="bsl")
        V(lambda e: e.memset(self.mxk[0:1, 4:5], 0.0), r=[B("mxk")], w=[B("mxk")], l="zmxm")
        for qb in range(self.NQB):
            t0 = qb * QB
            for c in range(4):
                S_(lambda e, c=c, t0=t0: e.activation(
                    self.sqX[:, 0, :], self.kvpack[:, c, t0:t0 + 512], AF.Square),
                   r=[B("kvp"), B("sqX0")], w=[B("sqX0")], l=f"mks{qb}_{c}")
                T(lambda e, c=c: e.matmul(self.S4[0:1, 0:512], self.ones_b[:, 0:1],
                                          self.sqX[:, 0, :], start=(c == 0), stop=False),
                  r=[B("sqX0"), B("ones")] + ([B("pS0")] if c == 0 else []),
                  w=[B("pS0")], l=f"mkm{qb}_{c}")
            S_(lambda e, t0=t0: e.activation(self.sqX[0:32, 0, :],
                                             self.krass[0:32, t0:t0 + 512], AF.Square),
               r=[B("krass"), B("sqX0")], w=[B("sqX0")], l=f"mksr{qb}")
            T(lambda e: e.matmul(self.S4[0:1, 0:512], self.ones_b[0:32, 0:1],
                                 self.sqX[0:32, 0, :], start=False, stop=True),
              r=[B("sqX0"), B("ones")], w=[B("pS0")], l=f"mkmr{qb}")
            V(lambda e: e.reduce_max(self.mxk[0:1, 5:6], self.S4[0:1, 0:512], axis=AX.X),
              r=[B("pS0"), B("mxk")], w=[B("mxk")], l=f"mkx{qb}")
            V(lambda e: e.tensor_tensor(self.mxk[0:1, 4:5], self.mxk[0:1, 4:5],
                                        self.mxk[0:1, 5:6], ALU.max),
              r=[B("mxk")], w=[B("mxk")], l=f"mkr2{qb}")

        V(lambda e: e.memset(self.Qra[:, :, :], 0.0), r=[B("Qra")], w=[B("Qra")], l="zQra")
        V(lambda e: e.memset(self.sel32[:, :], 0.0), r=[B("sel32")], w=[B("sel32")], l="zsel")
        for m_ in range(4):
            V(lambda e, m_=m_: e.memset(
                self.sel32[32 * m_:32 * m_ + 1, 32 * m_:32 * m_ + 32], 1.0),
              r=[B("sel32")], w=[B("sel32")], l=f"sel{m_}")
        V(lambda e: e.memset(self.rDc[:, :], 0.0), r=[B("rDc")], w=[B("rDc")], l="zrdc")
        if mstop == "prep":
            return
        def emit_reloads(sn):
            if sn >= self.NSLOT:
                return
            pb = sn % 2
            q0n = sn * QB
            for s_ in range(4):
                xp = s_ % 2
                t0_ = 2 * sn * QB + s_ * 128
                t1_ = (2 * sn + 1) * QB + s_ * 128
                DG(lambda e, t0_=t0_, xp=xp: e.dma_start(
                    self.xnsu[:, xp, :], self.xn1_sp[t0_:t0_ + 128, :]),
                   r=[B("xn1sp"), B(f"xnsu{xp}")], w=[B(f"xnsu{xp}")], l=f"lxnA{sn}_{s_}")
                DG(lambda e, t1_=t1_: e.dma_start(
                    self.xns2[:, :], self.xn1_sp[t1_:t1_ + 128, :]),
                   r=[B("xn1sp"), B("xns2")], w=[B("xns2")], l=f"lxnB{sn}_{s_}")
                V(lambda e, xp=xp: e.tensor_tensor(self.xnsu[:, xp, :], self.xnsu[:, xp, :],
                                                   self.xns2[:, :], ALU.add),
                  r=[B("xns2"), B(f"xnsu{xp}")], w=[B(f"xnsu{xp}")], l=f"sxn{sn}_{s_}")
                DS(lambda e, s_=s_, xp=xp, pb=pb: e.dma_start_transpose(
                    self.ap(self.xn1f, [128, 2, 4, NCH, 128], 0, 128,
                            (pb * 4 + s_) * NCH * 128, [[128, NCH], [1, 128]]),
                    self.xnsu[:, xp, :]),
                   r=[B(f"xnsu{xp}"), B(f"xn1fM{pb}")], w=[B(f"xn1fM{pb}")],
                   l=f"txq{sn}_{s_}")
            DS(lambda e, q0n=q0n: e.dma_start(self.ABq_sl[:, 0, :], i["Aq"][:, q0n:q0n + QB]),
               r=[B("ABq")], w=[B("ABq")], l=f"ldAq{sn}")
            DS(lambda e, q0n=q0n: e.dma_start(self.ABq_sl[:, 1, :], i["Bq"][:, q0n:q0n + QB]),
               r=[B("ABq")], w=[B("ABq")], l=f"ldBq{sn}")

        emit_reloads(0)
        sl_off = 0
        for s in range(self.NSLOT):
            if mstop in ("rel1", "q1", "att1", "slot1") and s > 0:
                break
            L = self.Ls[s]
            pb = s % 2

            if mstop == "rel1":
                continue
            import os
            qheads = int(os.environ.get("K_QHEADS", HEADS))
            qskip = os.environ.get("K_QSKIP", "")
            aug_q = []

            def flush_aug():
                while aug_q:
                    fn = aug_q.pop(0)
                    fn()

            for h in range(qheads):
                hr = h % 4
                qps = (self.PAB, self.PAB, self.S4, self.S4)[hr]
                po = (0, 512, 512, 1024)[hr]
                qbn = ("pA0", "pA1", "pS1", "pS2")[hr]
                hp = h % 2
                for c in range(NCH):
                    T(lambda e, c=c, h=h, po=po, qps=qps, pb=pb: e.matmul(
                        qps[0:96, po:po + 512],
                        self.ap(self.WqA, [128, NCH, HEADS, 96], 0, 128, (c * HEADS + h) * 96,
                                [[1, 96]]),
                        self.ap(self.xn1f, [128, 2, 4, NCH, 128], 0, 128,
                                pb * 4 * NCH * 128 + c * 128,
                                [[NCH * 128, 4], [1, 128]]),
                        start=(c == 0), stop=(c == NCH - 1)),
                      r=[B(f"xn1fM{pb}"), B("WqA")] + ([B(qbn)] if c == 0 else []),
                      w=[B(qbn)], l=f"q{s}_{h}_{c}")
                m = h % 4
                j = h // 4
                if "n" not in qskip:
                    V(lambda e, m=m, j=j, po=po, qps=qps: e.tensor_copy(
                        self.Qnope[32 * m:32 * m + 32, j, :], qps[32:64, po:po + 512]),
                      r=[B(qbn), B("Qnope")], w=[B("Qnope")], l=f"eqn{s}_{h}")
                rb = 64 * (h % 2)
                cb = h // 2
                if "r" not in qskip:
                    V(lambda e, po=po, qps=qps: e.tensor_tensor(
                        self.t1s[0:32, :], qps[0:32, po:po + 512], self.ABq_sl[0:32, 0, :],
                        ALU.mult),
                      r=[B(qbn), B("ABq"), B("t1")], w=[B("t1")], l=f"qr1{s}_{h}")
                    V(lambda e, po=po, qps=qps: e.tensor_tensor(
                        self.t2s[0:32, :], qps[64:96, po:po + 512], self.ABq_sl[0:32, 1, :],
                        ALU.mult),
                      r=[B(qbn), B("ABq"), B("t2")], w=[B("t2")], l=f"qr2{s}_{h}")
                    V(lambda e, rb=rb, cb=cb: e.tensor_tensor(
                        self.Qra[rb:rb + 32, cb, :], self.t1s[0:32, :], self.t2s[0:32, :], ALU.add),
                      r=[B("t1"), B("t2"), B("Qra")], w=[B("Qra")], l=f"qr3{s}_{h}")
                if "a" not in qskip:
                    S_(lambda e, po=po, hp=hp, qps=qps: e.activation(
                        self.sqX[0:64, hp, :], qps[0:64, po:po + 512], AF.Square),
                       r=[B(qbn), B(f"sqX{hp}")], w=[B(f"sqX{hp}")], l=f"qs{s}_{h}")

                    def aug(hp=hp, rb=rb, cb=cb, s=s, h=h):
                        T(lambda e, hp=hp: e.matmul(
                            self.S4[0:1, 1536:2048], self.ones_b[0:64, 0:1],
                            self.sqX[0:64, hp, :], start=True, stop=True),
                          r=[B(f"sqX{hp}"), B("ones"), B("pS3")], w=[B("pS3")],
                          l=f"qn{s}_{h}")
                        V(lambda e, rb=rb, cb=cb: e.tensor_scalar(
                            self.Qra[rb + 32:rb + 33, cb, :], self.S4[0:1, 1536:2048],
                            self.mxk[0:1, 4:5], -0.5, ALU.add, ALU.mult),
                          r=[B("pS3"), B("mxk"), B("Qra")], w=[B("Qra")], l=f"eqa{s}_{h}")
                    aug_q.append(aug)
                    if len(aug_q) > 1:
                        aug_q.pop(0)()
            flush_aug()
            emit_reloads(s + 1)

            if mstop == "q1":
                continue
            for j in range(4):
                def avs(kt, first, last):
                    ep = kt % 2
                    for m in range(4):
                        h = 4 * j + m
                        ypt = (self.PAB, self.PAB, self.Y2, self.Y2)[m]
                        yb = 512 * (m % 2)
                        ybn = ("pA0", "pA1", "pY0", "pY1")[m]
                        T(lambda e, m=m, kt=kt, h=h, ypt=ypt, yb=yb, ep=ep,
                          first=first, last=last: e.matmul(
                              ypt[0:33, yb:yb + 512],
                              self.ap(self.kvt, [128, self.NKT, HEADS * 33], 0, 128,
                                      kt * HEADS * 33 + 33 * h, [[1, 33]]),
                              self.expT[:, ep, m, :], start=first, stop=last),
                          r=[B(f"ex{ep}{m}"), B("kvt")] + ([B(ybn)] if first else []),
                          w=[B(ybn)], l=f"a{s}_{j}_{kt}_{m}")
                for kt in range(L):
                    mskd = (kt >= 8 * s)
                    ep = kt % 2
                    for m in range(4):
                        h = 4 * j + m
                        so = 512 * m
                        psn = f"pS{m}"
                        T(lambda e, m=m, j=j, kt=kt, so=so: e.matmul(
                            self.S4[0:128, so:so + 512],
                            self.kvpack[32 * m:32 * m + 32, j, kt * KT:(kt + 1) * KT],
                            self.Qnope[32 * m:32 * m + 32, j, :],
                            start=True, stop=False, tile_position=(32 * m, 0)),
                          r=[B("kvp"), B("Qnope"), B(psn)], w=[B(psn)], l=f"n{s}_{j}_{kt}_{m}")
                        rb = 64 * (m % 2)
                        cb = 2 * j + m // 2
                        T(lambda e, rb=rb, cb=cb, kt=kt, so=so: e.matmul(
                            self.S4[0:128, so:so + 512],
                            self.krass[rb:rb + 64, kt * KT:(kt + 1) * KT],
                            self.Qra[rb:rb + 64, cb, :],
                            start=False, stop=True, tile_position=(rb, 0)),
                          r=[B("krass"), B("Qra")], w=[B(psn)], l=f"r{s}_{j}_{kt}_{m}")
                        if m % 2 == 1:
                            S_(lambda e, m=m, kt=kt, sl_off=sl_off, ep=ep: e.activation(
                                self.ap(self.expT, [128, 2, 4, QB], 0, 128,
                                        (ep * 4 + m - 1) * QB, [[1, 2 * QB]]),
                                self.S4[0:128, (m - 1) * 512:(m + 1) * 512], AF.Exp,
                                bias=self.biasKsl[:, sl_off + kt:sl_off + kt + 1], scale=0.125),
                               r=[B(f"pS{m-1}"), B(psn), B("biasKsl"),
                                  B(f"ex{ep}{m-1}"), B(f"ex{ep}{m}")],
                               w=[B(f"ex{ep}{m-1}"), B(f"ex{ep}{m}")],
                               l=f"x{s}_{j}_{kt}_{m}")
                    if mskd:
                        md = kt - 8 * s
                        V(lambda e, md=md, ep=ep: e.tensor_tensor(
                            self.ap(self.expT, [128, 2, 4, QB], 0, 128, ep * 4 * QB,
                                    [[1, 4 * QB]]),
                            self.ap(self.expT, [128, 2, 4, QB], 0, 128, ep * 4 * QB,
                                    [[1, 4 * QB]]),
                            self.ap(self.masks_s, [128, 8, QB], 0, 128, md * QB,
                                    [[0, 4], [1, QB]]),
                            ALU.mult),
                          r=[B("masks")] + [B(f"ex{ep}{m}") for m in range(4)],
                          w=[B(f"ex{ep}{m}") for m in range(4)], l=f"m{s}_{j}_{kt}")
                    if kt > 0:
                        avs(kt - 1, kt - 1 == 0, False)
                avs(L - 1, L == 1, True)
                ypts = (self.PAB, self.PAB, self.Y2, self.Y2)
                ybs = (0, 512, 0, 512)
                ybns = ("pA0", "pA1", "pY0", "pY1")
                for m in range(4):
                    V(lambda e, m=m: e.tensor_copy(
                        self.rDc[32 * m:32 * m + 1, :], ypts[m][32:33, ybs[m]:ybs[m] + 512]),
                      r=[B(ybns[m]), B("rDc")], w=[B("rDc")], l=f"rd{s}_{j}_{m}")
                T(lambda e: e.matmul(self.S4[0:128, 1536:2048], self.sel32[:, :],
                                     self.rDc[:, :], start=True, stop=True),
                  r=[B("rDc"), B("sel32"), B("pS3")], w=[B("pS3")], l=f"bc{s}_{j}")
                V(lambda e: e.reciprocal(self.Bbc4[:, :], self.S4[0:128, 1536:2048]),
                  r=[B("pS3"), B("Bbc")], w=[B("Bbc")], l=f"ebc{s}_{j}")
                for m in range(4):
                    V(lambda e, m=m, j=j: e.tensor_tensor(
                        self.yasm[32 * m:32 * m + 32, j, :],
                        ypts[m][0:32, ybs[m]:ybs[m] + 512],
                        self.Bbc4[32 * m:32 * m + 32, :], ALU.mult),
                      r=[B(ybns[m]), B("Bbc"), B("yasm")], w=[B("yasm")],
                      l=f"ny{s}_{j}_{m}")

            if mstop == "att1":
                continue
            for s_ in range(4):
                t0_ = 2 * s * QB + s_ * 128
                t1_ = (2 * s + 1) * QB + s_ * 128
                DS(lambda e, t0_=t0_: e.dma_start(
                    self.x1su[:, :], self.x1_sp[t0_:t0_ + 128, :]),
                   r=[B("x1sp"), B("x1su")], w=[B("x1su")], l=f"lx1A{s}_{s_}")
                DS(lambda e, t1_=t1_: e.dma_start(
                    self.x1s2[:, :], self.x1_sp[t1_:t1_ + 128, :]),
                   r=[B("x1sp"), B("x1s2")], w=[B("x1s2")], l=f"lx1B{s}_{s_}")
                V(lambda e: e.tensor_tensor(self.x1su[:, :], self.x1su[:, :],
                                            self.x1s2[:, :], ALU.add),
                  r=[B("x1s2"), B("x1su")], w=[B("x1su")], l=f"sx1{s}_{s_}")
                for j in range(4):
                    for half in range(2):
                        T(lambda e, s_=s_, j=j, half=half: e.matmul(
                            self.PAB[0:128, half * 512:(half + 1) * 512],
                            self.yasm[:, j, s_ * 128:(s_ + 1) * 128],
                            self.WoE[:, j, half * 512:(half + 1) * 512],
                            start=(j == 0), stop=(j == 3)),
                          r=[B("yasm"), B("WoE")] + ([B(f"pA{half}")] if j == 0 else []),
                          w=[B(f"pA{half}")], l=f"mo{s}_{s_}_{j}_{half}")
                V(lambda e: e.scalar_tensor_tensor(
                    self.vscr[:, :], self.PAB[0:128, 0:1024], 1.0, self.x1su[:, :],
                    ALU.mult, ALU.add),
                  r=[B("pA0"), B("pA1"), B("x1su"), B("vscr")], w=[B("vscr")], l=f"y1{s}_{s_}")
                V(lambda e: e.tensor_tensor(self.xto[:, :], self.vscr[:, :],
                                            self.x1su[:, :], ALU.add),
                  r=[B("vscr"), B("x1su"), B("xto")], w=[B("xto")], l=f"g{s}_{s_}")
                DS(lambda e, s=s, s_=s_: e.dma_start(
                    self.g_sp[(s * 4 + s_) * 128:(s * 4 + s_ + 1) * 128, :], self.xto[:, :]),
                   r=[B("xto")], w=[B("gsp")], l=f"spg{s}_{s_}")
                V(lambda e: e.scalar_tensor_tensor(self.sqscr[:, :], self.vscr[:, :], 1.0,
                                                   self.vscr[:, :], ALU.mult, ALU.mult,
                                                   accum_out=self.stc[:, 9:10]),
                  r=[B("vscr"), B("sqscr"), B("stc")], w=[B("sqscr"), B("stc")], l=f"y2{s}_{s_}")
                V(lambda e: e.tensor_scalar(self.stc[:, 10:11], self.stc[:, 9:10], 1.0 / D,
                                            None, ALU.mult),
                  r=[B("stc")], w=[B("stc")], l=f"ms{s}_{s_}")
                S_(lambda e: e.activation(self.stc[:, 11:12], self.stc[:, 10:11], AF.Ln,
                                          bias=EPS),
                   r=[B("stc"), B("econst")], w=[B("stc")], l=f"l2{s}_{s_}")
                S_(lambda e: e.activation(self.stc[:, 11:12], self.stc[:, 11:12], AF.Exp,
                                          scale=-0.5),
                   r=[B("stc")], w=[B("stc")], l=f"r2{s}_{s_}")
                xp = s_ % 2
                V(lambda e, xp=xp: e.tensor_scalar(self.xnb[:, xp, :], self.vscr[:, :],
                                                   self.stc[:, 11:12], None, ALU.mult),
                  r=[B("vscr"), B("stc"), B(f"xnb{xp}")], w=[B(f"xnb{xp}")], l=f"x2{s}_{s_}")
                so_ = s * 4 + s_
                DG(lambda e, so_=so_, xp=xp: e.dma_start(
                    self.xn2_sp[so_ * 128:(so_ + 1) * 128, :], self.xnb[:, xp, :]),
                   r=[B(f"xnb{xp}")], w=[B("xn2sp")], l=f"sp2{s}_{s_}")
            sl_off += L

    def phase_moe(self):
        B, T, V, S_, G, DS, DG = self.B, self.T, self.V, self.S_, self.G, self.DS, self.DG
        i = self.inp
        NOS = self.NOS
        def emit_rebuild(ss):
            xp = ss % 4
            DG(lambda e, ss=ss, xp=xp: e.dma_start(
                self.xe[:, xp, :], self.xn2_sp[ss * 128:(ss + 1) * 128, :]),
               r=[B("xn2sp"), B(f"xe{xp}")], w=[B(f"xe{xp}")], l=f"lx2{ss}")
            DS(lambda e, ss=ss, xp=xp: e.dma_start_transpose(
                self.ap(self.xn2f, [128, self.NOS, NCH, 128], 0, 128, ss * NCH * 128,
                        [[128, NCH], [1, 128]]),
                self.xe[:, xp, :]),
               r=[B(f"xe{xp}"), B("xn2f")], w=[B("xn2f")], l=f"tx2{ss}")
            for c in range(NCH):
                V(lambda e, ss=ss, c=c: e.tensor_scalar(
                    self.xn2f[:, ss, c, :], self.xn2f[:, ss, c, :],
                    self.wln2_c[:, c:c + 1], None, ALU.mult),
                  r=[B("xn2f"), B("w_ln2")], w=[B("xn2f")], l=f"w2{ss}_{c}")

        def emit_gate(ss):
            for c in range(NCH):
                T(lambda e, ss=ss, c=c: e.matmul(
                    self.S4[0:128, 1536:1536 + EE],
                    self.ap(self.xn2f, [128, NOS, NCH, 128], 0, 128, (ss * NCH + c) * 128,
                            [[1, 128]]),
                    self.Wgate_s[:, c, :], start=(c == 0), stop=(c == NCH - 1)),
                  r=[B("xn2f"), B("Wgate")] + ([B("pS3")] if c == 0 else []),
                  w=[B("pS3")], l=f"gm{ss}_{c}")
            gp = self.gpr
            V(lambda e: e.tensor_tensor(gp[:, 0, :], self.S4[0:128, 1536:1536 + EE],
                                        self.ebias_b[:, :], ALU.add),
              r=[B("pS3"), B("ebias"), B("gp")], w=[B("gp")], l=f"gb{ss}")
            V(lambda e: e.reduce_max(self.gcol[:, 0:1], gp[:, 0, :], axis=AX.X, negate=True),
              r=[B("gp"), B("gc")], w=[B("gc")], l=f"gx{ss}")
            S_(lambda e: e.activation(gp[:, 1, :], gp[:, 0, :], AF.Exp, bias=self.gcol[:, 0:1]),
               r=[B("gp"), B("gc")], w=[B("gp")], l=f"ge{ss}")
            V(lambda e: e.reduce_max(self.gcol[:, 1:2], gp[:, 1, :], axis=AX.X),
              r=[B("gp"), B("gc")], w=[B("gc")], l=f"m1{ss}")
            V(lambda e: e.tensor_scalar(gp[:, 2, :], gp[:, 1, :], self.gcol[:, 1:2], None,
                                        ALU.is_ge),
              r=[B("gp"), B("gc")], w=[B("gp")], l=f"q1{ss}")
            V(lambda e: e.tensor_tensor(gp[:, 3, :], gp[:, 1, :], gp[:, 2, :], ALU.mult),
              r=[B("gp")], w=[B("gp")], l=f"tt{ss}")
            V(lambda e: e.tensor_tensor(gp[:, 3, :], gp[:, 1, :], gp[:, 3, :], ALU.subtract),
              r=[B("gp")], w=[B("gp")], l=f"e2{ss}")
            V(lambda e: e.reduce_max(self.gcol[:, 2:3], gp[:, 3, :], axis=AX.X),
              r=[B("gp"), B("gc")], w=[B("gc")], l=f"m2{ss}")
            V(lambda e: e.tensor_scalar(gp[:, 3, :], gp[:, 3, :], self.gcol[:, 2:3], None,
                                        ALU.is_ge),
              r=[B("gp"), B("gc")], w=[B("gp")], l=f"q2{ss}")
            V(lambda e: e.tensor_tensor(gp[:, 2, :], gp[:, 2, :], gp[:, 3, :], ALU.add),
              r=[B("gp")], w=[B("gp")], l=f"qs{ss}")
            V(lambda e: e.tensor_tensor(gp[:, 2, :], gp[:, 1, :], gp[:, 2, :], ALU.mult),
              r=[B("gp")], w=[B("gp")], l=f"nu{ss}")
            V(lambda e: e.tensor_tensor(self.gcol[:, 3:4], self.gcol[:, 1:2],
                                        self.gcol[:, 2:3], ALU.add),
              r=[B("gc")], w=[B("gc")], l=f"de{ss}")
            V(lambda e: e.reciprocal(self.gcol[:, 4:5], self.gcol[:, 3:4]),
              r=[B("gc")], w=[B("gc")], l=f"rr{ss}")
            V(lambda e, ss=ss: e.tensor_scalar(self.comb_t[:, ss, :], gp[:, 2, :],
                                               self.gcol[:, 4:5], None, ALU.mult),
              r=[B("gp"), B("gc"), B("cmb")], w=[B("cmb")], l=f"cm{ss}")
            T(lambda e, ss=ss: e.transpose(self.Y2[0:8, 0:128], self.comb_t[:, ss, :],
                                           self.iden_f[:]),
              r=[B("cmb"), B("iden"), B("pY0")], w=[B("pY0")], l=f"ct{ss}")
            V(lambda e: e.tensor_copy(self.comb_st[:, :], self.Y2[0:8, 0:128]),
              r=[B("pY0"), B("cst")], w=[B("cst")], l=f"ec{ss}")
            DS(lambda e, ss=ss: e.dma_start(self.comb_dram[:, ss * 128:(ss + 1) * 128],
                                            self.comb_st[:, :]),
               r=[B("cst")], w=[B("cfd")], l=f"wcf{ss}")

        def emit_expert_blk(ei, eidx, pp, blk):
            if eidx is not None:
                DS(lambda e, eidx=eidx, blk=blk: e.dma_start(
                    self.comb_row[:, :],
                    self.comb_dram[eidx, blk * 512:(blk + 1) * 512][None, :]),
                   r=[B("cfd"), B("crow")], w=[B("crow")], l=f"lcf{ei}_{blk}")
                T(lambda e: e.matmul(
                    self.Y2[0:128, 0:512], self.ones_f[0:1, :],
                    self.comb_row[:, :], start=True, stop=True),
                  r=[B("crow"), B("onesf"), B("pY0")], w=[B("pY0")], l=f"be{ei}_{blk}")
                V(lambda e: e.tensor_copy(self.Be[:, :], self.Y2[0:128, 0:512]),
                  r=[B("pY0"), B("Be")], w=[B("Be")], l=f"eb{ei}_{blk}")
            for ic in range(4):
                gso = 1024 * (ic % 2)
                gb = f"pS{2 * (ic % 2)}"
                ub = f"pS{2 * (ic % 2) + 1}"
                for c in range(NCH):
                    T(lambda e, ic=ic, c=c, pp=pp, blk=blk, gso=gso: e.matmul(
                        self.S4[0:128, gso:gso + 512],
                        self.ap(self.Wg_s, [128, 2, NCH, II], 0, 128,
                                (pp * NCH + c) * II + ic * 128, [[1, 128]]),
                        self.ap(self.xn2f, [128, self.NOS, NCH, 128], 0, 128,
                                blk * 4 * NCH * 128 + c * 128, [[NCH * 128, 4], [1, 128]]),
                        start=(c == 0), stop=(c == NCH - 1)),
                      r=[B(f"wg{pp}"), B("xn2f")] + ([B(gb)] if c == 0 else []),
                      w=[B(gb)], l=f"hg{ei}_{blk}_{ic}_{c}")
                S_(lambda e, ic=ic, gso=gso, gb=gb: e.activation(
                    self.sg[:, ic, :], self.S4[0:128, gso:gso + 512], AF.Silu),
                   r=[B(gb), B("sg")], w=[B("sg")], l=f"si{ei}_{blk}_{ic}")
                for c in range(NCH):
                    T(lambda e, ic=ic, c=c, pp=pp, blk=blk, gso=gso: e.matmul(
                        self.S4[0:128, gso + 512:gso + 1024],
                        self.ap(self.Wu_s, [128, 2, NCH, II], 0, 128,
                                (pp * NCH + c) * II + ic * 128, [[1, 128]]),
                        self.ap(self.xn2f, [128, self.NOS, NCH, 128], 0, 128,
                                blk * 4 * NCH * 128 + c * 128, [[NCH * 128, 4], [1, 128]]),
                        start=(c == 0), stop=(c == NCH - 1)),
                      r=[B(f"wu{pp}"), B("xn2f")] + ([B(ub)] if c == 0 else []),
                      w=[B(ub)], l=f"hu{ei}_{blk}_{ic}_{c}")
                if eidx is None:
                    V(lambda e, ic=ic, gso=gso, ub=ub: e.tensor_copy(
                        self.hu[:, ic, :], self.S4[0:128, gso + 512:gso + 1024]),
                      r=[B(ub), B("hu")], w=[B("hu")], l=f"eu{ei}_{blk}_{ic}")
                else:
                    V(lambda e, ic=ic, gso=gso, ub=ub: e.tensor_tensor(
                        self.hu[:, ic, :], self.S4[0:128, gso + 512:gso + 1024],
                        self.Be[:, :], ALU.mult),
                      r=[B(ub), B("Be"), B("hu")], w=[B("hu")], l=f"eu{ei}_{blk}_{ic}")
                V(lambda e, ic=ic: e.tensor_tensor(self.gu[:, ic, :], self.sg[:, ic, :],
                                                   self.hu[:, ic, :], ALU.mult),
                  r=[B("sg"), B("hu"), B("gu")], w=[B("gu")], l=f"gu{ei}_{blk}_{ic}")
            for s4 in range(4):
                sub = blk * 4 + s4
                dps = self.PAB if s4 % 2 == 0 else self.Y2
                db0 = "pA0" if s4 % 2 == 0 else "pY0"
                db1 = "pA1" if s4 % 2 == 0 else "pY1"
                for ic in range(4):
                    for half in range(2):
                        T(lambda e, ic=ic, s4=s4, half=half, pp=pp, dps=dps: e.matmul(
                            dps[0:128, half * 512:(half + 1) * 512],
                            self.gu[:, ic, s4 * 128:(s4 + 1) * 128],
                            self.ap(self.Wd_s, [128, 2, 4, D], 0, 128,
                                    (pp * 4 + ic) * D + half * 512, [[1, 512]]),
                            start=(ic == 0), stop=(ic == 3)),
                          r=[B("gu"), B(f"wd{pp}")] + ([B(db0 if half == 0 else db1)] if ic == 0 else []),
                          w=[B(db0 if half == 0 else db1)], l=f"wd{ei}_{blk}_{s4}_{ic}_{half}")
                V(lambda e, sub=sub, dps=dps: e.tensor_tensor(self.acc[:, sub, :],
                                                              dps[0:128, 0:1024],
                                                              self.acc[:, sub, :], ALU.add),
                  r=[B(db0), B(db1), B("acc")], w=[B("acc")], l=f"ac{ei}_{blk}_{s4}")

        def emit_wload(ei, pp, wg_src, wu_src, wd_src, sl):
            DG(lambda e, wg_src=wg_src, sl=sl, pp=pp: e.dma_start(
                self.Wg_s[:, pp, :, :], wg_src[sl].rearrange("(c p) i -> p c i", p=128)),
               r=[B(f"wg{pp}")], w=[B(f"wg{pp}")], l=f"lg{ei}")
            DG(lambda e, wu_src=wu_src, sl=sl, pp=pp: e.dma_start(
                self.Wu_s[:, pp, :, :], wu_src[sl].rearrange("(c p) i -> p c i", p=128)),
               r=[B(f"wu{pp}")], w=[B(f"wu{pp}")], l=f"lu{ei}")
            DG(lambda e, wd_src=wd_src, sl=sl, pp=pp: e.dma_start(
                self.Wd_s[:, pp, :, :], wd_src[sl].rearrange("(c p) d -> p c d", p=128)),
               r=[B(f"wd{pp}")], w=[B(f"wd{pp}")], l=f"ld{ei}")

        NBLK = self.OWN_T // QB
        emit_wload(0, 0, i["sWg"], i["sWu"], i["sWd"], 0)
        for blk in range(NBLK):
            DS(lambda e, blk=blk: e.dma_start(
                self.acc[:, blk * 4:(blk + 1) * 4, :],
                self.g_sp[blk * 512:(blk + 1) * 512].rearrange("(k p) d -> p k d", p=128)),
               r=[B("gsp"), B("acc")], w=[B("acc")], l=f"ldG{blk}")
            for s4 in range(4):
                ss = blk * 4 + s4
                emit_rebuild(ss)
                emit_gate(ss)
            emit_expert_blk(0, None, 0, blk)
        for ex in range(EE):
            ei = ex + 1
            pp = ei % 2
            emit_wload(ei, pp, i["Wg"], i["Wu"], i["Wd"], ex)
            for blk in range(NBLK):
                emit_expert_blk(ei, ex, pp, blk)

    def phase_final(self):
        B, T, V, S_, DS = self.B, self.T, self.V, self.S_, self.DS
        st = self.stc
        for sub in range(self.NOS):
            V(lambda e, sub=sub: e.reduce_sum(st[:, 8:9], self.acc[:, sub, :], axis=AX.X),
              r=[B("acc"), B("stc")], w=[B("stc")], l=f"fs{sub}")
            V(lambda e, sub=sub: e.scalar_tensor_tensor(
                self.sqscr[:, :], self.acc[:, sub, :], 1.0, self.acc[:, sub, :],
                ALU.mult, ALU.mult, accum_out=st[:, 9:10]),
              r=[B("acc"), B("sqscr"), B("stc")], w=[B("sqscr"), B("stc")], l=f"f2{sub}")
            V(lambda e: e.tensor_scalar(st[:, 10:11], st[:, 8:9], 1.0 / D, None, ALU.mult),
              r=[B("stc")], w=[B("stc")], l=f"fm{sub}")
            V(lambda e: e.tensor_tensor(st[:, 11:12], st[:, 10:11], st[:, 10:11], ALU.mult),
              r=[B("stc")], w=[B("stc")], l=f"f3{sub}")
            V(lambda e: e.scalar_tensor_tensor(st[:, 11:12], st[:, 9:10], 1.0 / D,
                                               st[:, 11:12], ALU.mult, ALU.subtract),
              r=[B("stc")], w=[B("stc")], l=f"fv{sub}")
            S_(lambda e: e.activation(st[:, 12:13], st[:, 11:12], AF.Ln, bias=EPS),
               r=[B("stc"), B("econst")], w=[B("stc")], l=f"fl{sub}")
            S_(lambda e: e.activation(st[:, 12:13], st[:, 12:13], AF.Exp, scale=-0.5),
               r=[B("stc")], w=[B("stc")], l=f"fr{sub}")
            V(lambda e: e.scalar_tensor_tensor(st[:, 15:16], st[:, 10:11], -1.0,
                                               st[:, 12:13], ALU.mult, ALU.mult),
              r=[B("stc")], w=[B("stc")], l=f"fn{sub}")
            S_(lambda e, sub=sub: e.activation(self.xto[:, :], self.acc[:, sub, :], AF.Identity,
                                               bias=st[:, 15:16], scale=st[:, 12:13]),
               r=[B("acc"), B("stc"), B("xto")], w=[B("xto")], l=f"fo{sub}")
            DS(lambda e, sub=sub: e.dma_start(self.out[sub * 128:(sub + 1) * 128, :],
                                              self.xto[:, :]),
               r=[B("xto")], w=[B("outd")], l=f"o{sub}")
        self.p.barrier(list(self._bufs.values()))

    def build(self, stop_after=None):
        self.declare_io()
        self.alloc_persist()
        self.nc.const_aps.register(F32, EPS, self.econst[:, 0:1])
        self.alloc_outer()
        self.xn1f_oa = self.sb("xn1f_oa", [128, 4, NCH, 128], BF16)
        self.phase_prep()
        if stop_after != "prep":
            self.phase_outer()
            if self.debug:
                for nm, bufn, tile, shape, dt in (
                        ("qA", "qA", self.qA, [128, self.S], BF16),
                        ("kA", "kA", self.kA, [128, self.S], BF16),
                        ("vt", "vt", self.vt, [128, self.NKT, 129], BF16),
                        ("krass", "krass", self.krass, [128, self.S], BF16),
                        ("kvpack", "kvp", self.kvpack, [128, 4, self.S], BF16),
                        ("kvt", "kvt", self.kvt, [128, self.NKT, HEADS * 33], BF16)):
                    t = self.nc.dram_tensor("dbg_" + nm, shape, dt,
                                            kind="ExternalOutput").ap()
                    self.p.add("sync", lambda e, t=t, tile=tile: e.dma_start(t[:], tile[:]),
                               reads=(self.B(bufn),), writes=(self.B("dbg"),),
                               label=f"dump_{nm}", dma=True)
        if stop_after in (None, "moe") or (stop_after or "").startswith("mla"):
            self.free_arena(set(self.OUTER_NAMES + ["xn1f_oa"]))
            self.alloc_mla()
            ms = stop_after[4:] if (stop_after or "").startswith("mla_") else None
            self.phase_mla(mstop=ms)
        if stop_after in (None, "moe"):
            self.free_arena(set(self.MLA_NAMES))
            self.alloc_moe()
            self.phase_moe()
        if stop_after is None:
            self.phase_final()
        else:
            self.p.barrier(list(self._bufs.values()))
            self.p.add("vector", lambda e: e.memset(self.xto[:, :], 0.0),
                       reads=(), writes=(self.B("xto"),), label="dumz")
            for sub in range(self.NOS):
                self.p.add("sync", lambda e, sub=sub: e.dma_start(
                    self.out[sub * 128:(sub + 1) * 128, :], self.xto[:, :]),
                    reads=(self.B("xto"),), writes=(self.B("outd"),),
                    label=f"do{sub}", dma=True)
            self.p.barrier([self.B("outd")])
        self.p.emit_all(self.nc)
        for name, cm in reversed(self._arena):
            cm.__exit__(None, None, None)
        return self.nc


def _dims_n(dims):
    n = 1
    for d in dims:
        n *= d[1]
    return n



def host_tables(S, half):
    NQB = S // QB
    NSLOT = NQB // 2
    Ls = [8 * (s + 1) for s in range(NSLOT)]
    A, Bm = outer_tables(S)
    own_qbs = [2 * s + half for s in range(NSLOT)]
    pos = np.concatenate([np.arange(qb * QB, (qb + 1) * QB) for qb in own_qbs])
    Aq, Bq = mla_tables(pos, 4)
    Ak, Bk = mla_tables(np.arange(S), 1)
    orow = np.zeros((1, NQB), np.float32)
    for qb in own_qbs:
        orow[0, qb] = 1.0
    padrow = np.zeros((1, sum(Ls)), np.float32)
    off = 0
    for s in range(NSLOT):
        lim = 4 * (2 * s + half) + 4
        for j in range(Ls[s]):
            if j >= lim:
                padrow[0, off + j] = NEG
        off += Ls[s]
    masks = np.ones((128, 8, QB), np.float32)
    f = np.arange(QB)[None, :]
    pp = np.arange(128)[:, None]
    for j in range(8):
        if half == 0 and j < 4:
            masks[:, j, :] = (f - pp - 128 * j >= 0)
        if half == 1 and j >= 4:
            masks[:, j, :] = (f - pp - 128 * (j - 4) >= 0)
    return {
        "Aout": A, "Bout": Bm, "Aq": Aq, "Bq": Bq, "Ak": Ak, "Bk": Bk,
        "iden": np.eye(128, dtype=np.float32), "orow": orow, "padrow": padrow,
        "masks": masks.astype(BF),
    }


_nc_cache = {}


def kernel(L0_attn_x, L0_attn_wq, L0_attn_wk, L0_attn_wv, L0_attn_wo,
           L0_ffn_w_ln1, L0_ffn_Wq, L0_ffn_Wkva, L0_ffn_Wkvb, L0_ffn_Wo,
           L0_ffn_w_ln2, L0_ffn_Wgate, L0_ffn_expert_bias, L0_ffn_Wg,
           L0_ffn_Wu, L0_ffn_Wd, L0_ffn_sWg, L0_ffn_sWu, L0_ffn_sWd):
    x = np.ascontiguousarray(np.asarray(L0_attn_x, dtype=np.float32))
    nb, S, _ = x.shape
    weights = {
        "wq": L0_attn_wq, "wk": L0_attn_wk, "wv": L0_attn_wv, "wo": L0_attn_wo,
        "w_ln1": L0_ffn_w_ln1, "Wq": L0_ffn_Wq, "Wkva": L0_ffn_Wkva,
        "Wkvb": L0_ffn_Wkvb, "Wo": L0_ffn_Wo, "w_ln2": L0_ffn_w_ln2,
        "Wgate": L0_ffn_Wgate, "ebias": L0_ffn_expert_bias, "Wg": L0_ffn_Wg,
        "Wu": L0_ffn_Wu, "Wd": L0_ffn_Wd, "sWg": L0_ffn_sWg, "sWu": L0_ffn_sWu,
        "sWd": L0_ffn_sWd,
    }
    weights = {k: np.ascontiguousarray(np.asarray(v, dtype=np.float32))
               for k, v in weights.items()}
    if S not in _nc_cache:
        _nc_cache[S] = Builder(S).build()
    nc = _nc_cache[S]
    tabs = [host_tables(S, h) for h in range(2)]
    n_cores = 2 * nb
    in_maps = []
    for b in range(nb):
        for h in range(2):
            m = dict(weights)
            m["x"] = x[b]
            m.update(tabs[h])
            in_maps.append(m)
    res = run_bass_kernel_spmd(nc, in_maps, core_ids=list(range(n_cores)))
    out = np.empty((nb, S, D), np.float32)
    NSLOT = (S // QB) // 2
    for b in range(nb):
        for h in range(2):
            o = res.results[2 * b + h]["out"]
            for s in range(NSLOT):
                qb = 2 * s + h
                out[b, qb * QB:(qb + 1) * QB, :] = o[s * QB:(s + 1) * QB, :]
    return out


# version 14
# speedup vs baseline: 1.2352x; 1.0173x over previous
import math
import numpy as np
import ml_dtypes
import concourse.bass as bass
import concourse.mybir as mybir
from concourse.bass_utils import run_bass_kernel_spmd

F32 = mybir.dt.float32
BF16 = mybir.dt.bfloat16
AF = mybir.ActivationFunctionType
ALU = mybir.AluOpType
AX = mybir.AxisListType
BF = ml_dtypes.bfloat16

D = 1024
NCH = 8
HEADS = 16
RLAT = 128
KR = 32
EE = 8
II = 512
OHD = 128
QB = 512
KT = 128
EPS = 1e-5
NEG = -1e30

ENGS = ("tensor", "vector", "scalar", "gpsimd", "sync")


class _Item:
    __slots__ = ("idx", "eng", "emit", "deps", "target", "label", "amt",
                 "semname", "use")

    def __init__(self, idx, eng, emit, label, amt):
        self.idx, self.eng, self.emit, self.label, self.amt = idx, eng, emit, label, amt
        self.deps = []
        self.target = None


class Buf:
    __slots__ = ("name", "last_write", "readers")

    def __init__(self, name):
        self.name = name
        self.last_write = None
        self.readers = []


class Prog:
    def __init__(self):
        self.items = []
        self.fence_items = None
        self.fenced_engines = set()

    EXCL = frozenset(("pA0", "pA1", "pS0", "pS1", "pS2", "pS3", "pY0", "pY1"))

    def add(self, eng, emit, reads=(), writes=(), label="", dma=False):
        it = _Item(len(self.items), eng, emit, label, 16 if dma else 1)
        deps = []
        if self.fence_items is not None and eng not in self.fenced_engines:
            deps.extend(self.fence_items)
            self.fenced_engines.add(eng)
        excl = [b for b in reads if b.name in self.EXCL]
        reads = [b for b in reads if b.name not in self.EXCL]
        writes = tuple(writes) + tuple(excl)
        for b in reads:
            if b.last_write is not None:
                deps.append(b.last_write)
            b.readers.append(it)
        for b in writes:
            if b.last_write is not None:
                deps.append(b.last_write)
            deps.extend(b.readers)
            b.readers = []
            b.last_write = it
        seen = set()
        for d in deps:
            if d is it or d.idx in seen:
                continue
            seen.add(d.idx)
            if d.eng == eng:
                if d.amt == 1 and it.amt == 1 and eng == "tensor":
                    continue
                if d.amt == 1 and it.amt == 16:
                    continue
            it.deps.append(d)
        self.items.append(it)
        return it

    def barrier(self, bufs):
        fence = []
        fb = Buf("_fence")
        for e in ENGS:
            fence.append(self.add(e, lambda h: h.nop(), reads=(), writes=tuple(bufs) + (fb,),
                                  label=f"fence_{e}", dma=False))
        self.fence_items = fence
        self.fenced_engines = set()

    NPOOL = 10

    def emit_all(self, nc):
        dma_cnt = {}
        for it in self.items:
            if it.amt == 16:
                q = it.eng
                idx = dma_cnt.get(q, 0)
                dma_cnt[q] = idx + 1
                it.semname = f"dpool_{q}_{idx % self.NPOOL}"
                it.use = idx // self.NPOOL
                it.target = 16 * (it.use + 1)
            else:
                it.semname = f"sem_{it.eng}"
                it.use = None
        needed = set()
        for it in self.items:
            for d in it.deps:
                needed.add(d.idx)
        running = {}
        for it in self.items:
            if it.amt == 1:
                if it.idx in needed:
                    running[it.eng] = running.get(it.eng, 0) + 1
                    it.target = running[it.eng]
                else:
                    it.target = None
        engs = [e for e in ENGS if any(it.eng == e for it in self.items)]
        semnames = sorted(set(it.semname for it in self.items))
        sem_cms = {n: nc.semaphore(n) for n in semnames}
        sems = {n: cm.__enter__() for n, cm in sem_cms.items()}
        blk_cm = nc.Block()
        block = blk_cm.__enter__()
        by_eng = {e: [it for it in self.items if it.eng == e] for e in engs}

        def mk(eng_items):
            def body(h):
                seen = {}
                for it in eng_items:
                    wmax = {}
                    for d in it.deps:
                        assert d.idx < it.idx and d.target is not None, (d.label, it.label)
                        wmax[d.semname] = max(wmax.get(d.semname, 0), d.target)
                    if it.amt == 16 and it.use > 0:
                        v = 16 * it.use
                        if wmax.get(it.semname, 0) < v:
                            wmax[it.semname] = v
                    for sn in sorted(wmax):
                        if seen.get(sn, -1) >= wmax[sn]:
                            continue
                        h.wait_ge(sems[sn], wmax[sn])
                        seen[sn] = wmax[sn]
                    ins = it.emit(h)
                    if it.target is not None:
                        ins.then_inc(sems[it.semname], it.amt)
            return body

        dec = {"tensor": block.tensor, "vector": block.vector, "scalar": block.scalar,
               "gpsimd": block.gpsimd, "sync": block.sync}
        for e in engs:
            dec[e](mk(by_eng[e]))
        blk_cm.__exit__(None, None, None)
        for n in semnames:
            sem_cms[n].__exit__(None, None, None)


def fe(shape):
    n = 1
    for s in shape[1:]:
        n *= s
    return n


def outer_tables(S):
    j = np.arange(64)
    invf = (10000.0 ** (-(2.0 * j) / 128.0))[:, None]
    ang = (np.arange(S)[None, :] - 1) * invf
    c = np.cos(ang)
    s = np.sin(ang)
    c[:, 0] = 1.0
    s[:, 0] = 0.0
    return (np.concatenate([c, c], 0).astype(np.float32),
            np.concatenate([-s, s], 0).astype(np.float32))


def mla_tables(positions, nblk):
    j = np.arange(16)
    freqs = (100000.0 ** (-(2.0 * j) / 32.0))[:, None]
    ang = positions[None, :].astype(np.float64) * freqs
    c = np.cos(ang)
    s = np.sin(ang)
    blkA = np.concatenate([c, c], 0)
    blkB = np.concatenate([-s, s], 0)
    A = np.concatenate([blkA] * nblk, 0).astype(np.float32)
    Bm = np.concatenate([blkB] * nblk, 0).astype(np.float32)
    return A, Bm


class Builder:
    def __init__(self, S, debug=False):
        self.S = S
        self.debug = debug
        self.NQB = S // QB
        self.NKT = S // KT
        self.NSLOT = self.NQB // 2
        self.OWN_T = self.NSLOT * QB
        self.NOS = self.OWN_T // 128
        self.Ls = [8 * (s + 1) for s in range(self.NSLOT)]
        self.nc = bass.Bass("TRN2", target_bir_lowering=False)
        self.p = Prog()
        self._bufs = {}
        self._arena = []

    def T(self, emit, r=(), w=(), l=""):
        return self.p.add("tensor", emit, r, w, l)

    def V(self, emit, r=(), w=(), l=""):
        return self.p.add("vector", emit, r, w, l)

    def S_(self, emit, r=(), w=(), l=""):
        return self.p.add("scalar", emit, r, w, l)

    def G(self, emit, r=(), w=(), l=""):
        return self.p.add("gpsimd", emit, r, w, l)

    def DS(self, emit, r=(), w=(), l=""):
        return self.p.add("sync", emit, r, w, l, dma=True)

    def DG(self, emit, r=(), w=(), l=""):
        return self.p.add("gpsimd", emit, r, w, l, dma=True)

    def B(self, name):
        if name not in self._bufs:
            self._bufs[name] = Buf(name)
        return self._bufs[name]

    def sb(self, name, shape, dtype):
        cm = self.nc.sbuf_tensor(name, shape, dtype)
        t = cm.__enter__()
        self._arena.append((name, cm))
        return t

    def free_arena(self, names, bufnames=None):
        self.p.barrier(list(self._bufs.values()))
        keep = []
        for name, cm in reversed(self._arena):
            if name in names:
                cm.__exit__(None, None, None)
            else:
                keep.append((name, cm))
        self._arena = list(reversed(keep))

    def ap(self, tile, shape, part0, nparts, free_off, dims):
        return bass.AP(tile, part0 * fe(shape) + free_off, [[fe(shape), nparts]] + list(dims))

    def declare_io(self):
        nc = self.nc
        S = self.S
        d = {}
        def di(name, shape, dtype=F32):
            d[name] = nc.dram_tensor(name, shape, dtype, kind="ExternalInput").ap()
        di("x", [S, D]); di("wq", [OHD, D]); di("wk", [OHD, D]); di("wv", [OHD, D])
        di("wo", [D, OHD]); di("w_ln1", [D]); di("Wq", [D, D]); di("Wkva", [D, RLAT + KR])
        di("Wkvb", [RLAT, HEADS * 32]); di("Wo", [HEADS * 64, D]); di("w_ln2", [D])
        di("Wgate", [D, EE]); di("ebias", [EE]); di("Wg", [EE, D, II]); di("Wu", [EE, D, II])
        di("Wd", [EE, II, D]); di("sWg", [1, D, II]); di("sWu", [1, D, II]); di("sWd", [1, II, D])
        di("Aout", [128, S]); di("Bout", [128, S])
        di("Aq", [128, self.OWN_T]); di("Bq", [128, self.OWN_T])
        di("Ak", [32, S]); di("Bk", [32, S]); di("iden", [128, 128])
        di("orow", [1, self.NQB]); di("padrow", [1, sum(self.Ls)])
        di("masks", [128, 8, QB], BF16)
        self.inp = d
        self.out = nc.dram_tensor("out", [self.OWN_T, D], F32, kind="ExternalOutput").ap()
        k = "ExternalOutput" if self.debug else "Internal"
        self.x1_sp = nc.dram_tensor("x1_sp", [S, D], F32, kind=k).ap()
        self.xn1_sp = nc.dram_tensor("xn1_sp", [S, D], BF16, kind=k).ap()
        self.g_sp = nc.dram_tensor("g_sp", [self.OWN_T, D], F32, kind=k).ap()
        self.comb_dram = nc.dram_tensor("comb_dram", [EE, self.OWN_T], F32, kind=k).ap()
        self.xn2_sp = nc.dram_tensor("xn2_sp", [self.OWN_T, D], BF16, kind=k).ap()

    def alloc_persist(self):
        sb = self.sb
        S = self.S
        self.biasKsl = sb("biasKsl", [128, sum(self.Ls)], F32)
        self.wln1_c = sb("wln1_c", [128, NCH], F32)
        self.wln2_c = sb("wln2_c", [128, NCH], F32)
        self.iden_f = sb("iden_f", [128, 128], F32)
        self.ones_b = sb("ones_b", [128, 2], BF16)
        self.ones_f = sb("ones_f", [128, 128], F32)
        self.negh = sb("negh", [128, 1], BF16)
        self.econst = sb("econst", [128, 1], F32)
        self.ocol = sb("ocol", [128, self.NQB], F32)
        self.ebias_b = sb("ebias_b", [128, EE], F32)
        self.Wgate_s = sb("Wgate_s", [128, NCH, EE], BF16)
        self.vscr = sb("vscr", [128, D], F32)
        self.sqscr = sb("sqscr", [128, D], BF16)
        self.stc = sb("stc", [128, 24], F32)
        self.xto = sb("xto", [128, D], F32)
        self.xnb = sb("xnb", [128, 2, D], BF16)
        self.sqX = sb("sqX", [128, 2, 512], BF16)
        self.nkrow = sb("nkrow", [1, 512], F32)
        self.mxk = sb("mxk", [1, 8], F32)
        self.t1s = sb("t1s", [128, QB], F32)
        self.t2s = sb("t2s", [128, QB], F32)
        self.kvpack = sb("kvpack", [128, 4, S], BF16)
        self.kvt = sb("kvt", [128, self.NKT, HEADS * 33], BF16)
        self.krass = sb("krass", [128, S], BF16)
        self.PAB = self.nc.psum_tensor("PAB", [128, 1024], F32).__enter__()
        self.S4 = self.nc.psum_tensor("S4", [128, 2048], F32).__enter__()
        self.Y2 = self.nc.psum_tensor("Y2", [128, 1024], F32).__enter__()

    def alloc_outer(self):
        sb = self.sb
        self.vdum = sb("vdum", [128, D], BF16)
        S = self.S
        self.WkvaA = sb("WkvaA", [128, NCH, 192], BF16)
        self.WkvbT = sb("WkvbT", [128, 4, 128], BF16)
        self.Wkvb_n = sb("Wkvb_n", [128, HEADS * 32], BF16)
        self.wqX = sb("wqX", [128, NCH, 128], BF16)
        self.wqXs = sb("wqXs", [128, NCH, 128], BF16)
        self.wkX = sb("wkX", [128, NCH, 128], BF16)
        self.wkXs = sb("wkXs", [128, NCH, 128], BF16)
        self.wvT = sb("wvT", [128, NCH, 128], BF16)
        self.woT = sb("woT", [128, NCH, 128], BF16)
        self.qA = sb("qA", [128, S], BF16)
        self.kA = sb("kA", [128, S], BF16)
        self.vscr2 = sb("vscr2", [128, D], F32)
        self.biasG = sb("biasG", [128, 1], F32)
        self.vt = sb("vt", [128, self.NKT, 129], BF16)
        self.x0f = sb("x0f", [128, 2, NCH, 512], BF16)
        self.x0tb = sb("x0tb", [128, 2, D], BF16)
        self.x0tf = sb("x0tf", [128, 2, D], F32)
        self.AB_blk = sb("AB_blk", [128, 2, 512], F32)
        self.ABk_blk = sb("ABk_blk", [32, 2, 512], F32)
        self.wld = sb("wld", [128, D], F32)
        self.wld2 = sb("wld2", [128, 128], F32)
        self.wrow = sb("wrow", [8, 128], F32)
        self.lat = sb("lat", [128, 512], BF16)
        self.expO = sb("expO", [128, 4, 512], BF16)

    OUTER_NAMES = ["vdum", "WkvaA", "WkvbT", "Wkvb_n",
                   "wqX", "wqXs", "wkX", "wkXs", "wvT", "woT", "qA", "kA",
                   "vscr2", "biasG", "vt", "x0f", "x0tb", "x0tf", "AB_blk",
                   "ABk_blk", "wld", "wld2", "wrow", "lat", "expO"]
    OUTER_BUFS = ["wq0", "wq1", "wk0", "wk1", "wvT", "woT", "qA", "qB", "kA", "kB",
                  "vt", "biasKo", "x0f0", "x0f1", "x0tb0", "x0tb1", "x0tf0", "x0tf1",
                  "ABb", "ABk", "wld", "wld2", "wrow", "lat",
                  "psXq", "psXqs", "psXk", "psXks", "psv", "psQn", "psKc", "psKc2",
                  "psS0", "psS512", "psY1", "psY2", "psA", "exp0", "exp1",
                  "t1", "t2", "sqX", "nkr", "vscr", "sqscr", "stc", "xto", "xsc",
                  "xnb0", "xnb1", "xnsc", "x1sp", "xn1sp", "krass", "kvp", "kvt",
                  "biasKm", "ocol"]

    def alloc_mla(self):
        sb = self.sb
        self.WqA = sb("WqA", [128, NCH, HEADS, 96], BF16)
        self.WoE = sb("WoE", [128, 4, D], BF16)
        self.Qnope = sb("Qnope", [128, 4, QB], BF16)
        self.Qra = sb("Qra", [128, 8, QB], BF16)
        self.expT = sb("expT", [128, 2, 4, QB], BF16)
        self.yasm = sb("yasm", [128, 4, QB], BF16)
        self.rDr = sb("rDr", [1, 4, QB], F32)
        self.sel32 = sb("sel32", [128, 128], F32)
        self.rDc = sb("rDc", [128, QB], F32)
        self.Bbc4 = sb("Bbc4", [128, QB], F32)
        self.xn1f = sb("xn1f", [128, 2, 4, NCH, 128], BF16)
        self.x1s2 = sb("x1s2", [128, D], F32)
        self.x1su = sb("x1su", [128, D], F32)
        self.xns2 = sb("xns2", [128, D], BF16)
        self.xnsu = sb("xnsu", [128, 2, D], BF16)
        self.masks_s = sb("masks_s", [128, 8, QB], BF16)
        self.ABq_sl = sb("ABq_sl", [128, 2, QB], F32)
        self.wldM = self.x1su

    MLA_NAMES = ["WqA", "WoE", "Qnope", "Qra", "expT", "yasm", "rDr", "sel32",
                 "rDc", "Bbc4",
                 "xn1f", "x1s2", "x1su", "xns2", "xnsu", "masks_s", "ABq_sl",
                 "kvpack", "kvt", "krass"]
    MLA_BUFS = ["WqA", "WoE", "Qnope", "Qra", "ex0", "ex1", "ex2", "ex3", "yasm",
                "rD", "Bbc", "xn1f", "x1qb", "x1qb2", "xnqb", "xnqb2", "masks",
                "ABq", "wldM", "psQ0", "psQ1", "psS0", "psS512", "psS1024",
                "psS1536", "psYm0", "psYm1", "psA", "psKc", "gsp", "kvp", "kvt",
                "krass", "biasKsl", "t1", "t2", "sqX", "vscr", "sqscr", "stc",
                "xto", "xnb0", "xnb1", "xn2f"]

    def alloc_moe(self):
        sb = self.sb
        self.acc = sb("acc", [128, self.NOS, D], F32)
        self.Wg_s = sb("Wg_s", [128, 2, NCH, II], BF16)
        self.Wu_s = sb("Wu_s", [128, 2, NCH, II], BF16)
        self.Wd_s = sb("Wd_s", [128, 2, 4, D], BF16)
        self.comb_t = sb("comb_t", [128, self.NOS, 8], F32)
        self.comb_st = sb("comb_st", [8, 128], F32)
        self.comb_row = sb("comb_row", [1, 512], F32)
        self.gpr = sb("gpr", [128, 4, 8], F32)
        self.gcol = sb("gcol", [128, 8], F32)
        self.sg = sb("sg", [128, 4, 512], BF16)
        self.hu = sb("hu", [128, 4, 512], BF16)
        self.gu = sb("gu", [128, 4, 512], BF16)
        self.Be = sb("Be", [128, 512], F32)
        self.xn2f = sb("xn2f", [128, self.NOS, NCH, 128], BF16)
        self.xe = sb("xe", [128, 4, D], BF16)

    def phase_prep(self):
        B, T, V, S_, G, DS, DG = self.B, self.T, self.V, self.S_, self.G, self.DS, self.DG
        i = self.inp
        DS(lambda e: e.dma_start(self.iden_f[:], i["iden"][:]), w=[B("iden")])
        V(lambda e: e.memset(self.ones_b[:], 1.0), w=[B("ones")])
        V(lambda e: e.memset(self.ones_f[:], 1.0), w=[B("onesf")])
        V(lambda e: e.memset(self.negh[:], -0.5), w=[B("negh")])
        V(lambda e: e.memset(self.econst[:], EPS), w=[B("econst")])
        V(lambda e: e.memset(self.mxk[:], 0.0), w=[B("mxk")])
        V(lambda e: e.memset(
            self.ap(self.kvt, [128, self.NKT, HEADS * 33], 0, 128, 32,
                    [[33, HEADS * self.NKT]]), 1.0), w=[B("kvt")])
        V(lambda e: e.memset(
            self.ap(self.vt, [128, self.NKT, 129], 0, 128, 64,
                    [[129, self.NKT]]), 1.0), w=[B("vt")])
        V(lambda e: e.memset(self.krass[32:64, :], 0.0), w=[B("krass")])
        V(lambda e: e.memset(self.krass[32:33, :], 1.0), r=[B("krass")], w=[B("krass")])
        V(lambda e: e.memset(self.krass[96:128, :], 0.0), r=[B("krass")], w=[B("krass")])
        V(lambda e: e.memset(self.krass[96:97, :], 1.0), r=[B("krass")], w=[B("krass")])

        for nm, dst, bn in (("w_ln1", self.wln1_c, "w_ln1"), ("w_ln2", self.wln2_c, "w_ln2")):
            DS(lambda e, nm=nm: e.dma_start(self.wrow[:], i[nm].rearrange("(c p) -> c p", p=128)),
               r=[B("wrow")], w=[B("wrow")], l=f"ld{nm}")
            T(lambda e: e.transpose(self.PAB[0:128, 0:8], self.wrow[:], self.iden_f[0:8, 0:8]),
              r=[B("wrow"), B("iden"), B("pA0")], w=[B("pA0")], l=f"t{nm}")
            V(lambda e, dst=dst: e.tensor_copy(dst[:, 0:NCH], self.PAB[0:128, 0:8]),
              r=[B("pA0")], w=[B(bn)], l=f"ev{nm}")
        DS(lambda e: e.dma_start(self.wrow[0:1, 0:EE], i["ebias"][None, :]),
           r=[B("wrow")], w=[B("wrow")], l="ldeb")
        T(lambda e: e.matmul(self.PAB[0:128, 256:256 + EE], self.ones_f[0:1, :],
                             self.wrow[0:1, 0:EE], start=True, stop=True),
          r=[B("wrow"), B("onesf"), B("pA0")], w=[B("pA0")], l="mmEb")
        V(lambda e: e.tensor_copy(self.ebias_b[:], self.PAB[0:128, 256:256 + EE]),
          r=[B("pA0")], w=[B("ebias")], l="evEb")
        DS(lambda e: e.dma_start(self.wrow[0:1, 0:self.NQB], i["orow"][:]),
           r=[B("wrow")], w=[B("wrow")], l="ldor")
        T(lambda e: e.matmul(self.PAB[0:128, 256:256 + self.NQB], self.ones_f[0:1, :],
                             self.wrow[0:1, 0:self.NQB], start=True, stop=True),
          r=[B("wrow"), B("onesf"), B("pA0")], w=[B("pA0")], l="mmOr")
        V(lambda e: e.tensor_copy(self.ocol[:], self.PAB[0:128, 256:256 + self.NQB]),
          r=[B("pA0")], w=[B("ocol")], l="evOr")

        def outer_w(name, dstX, dstXs, tag):
            wview = i[name].rearrange("(j two) d -> j two d", two=2)
            for swap in (0, 1):
                dst = dstXs if swap else dstX
                DS(lambda e, swap=swap, wview=wview: e.dma_start(self.wld[0:64, :], wview[:, swap, :]),
                   r=[B("wld")], w=[B("wld")], l=f"{tag}a{swap}")
                DS(lambda e, swap=swap, wview=wview: e.dma_start(self.wld[64:128, :], wview[:, 1 - swap, :]),
                   r=[B("wld")], w=[B("wld")], l=f"{tag}b{swap}")
                for c in range(NCH):
                    T(lambda e, c=c: e.transpose(self.PAB[0:128, 0:128],
                                                 self.wld[:, c * 128:(c + 1) * 128], self.iden_f[:]),
                      r=[B("wld"), B("iden"), B("pA0")], w=[B("pA0")], l=f"t{tag}{swap}{c}")
                    V(lambda e, c=c, dst=dst, tag=tag, swap=swap: e.tensor_copy(
                        dst[:, c, :], self.PAB[0:128, 0:128]),
                      r=[B("pA0")], w=[B(f"{tag}{swap}")], l=f"e{tag}{swap}{c}")
        outer_w("wq", self.wqX, self.wqXs, "wq")
        outer_w("wk", self.wkX, self.wkXs, "wk")
        DS(lambda e: e.dma_start(self.wld[:, :], i["wv"][:]), r=[B("wld")], w=[B("wld")], l="ldwv")
        for c in range(NCH):
            T(lambda e, c=c: e.transpose(self.PAB[0:128, 0:128],
                                         self.wld[:, c * 128:(c + 1) * 128], self.iden_f[:]),
              r=[B("wld"), B("iden"), B("pA0")], w=[B("pA0")], l=f"twv{c}")
            V(lambda e, c=c: e.tensor_copy(self.wvT[:, c, :], self.PAB[0:128, 0:128]),
              r=[B("pA0")], w=[B("wvT")], l=f"evwv{c}")
        for c in range(NCH):
            DS(lambda e, c=c: e.dma_start(self.wld2[:, :], i["wo"][c * 128:(c + 1) * 128, :]),
               r=[B("wld2")], w=[B("wld2")], l=f"ldwo{c}")
            T(lambda e: e.transpose(self.PAB[0:128, 128:256], self.wld2[:, :], self.iden_f[:]),
              r=[B("wld2"), B("iden"), B("pA0")], w=[B("pA0")], l=f"two{c}")
            V(lambda e, c=c: e.tensor_copy(self.woT[:, c, :], self.PAB[0:128, 128:256]),
              r=[B("pA0")], w=[B("woT")], l=f"evwo{c}")
        for c in range(NCH):
            DS(lambda e, c=c: e.dma_start(self.wld[:, 0:RLAT + KR], i["Wkva"][c * 128:(c + 1) * 128, :]),
               r=[B("wld"), B("WkvaA")], w=[B("wld")], l=f"ldkva{c}")
            for (o_d, o_s, dims) in ((0, 0, [[1, 128]]), (128, 128, [[1, 2], [2, 16]]),
                                     (160, 128 + 33, [[2, 16]]), (176, 128 + 32, [[2, 16]])):
                V(lambda e, c=c, o_d=o_d, o_s=o_s, dims=dims: e.tensor_scalar(
                    self.WkvaA[:, c, o_d:o_d + _dims_n(dims)],
                    self.ap(self.wld, [128, D], 0, 128, o_s, dims),
                    self.wln1_c[:, c:c + 1], None, ALU.mult),
                  r=[B("wld"), B("w_ln1")], w=[B("WkvaA")], l=f"gkva{c}_{o_d}")
        DG(lambda e: e.dma_start(self.Wkvb_n[:], i["Wkvb"][:]), w=[B("Wkvb_n")], l="ldKvbN")
        DS(lambda e: e.dma_start(self.wld[:, 0:512], i["Wkvb"][:]),
           r=[B("wld"), B("WkvaA")], w=[B("wld")], l="ldKvbF")
        for c in range(4):
            T(lambda e, c=c: e.transpose(self.PAB[0:128, 0:128],
                                         self.wld[:, c * 128:(c + 1) * 128], self.iden_f[:]),
              r=[B("wld"), B("iden"), B("pA0")], w=[B("pA0")], l=f"tkvb{c}")
            V(lambda e, c=c: e.tensor_copy(self.WkvbT[:, c, :], self.PAB[0:128, 0:128]),
              r=[B("pA0")], w=[B("WkvbT")], l=f"ekvb{c}")
        DG(lambda e: e.dma_start(self.Wgate_s[:], i["Wgate"].rearrange("(c p) e -> p c e", p=128)),
           w=[B("Wgate")], l="ldWgate")

    def phase_outer(self):
        self.V(lambda e: e.memset(self.mxk[0:1, 4:5], 0.0),
               r=[self.B("mxk")], w=[self.B("mxk")], l="zmxm")
        B, T, V, S_, G, DS, DG = self.B, self.T, self.V, self.S_, self.G, self.DS, self.DG
        i = self.inp
        for blk in range(self.NQB):
            pp = blk % 2
            t0 = blk * QB
            DS(lambda e, t0=t0: e.dma_start(self.AB_blk[:, 0, :], i["Aout"][:, t0:t0 + QB]),
               r=[B("ABb")], w=[B("ABb")], l=f"ldA{blk}")
            DS(lambda e, t0=t0: e.dma_start(self.AB_blk[:, 1, :], i["Bout"][:, t0:t0 + QB]),
               r=[B("ABb")], w=[B("ABb")], l=f"ldB{blk}")
            for s_ in range(4):
                ts_ = t0 + s_ * 128
                DG(lambda e, ts_=ts_, pp=pp: e.dma_start(self.x0tb[:, pp, :], i["x"][ts_:ts_ + 128, :]),
                   r=[B(f"x0tb{pp}")], w=[B(f"x0tb{pp}")], l=f"ldx0{blk}_{s_}")
                DS(lambda e, pp=pp, s_=s_: e.dma_start_transpose(
                    self.ap(self.x0f, [128, 2, NCH, 512], 0, 128, pp * NCH * 512 + s_ * 128,
                            [[512, NCH], [1, 128]]),
                    self.x0tb[:, pp, :]),
                   r=[B(f"x0tb{pp}"), B(f"x0f{pp}")], w=[B(f"x0f{pp}")], l=f"tx0{blk}_{s_}")

            def proj(off, wgt, wbuf, tag):
                for c in range(NCH):
                    T(lambda e, c=c, off=off, wgt=wgt, pp=pp: e.matmul(
                        self.S4[0:128, off:off + 512], wgt[:, c, :],
                        self.ap(self.x0f, [128, 2, NCH, 512], 0, 128, pp * NCH * 512 + c * 512,
                                [[1, 512]]),
                        start=(c == 0), stop=(c == NCH - 1)),
                      r=[B(f"x0f{pp}"), B(wbuf)] + ([B(tag)] if c == 0 else []),
                      w=[B(tag)], l=f"{tag}{blk}_{c}")
            proj(0, self.wqX, "wq0", "pS0")
            proj(512, self.wqXs, "wq1", "pS1")
            proj(1024, self.wkX, "wk0", "pS2")
            proj(1536, self.wkXs, "wk1", "pS3")
            for s_ in range(4):
                for c in range(NCH):
                    T(lambda e, s_=s_, c=c, pp=pp: e.matmul(
                        self.Y2[0:128, s_ * 128:(s_ + 1) * 128],
                        self.ap(self.x0f, [128, 2, NCH, 512], 0, 128,
                                pp * NCH * 512 + c * 512 + s_ * 128, [[1, 128]]),
                        self.wvT[:, c, :], start=(c == 0), stop=(c == NCH - 1)),
                      r=[B(f"x0f{pp}"), B("wvT")] + ([B("pY0")] if c == 0 else []),
                      w=[B("pY0")], l=f"v{blk}_{s_}_{c}")
                V(lambda e, s_=s_, blk=blk: e.tensor_copy(
                    self.ap(self.vt, [128, self.NKT, 129], 0, 128, (blk * 4 + s_) * 129,
                            [[65, 2], [1, 64]]),
                    self.Y2[0:128, s_ * 128:(s_ + 1) * 128]),
                  r=[B("pY0"), B("vt")], w=[B("vt")], l=f"ev{blk}_{s_}")

            def rope(psA, psAs, dstA, dstB, bA, bAs, tag):
                V(lambda e, psA=psA: e.tensor_tensor(
                    self.t1s[:, :], self.S4[0:128, psA:psA + 512], self.AB_blk[:, 0, :],
                    ALU.mult),
                  r=[B(bA), B("ABb"), B("t1")], w=[B("t1")], l=f"r1{tag}{blk}")
                V(lambda e, psAs=psAs: e.tensor_tensor(
                    self.t2s[:, :], self.S4[0:128, psAs:psAs + 512], self.AB_blk[:, 1, :],
                    ALU.mult),
                  r=[B(bAs), B("ABb"), B("t2")], w=[B("t2")], l=f"r2{tag}{blk}")
                V(lambda e, dstA=dstA, t0=t0, tag=tag: e.tensor_tensor(
                    dstA[0:64, t0:t0 + 512], self.t1s[0:64, :], self.t2s[0:64, :], ALU.add),
                  r=[B("t1"), B("t2"), B(f"{tag}A")], w=[B(f"{tag}A")], l=f"r3{tag}{blk}")
                V(lambda e, dstA=dstA, t0=t0, tag=tag: e.tensor_tensor(
                    dstA[64:128, t0:t0 + 512], self.t1s[64:128, :], self.t2s[64:128, :], ALU.add),
                  r=[B("t1"), B("t2"), B(f"{tag}A")], w=[B(f"{tag}A")], l=f"r4{tag}{blk}")
            rope(0, 512, self.qA, None, "pS0", "pS1", "q")
            rope(1024, 1536, self.kA, None, "pS2", "pS3", "k")

            S_(lambda e: e.activation(self.sqX[:, 0, :], self.S4[0:128, 0:512], AF.Square),
               r=[B("pS0"), B("sqX")], w=[B("sqX")], l=f"sqQ{blk}")
            T(lambda e: e.matmul(self.PAB[0:1, 0:512], self.ones_b[:, 0:1], self.sqX[:, 0, :],
                                 start=True, stop=True),
              r=[B("sqX"), B("ones"), B("pA0")], w=[B("pA0")], l=f"mmQn{blk}")
            V(lambda e: e.reduce_max(self.mxk[0:1, 6:7], self.PAB[0:1, 0:512], axis=AX.X),
              r=[B("pA0"), B("mxk")], w=[B("mxk")], l=f"qmx{blk}")
            V(lambda e: e.tensor_tensor(self.mxk[0:1, 3:4], self.mxk[0:1, 3:4],
                                        self.mxk[0:1, 6:7], ALU.max),
              r=[B("mxk")], w=[B("mxk")], l=f"qmr{blk}")
            S_(lambda e: e.activation(self.sqX[:, 0, :], self.S4[0:128, 1024:1536], AF.Square),
               r=[B("pS2"), B("sqX")], w=[B("sqX")], l=f"sqK{blk}")
            T(lambda e: e.matmul(self.PAB[0:1, 0:512], self.ones_b[:, 0:1], self.sqX[:, 0, :],
                                 start=True, stop=True),
              r=[B("sqX"), B("ones"), B("pA0")], w=[B("pA0")], l=f"mmKn{blk}")
            V(lambda e: e.reduce_max(self.mxk[0:1, 1:2], self.PAB[0:1, 0:512], axis=AX.X),
              r=[B("pA0"), B("mxk")], w=[B("mxk")], l=f"kmx{blk}")
            V(lambda e: e.tensor_tensor(self.mxk[0:1, 0:1], self.mxk[0:1, 0:1],
                                        self.mxk[0:1, 1:2], ALU.max),
              r=[B("mxk")], w=[B("mxk")], l=f"kmr{blk}")

        V(lambda e: e.tensor_tensor(self.mxk[0:1, 2:3], self.mxk[0:1, 0:1],
                                    self.mxk[0:1, 3:4], ALU.add),
          r=[B("mxk")], w=[B("mxk")], l="gb1")
        V(lambda e: e.tensor_scalar(self.mxk[0:1, 2:3], self.mxk[0:1, 2:3],
                                    -0.5 / math.sqrt(OHD), None, ALU.mult),
          r=[B("mxk")], w=[B("mxk")], l="gb2")
        T(lambda e: e.matmul(self.PAB[0:128, 512:513], self.ones_f[0:1, :],
                             self.mxk[0:1, 2:3], start=True, stop=True),
          r=[B("mxk"), B("onesf"), B("pA1")], w=[B("pA1")], l="gb3")
        V(lambda e: e.tensor_copy(self.biasG[:, :], self.PAB[0:128, 512:513]),
          r=[B("pA1"), B("bG")], w=[B("bG")], l="gb4")

        for qb in range(self.NQB):
            t0 = qb * QB
            def oav(kp2):
                for kt in (2 * kp2, 2 * kp2 + 1):
                    eb = f"expO{kt % 4}"
                    T(lambda e, kt=kt: e.matmul(
                        self.Y2[0:65, 0:512],
                        self.ap(self.vt, [128, self.NKT, 129], 0, 128, kt * 129, [[1, 65]]),
                        self.expO[:, kt % 4, :], start=(kt == 0), stop=(kt == self.NKT - 1)),
                      r=[B(eb), B("vt")] + ([B("pY0")] if kt == 0 else []),
                      w=[B("pY0")], l=f"a1_{qb}_{kt}")
                    T(lambda e, kt=kt: e.matmul(
                        self.Y2[0:64, 512:1024],
                        self.ap(self.vt, [128, self.NKT, 129], 0, 128, kt * 129 + 65, [[1, 64]]),
                        self.expO[:, kt % 4, :], start=(kt == 0), stop=(kt == self.NKT - 1)),
                      r=[B(eb), B("vt")] + ([B("pY1")] if kt == 0 else []),
                      w=[B("pY1")], l=f"a2_{qb}_{kt}")

            for kp in range(self.NKT // 2):
                for kt in (2 * kp, 2 * kp + 1):
                    so = 512 * (kt % 4)
                    pbank = f"pS{kt % 4}"
                    T(lambda e, kt=kt, t0=t0, so=so: e.matmul(
                        self.S4[0:128, so:so + 512], self.kA[:, kt * KT:(kt + 1) * KT],
                        self.qA[:, t0:t0 + 512], start=True, stop=True),
                      r=[B("qA"), B("kA"), B(pbank)], w=[B(pbank)], l=f"s1_{qb}_{kt}")
                b0 = (2 * kp) % 4
                S_(lambda e, b0=b0: e.activation(
                    self.ap(self.expO, [128, 4, 512], 0, 128, b0 * 512, [[1, 1024]]),
                    self.S4[0:128, b0 * 512:(b0 + 2) * 512], AF.Exp,
                    bias=self.biasG[:, :], scale=1.0 / math.sqrt(OHD)),
                   r=[B(f"pS{b0}"), B(f"pS{b0 + 1}"), B("bG"),
                      B(f"expO{b0}"), B(f"expO{b0 + 1}")],
                   w=[B(f"expO{b0}"), B(f"expO{b0 + 1}")], l=f"e_{qb}_{kp}")
                if kp > 0:
                    oav(kp - 1)
            oav(self.NKT // 2 - 1)
            V(lambda e: e.tensor_copy(self.sqX[0:64, 0, :], self.Y2[0:64, 0:512]),
              r=[B("pY0"), B("sqX")], w=[B("sqX")], l=f"ey1_{qb}")
            V(lambda e: e.tensor_copy(self.sqX[64:128, 0, :], self.Y2[0:64, 512:1024]),
              r=[B("pY1"), B("sqX")], w=[B("sqX")], l=f"ey2_{qb}")
            V(lambda e: e.tensor_copy(self.nkrow[:], self.Y2[64:65, 0:512]),
              r=[B("pY0"), B("nkr")], w=[B("nkr")], l=f"rD_{qb}")
            for s_ in range(4):
                T(lambda e, s_=s_: e.matmul(
                    self.PAB[0:128, 512 + s_:513 + s_],
                    self.nkrow[0:1, s_ * 128:(s_ + 1) * 128], self.ones_f[0:1, 0:1],
                    start=True, stop=True),
                  r=[B("nkr"), B("onesf")] + ([B("pA1")] if s_ == 0 else []),
                  w=[B("pA1")], l=f"rdc_{qb}_{s_}")
            V(lambda e: e.reciprocal(self.stc[:, 0:4], self.PAB[0:128, 512:516]),
              r=[B("pA1"), B("stc")], w=[B("stc")], l=f"erdc_{qb}")

            for s_ in range(4):
                sub = qb * 4 + s_
                ts_ = t0 + s_ * 128
                xp = s_ % 2
                DS(lambda e, ts_=ts_, xp=xp: e.dma_start(self.x0tf[:, xp, :],
                                                          i["x"][ts_:ts_ + 128, :]),
                   r=[B(f"x0tf{xp}")], w=[B(f"x0tf{xp}")], l=f"ldxf{sub}")
                wps = self.PAB
                wb0 = "pA0"
                wb1 = "pA1"
                for half in range(2):
                    T(lambda e, s_=s_, half=half, wps=wps: e.matmul(
                        wps[0:128, half * 512:(half + 1) * 512],
                        self.sqX[:, 0, s_ * 128:(s_ + 1) * 128],
                        self.ap(self.woT, [128, NCH, 128], 0, 128, half * 512, [[1, 512]]),
                        start=True, stop=True),
                      r=[B("sqX"), B("woT"), B(wb0 if half == 0 else wb1)],
                      w=[B(wb0 if half == 0 else wb1)], l=f"wo_{sub}_{half}")
                xp2 = s_ % 2
                vsx = self.vscr if xp2 == 0 else self.vscr2
                vbx = "vscr" if xp2 == 0 else "vscr2"
                ca = 8 if xp2 == 0 else 16
                V(lambda e, s_=s_, wps=wps, vsx=vsx, ca=ca, xp=xp: e.scalar_tensor_tensor(
                    vsx[:, :], wps[0:128, 0:1024], self.stc[:, s_:s_ + 1],
                    self.x0tf[:, xp, :], ALU.mult, ALU.add,
                    accum_out=self.stc[:, ca:ca + 1]),
                  r=[B(wb0), B(wb1), B("stc"), B(f"x0tf{xp}"), B(vbx), B(f"stc{xp2}")],
                  w=[B(vbx), B(f"stc{xp2}")], l=f"vres_{sub}")
                self._ln1(qb, sub)
            self._mla_kv(qb)

    def _ln1(self, qb, sub):
        B, T, V, S_, DS, DG = self.B, self.T, self.V, self.S_, self.DS, self.DG
        st = self.stc
        xp = sub % 2
        cb_ = 8 if xp == 0 else 16
        sb_ = f"stc{xp}"
        vs = self.vscr if xp == 0 else self.vscr2
        vb = "vscr" if xp == 0 else "vscr2"
        c0, c1, c2, c3, c4, c5 = cb_, cb_ + 1, cb_ + 2, cb_ + 3, cb_ + 4, cb_ + 5
        c6, c7 = cb_ + 6, cb_ + 7
        V(lambda e: e.scalar_tensor_tensor(self.vdum[:, :], vs[:, :], 1.0,
                                           vs[:, :], ALU.mult, ALU.mult,
                                           accum_out=st[:, c1:c1 + 1]),
          r=[B(vb), B("vdum"), B(sb_)], w=[B("vdum"), B(sb_)], l=f"v2_{sub}")
        V(lambda e: e.tensor_scalar(st[:, c2:c2 + 1], st[:, c0:c0 + 1], 1.0 / D, None, ALU.mult),
          r=[B(sb_)], w=[B(sb_)], l=f"mu_{sub}")
        V(lambda e: e.tensor_tensor(st[:, c3:c3 + 1], st[:, c2:c2 + 1], st[:, c2:c2 + 1], ALU.mult),
          r=[B(sb_)], w=[B(sb_)], l=f"mu2_{sub}")
        V(lambda e: e.scalar_tensor_tensor(st[:, c3:c3 + 1], st[:, c1:c1 + 1], 1.0 / D,
                                           st[:, c3:c3 + 1], ALU.mult, ALU.subtract),
          r=[B(sb_)], w=[B(sb_)], l=f"var_{sub}")
        S_(lambda e: e.activation(st[:, c4:c4 + 1], st[:, c3:c3 + 1], AF.Ln, bias=EPS),
           r=[B(sb_), B("econst")], w=[B(sb_)], l=f"ln_{sub}")
        S_(lambda e: e.activation(st[:, c4:c4 + 1], st[:, c4:c4 + 1], AF.Exp, scale=-0.5),
           r=[B(sb_)], w=[B(sb_)], l=f"rstd_{sub}")
        V(lambda e: e.tensor_scalar(st[:, c5:c5 + 1], st[:, c3:c3 + 1], EPS, None, ALU.add),
          r=[B(sb_)], w=[B(sb_)], l=f"ve_{sub}")
        V(lambda e: e.reciprocal(st[:, c5:c5 + 1], st[:, c5:c5 + 1]),
          r=[B(sb_)], w=[B(sb_)], l=f"vr_{sub}")
        V(lambda e: e.tensor_tensor(st[:, c5:c5 + 1], st[:, c3:c3 + 1], st[:, c5:c5 + 1], ALU.mult),
          r=[B(sb_)], w=[B(sb_)], l=f"m2_{sub}")
        S_(lambda e: e.activation(st[:, c6:c6 + 1], st[:, c5:c5 + 1], AF.Ln, bias=EPS),
           r=[B(sb_), B("econst")], w=[B(sb_)], l=f"lnr_{sub}")
        S_(lambda e: e.activation(st[:, c6:c6 + 1], st[:, c6:c6 + 1], AF.Exp, scale=-0.5),
           r=[B(sb_)], w=[B(sb_)], l=f"rr_{sub}")
        V(lambda e: e.scalar_tensor_tensor(st[:, c7:c7 + 1], st[:, c2:c2 + 1], -1.0,
                                           st[:, c4:c4 + 1], ALU.mult, ALU.mult),
          r=[B(sb_)], w=[B(sb_)], l=f"nm_{sub}")
        S_(lambda e: e.activation(self.xto[:, :], vs[:, :], AF.Identity,
                                  bias=st[:, c7:c7 + 1], scale=st[:, c4:c4 + 1]),
           r=[B(vb), B(sb_), B("xto")], w=[B("xto")], l=f"x1_{sub}")
        V(lambda e, xp=xp: e.tensor_scalar(self.xnb[:, xp, :], self.xto[:, :],
                                           st[:, c6:c6 + 1], None, ALU.mult),
          r=[B("xto"), B(sb_), B(f"xnb{xp}")], w=[B(f"xnb{xp}")], l=f"xn1_{sub}")
        V(lambda e, qb=qb, vs=vs: e.tensor_scalar(vs[:, :], self.xto[:, :],
                                                  self.ocol[:, qb:qb + 1], None, ALU.mult),
          r=[B("xto"), B("ocol"), B(vb)], w=[B(vb)], l=f"scx_{sub}")
        DG(lambda e, sub=sub, vs=vs: e.dma_start(self.x1_sp[sub * 128:(sub + 1) * 128, :], vs[:, :]),
           r=[B(vb)], w=[B("x1sp")], l=f"spx_{sub}")
        V(lambda e, qb=qb, xp=xp: e.tensor_scalar(self.sqscr[:, 0:D], self.xnb[:, xp, :],
                                                  self.ocol[:, qb:qb + 1], None, ALU.mult),
          r=[B(f"xnb{xp}"), B("ocol"), B("sqscr")], w=[B("sqscr")], l=f"scn_{sub}")
        DG(lambda e, sub=sub: e.dma_start(self.xn1_sp[sub * 128:(sub + 1) * 128, :],
                                          self.sqscr[:, 0:D]),
           r=[B("sqscr")], w=[B("xn1sp")], l=f"spn_{sub}")
        s_ = sub % 4
        DS(lambda e, s_=s_, xp=xp: e.dma_start_transpose(
            self.ap(self.xn1f_oa, [128, 4, NCH, 128], 0, 128, s_ * NCH * 128,
                    [[128, NCH], [1, 128]]),
            self.xnb[:, xp, :]),
           r=[B(f"xnb{xp}"), B("xn1f")], w=[B("xn1f")], l=f"txn_{sub}")

    def _mla_kv(self, qb):
        B, T, V, S_, DS = self.B, self.T, self.V, self.S_, self.DS
        i = self.inp
        t0 = qb * QB
        pp = qb % 2
        DS(lambda e, t0=t0: e.dma_start(self.ABk_blk[:, 0, :], i["Ak"][:, t0:t0 + QB]),
           r=[B("ABk")], w=[B("ABk")], l=f"ldAk{qb}")
        DS(lambda e, t0=t0: e.dma_start(self.ABk_blk[:, 1, :], i["Bk"][:, t0:t0 + QB]),
           r=[B("ABk")], w=[B("ABk")], l=f"ldBk{qb}")
        for c in range(NCH):
            rhs = lambda c=c: self.ap(self.xn1f_oa, [128, 4, NCH, 128], 0, 128, c * 128,
                                      [[NCH * 128, 4], [1, 128]])
            T(lambda e, c=c, rhs=rhs: e.matmul(
                self.PAB[0:128, 0:512],
                self.ap(self.WkvaA, [128, NCH, 192], 0, 128, c * 192, [[1, 128]]),
                rhs(), start=(c == 0), stop=(c == NCH - 1)),
              r=[B("xn1f"), B("WkvaA")] + ([B("pA0")] if c == 0 else []),
              w=[B("pA0")], l=f"lat{qb}_{c}")
            T(lambda e, c=c, rhs=rhs: e.matmul(
                self.PAB[0:32, 512:1024],
                self.ap(self.WkvaA, [128, NCH, 192], 0, 128, c * 192 + 128, [[1, 32]]),
                rhs(), start=(c == 0), stop=(c == NCH - 1)),
              r=[B("xn1f"), B("WkvaA")] + ([B("pA1")] if c == 0 else []),
              w=[B("pA1")], l=f"krX{qb}_{c}")
        for c in range(NCH):
            rhs = lambda c=c: self.ap(self.xn1f_oa, [128, 4, NCH, 128], 0, 128, c * 128,
                                      [[NCH * 128, 4], [1, 128]])
            T(lambda e, c=c, rhs=rhs: e.matmul(
                self.PAB[64:96, 512:1024],
                self.ap(self.WkvaA, [128, NCH, 192], 0, 128, c * 192 + 160, [[1, 32]]),
                rhs(), start=(c == 0), stop=(c == NCH - 1)),
              r=[B("xn1f"), B("WkvaA")], w=[B("pA1")], l=f"krXs{qb}_{c}")
        V(lambda e: e.tensor_copy(self.lat[:, :], self.PAB[0:128, 0:512]),
          r=[B("pA0"), B("lat")], w=[B("lat")], l=f"elat{qb}")
        V(lambda e: e.tensor_tensor(self.t1s[0:32, :], self.PAB[0:32, 512:1024],
                                    self.ABk_blk[:, 0, :], ALU.mult),
          r=[B("pA1"), B("ABk"), B("t1")], w=[B("t1")], l=f"kr1{qb}")
        V(lambda e: e.tensor_tensor(self.t2s[0:32, :], self.PAB[64:96, 512:1024],
                                    self.ABk_blk[:, 1, :], ALU.mult),
          r=[B("pA1"), B("ABk"), B("t2")], w=[B("t2")], l=f"kr2{qb}")
        V(lambda e, t0=t0: e.tensor_tensor(self.krass[0:32, t0:t0 + 512],
                                           self.t1s[0:32, :], self.t2s[0:32, :], ALU.add),
          r=[B("t1"), B("t2"), B("krass")], w=[B("krass")], l=f"kr3{qb}")
        V(lambda e, t0=t0: e.tensor_copy(self.krass[64:96, t0:t0 + 512],
                                         self.krass[0:32, t0:t0 + 512]),
          r=[B("krass")], w=[B("krass")], l=f"kr4{qb}")
        for c in range(4):
            yo = 512 * (c % 2)
            yb = f"pY{c % 2}"
            T(lambda e, c=c, yo=yo: e.matmul(self.Y2[0:128, yo:yo + 512],
                                             self.WkvbT[:, c, :],
                                             self.lat[:, :], start=True, stop=True),
              r=[B("lat"), B("WkvbT"), B(yb)], w=[B(yb)], l=f"kvf{qb}_{c}")
            V(lambda e, c=c, t0=t0, yo=yo: e.tensor_copy(self.kvpack[:, c, t0:t0 + 512],
                                                         self.Y2[0:128, yo:yo + 512]),
              r=[B(yb), B("kvp")], w=[B("kvp")], l=f"ekvf{qb}_{c}")
        for s_ in range(4):
            kt = qb * 4 + s_
            yo = 512 * (s_ % 2)
            yb = f"pY{s_ % 2}"
            T(lambda e, s_=s_, yo=yo: e.matmul(self.Y2[0:128, yo:yo + 512],
                                               self.lat[:, s_ * 128:(s_ + 1) * 128],
                                               self.Wkvb_n[:, :], start=True, stop=True),
              r=[B("lat"), B("Wkvb_n"), B(yb)], w=[B(yb)], l=f"kvt{kt}")
            V(lambda e, kt=kt, yo=yo: e.tensor_copy(
                self.ap(self.kvt, [128, self.NKT, HEADS * 33], 0, 128, kt * HEADS * 33,
                        [[33, 16], [1, 32]]),
                self.Y2[0:128, yo:yo + 512]),
              r=[B(yb), B("kvt")], w=[B("kvt")], l=f"ekvt{kt}")


        for c in range(4):
            S_(lambda e, c=c, t0=t0: e.activation(
                self.sqX[:, 0, :], self.kvpack[:, c, t0:t0 + 512], AF.Square),
               r=[B("kvp"), B("sqX0")], w=[B("sqX0")], l=f"mks{qb}_{c}")
            T(lambda e, c=c: e.matmul(self.S4[0:1, 0:512], self.ones_b[:, 0:1],
                                      self.sqX[:, 0, :], start=(c == 0), stop=False),
              r=[B("sqX0"), B("ones")] + ([B("pS0")] if c == 0 else []),
              w=[B("pS0")], l=f"mkm{qb}_{c}")
        S_(lambda e, t0=t0: e.activation(self.sqX[0:32, 0, :],
                                         self.krass[0:32, t0:t0 + 512], AF.Square),
           r=[B("krass"), B("sqX0")], w=[B("sqX0")], l=f"mksr{qb}")
        T(lambda e: e.matmul(self.S4[0:1, 0:512], self.ones_b[0:32, 0:1],
                             self.sqX[0:32, 0, :], start=False, stop=True),
          r=[B("sqX0"), B("ones")], w=[B("pS0")], l=f"mkmr{qb}")
        V(lambda e: e.reduce_max(self.mxk[0:1, 5:6], self.S4[0:1, 0:512], axis=AX.X),
          r=[B("pS0"), B("mxk")], w=[B("mxk")], l=f"mkx{qb}")
        V(lambda e: e.tensor_tensor(self.mxk[0:1, 4:5], self.mxk[0:1, 4:5],
                                    self.mxk[0:1, 5:6], ALU.max),
          r=[B("mxk")], w=[B("mxk")], l=f"mkr2{qb}")

    def phase_mla(self, mstop=None):
        B, T, V, S_, G, DS, DG = self.B, self.T, self.V, self.S_, self.G, self.DS, self.DG
        i = self.inp
        for c in range(NCH):
            wt = self.x1su if c % 2 == 0 else self.x1s2
            wtb = "x1su" if c % 2 == 0 else "x1s2"
            DS(lambda e, c=c, wt=wt: e.dma_start(wt[:, :], i["Wq"][c * 128:(c + 1) * 128, :]),
               r=[B(wtb), B("WqA")], w=[B(wtb)], l=f"ldWq{c}")
            for (o_d, o_s, dims, n_) in (
                    (0, 32, [[64, HEADS], [1, 2], [2, 16]], 32),
                    (32, 0, [[64, HEADS], [1, 32]], 32),
                    (64, 33, [[64, HEADS], [2, 16]], 16),
                    (80, 32, [[64, HEADS], [2, 16]], 16)):
                V(lambda e, c=c, o_d=o_d, o_s=o_s, dims=dims, n_=n_, wt=wt: e.tensor_scalar(
                    self.ap(self.WqA, [128, NCH, HEADS, 96], 0, 128,
                            c * HEADS * 96 + o_d, [[96, HEADS], [1, n_]]),
                    self.ap(wt, [128, D], 0, 128, o_s, dims),
                    self.wln1_c[:, c:c + 1], None, ALU.mult),
                  r=[B(wtb), B("w_ln1")], w=[B("WqA")], l=f"g{c}_{o_d}")
        for j in range(4):
            DG(lambda e, j=j: e.dma_start(
                self.WoE[:, j, :],
                bass.AP(i["Wo"].tensor, (256 * j) * D, [[64 * D, 4], [D, 32], [1, D]])),
               w=[B("WoE")], l=f"ldWoE{j}")
        DS(lambda e: e.dma_start(self.masks_s[:], i["masks"][:]), w=[B("masks")], l="ldmask")
        DS(lambda e: e.dma_start(self.nkrow[0:1, 0:sum(self.Ls)], i["padrow"][:]),
           r=[B("nkr")], w=[B("nkr")], l="ldpad")
        T(lambda e: e.matmul(self.PAB[0:128, 0:sum(self.Ls)], self.ones_f[0:1, :],
                             self.nkrow[0:1, 0:sum(self.Ls)], start=True, stop=True),
          r=[B("nkr"), B("onesf"), B("pA0")], w=[B("pA0")], l="mmpad")
        V(lambda e: e.tensor_copy(self.biasKsl[:, :], self.PAB[0:128, 0:sum(self.Ls)]),
          r=[B("pA0"), B("biasKsl")], w=[B("biasKsl")], l="bsl")
        V(lambda e: e.memset(self.Qra[:, :, :], 0.0), r=[B("Qra")], w=[B("Qra")], l="zQra")
        V(lambda e: e.memset(self.sel32[:, :], 0.0), r=[B("sel32")], w=[B("sel32")], l="zsel")
        for m_ in range(4):
            V(lambda e, m_=m_: e.memset(
                self.sel32[32 * m_:32 * m_ + 1, 32 * m_:32 * m_ + 32], 1.0),
              r=[B("sel32")], w=[B("sel32")], l=f"sel{m_}")
        V(lambda e: e.memset(self.rDc[:, :], 0.0), r=[B("rDc")], w=[B("rDc")], l="zrdc")
        if mstop == "prep":
            return
        def emit_reloads(sn):
            if sn >= self.NSLOT:
                return
            pb = sn % 2
            q0n = sn * QB
            for s_ in range(4):
                xp = s_ % 2
                t0_ = 2 * sn * QB + s_ * 128
                t1_ = (2 * sn + 1) * QB + s_ * 128
                DG(lambda e, t0_=t0_, xp=xp: e.dma_start(
                    self.xnsu[:, xp, :], self.xn1_sp[t0_:t0_ + 128, :]),
                   r=[B("xn1sp"), B(f"xnsu{xp}")], w=[B(f"xnsu{xp}")], l=f"lxnA{sn}_{s_}")
                DG(lambda e, t1_=t1_: e.dma_start(
                    self.xns2[:, :], self.xn1_sp[t1_:t1_ + 128, :]),
                   r=[B("xn1sp"), B("xns2")], w=[B("xns2")], l=f"lxnB{sn}_{s_}")
                V(lambda e, xp=xp: e.tensor_tensor(self.xnsu[:, xp, :], self.xnsu[:, xp, :],
                                                   self.xns2[:, :], ALU.add),
                  r=[B("xns2"), B(f"xnsu{xp}")], w=[B(f"xnsu{xp}")], l=f"sxn{sn}_{s_}")
                DS(lambda e, s_=s_, xp=xp, pb=pb: e.dma_start_transpose(
                    self.ap(self.xn1f, [128, 2, 4, NCH, 128], 0, 128,
                            (pb * 4 + s_) * NCH * 128, [[128, NCH], [1, 128]]),
                    self.xnsu[:, xp, :]),
                   r=[B(f"xnsu{xp}"), B(f"xn1fM{pb}")], w=[B(f"xn1fM{pb}")],
                   l=f"txq{sn}_{s_}")
            DS(lambda e, q0n=q0n: e.dma_start(self.ABq_sl[:, 0, :], i["Aq"][:, q0n:q0n + QB]),
               r=[B("ABq")], w=[B("ABq")], l=f"ldAq{sn}")
            DS(lambda e, q0n=q0n: e.dma_start(self.ABq_sl[:, 1, :], i["Bq"][:, q0n:q0n + QB]),
               r=[B("ABq")], w=[B("ABq")], l=f"ldBq{sn}")

        emit_reloads(0)
        sl_off = 0
        for s in range(self.NSLOT):
            if mstop in ("rel1", "q1", "att1", "slot1") and s > 0:
                break
            L = self.Ls[s]
            pb = s % 2

            if mstop == "rel1":
                continue
            import os
            qheads = int(os.environ.get("K_QHEADS", HEADS))
            qskip = os.environ.get("K_QSKIP", "")
            aug_q = []

            def flush_aug():
                while aug_q:
                    fn = aug_q.pop(0)
                    fn()

            for h in range(qheads):
                hr = h % 4
                qps = (self.PAB, self.PAB, self.S4, self.S4)[hr]
                po = (0, 512, 512, 1024)[hr]
                qbn = ("pA0", "pA1", "pS1", "pS2")[hr]
                hp = h % 2
                for c in range(NCH):
                    T(lambda e, c=c, h=h, po=po, qps=qps, pb=pb: e.matmul(
                        qps[0:96, po:po + 512],
                        self.ap(self.WqA, [128, NCH, HEADS, 96], 0, 128, (c * HEADS + h) * 96,
                                [[1, 96]]),
                        self.ap(self.xn1f, [128, 2, 4, NCH, 128], 0, 128,
                                pb * 4 * NCH * 128 + c * 128,
                                [[NCH * 128, 4], [1, 128]]),
                        start=(c == 0), stop=(c == NCH - 1)),
                      r=[B(f"xn1fM{pb}"), B("WqA")] + ([B(qbn)] if c == 0 else []),
                      w=[B(qbn)], l=f"q{s}_{h}_{c}")
                m = h % 4
                j = h // 4
                if "n" not in qskip:
                    V(lambda e, m=m, j=j, po=po, qps=qps: e.tensor_copy(
                        self.Qnope[32 * m:32 * m + 32, j, :], qps[32:64, po:po + 512]),
                      r=[B(qbn), B("Qnope")], w=[B("Qnope")], l=f"eqn{s}_{h}")
                rb = 64 * (h % 2)
                cb = h // 2
                if "r" not in qskip:
                    V(lambda e, po=po, qps=qps: e.tensor_tensor(
                        self.t1s[0:32, :], qps[0:32, po:po + 512], self.ABq_sl[0:32, 0, :],
                        ALU.mult),
                      r=[B(qbn), B("ABq"), B("t1")], w=[B("t1")], l=f"qr1{s}_{h}")
                    V(lambda e, po=po, qps=qps: e.tensor_tensor(
                        self.t2s[0:32, :], qps[64:96, po:po + 512], self.ABq_sl[0:32, 1, :],
                        ALU.mult),
                      r=[B(qbn), B("ABq"), B("t2")], w=[B("t2")], l=f"qr2{s}_{h}")
                    V(lambda e, rb=rb, cb=cb: e.tensor_tensor(
                        self.Qra[rb:rb + 32, cb, :], self.t1s[0:32, :], self.t2s[0:32, :], ALU.add),
                      r=[B("t1"), B("t2"), B("Qra")], w=[B("Qra")], l=f"qr3{s}_{h}")
                if "a" not in qskip:
                    S_(lambda e, po=po, hp=hp, qps=qps: e.activation(
                        self.sqX[0:64, hp, :], qps[0:64, po:po + 512], AF.Square),
                       r=[B(qbn), B(f"sqX{hp}")], w=[B(f"sqX{hp}")], l=f"qs{s}_{h}")

                    def aug(hp=hp, rb=rb, cb=cb, s=s, h=h):
                        T(lambda e, hp=hp: e.matmul(
                            self.S4[0:1, 1536:2048], self.ones_b[0:64, 0:1],
                            self.sqX[0:64, hp, :], start=True, stop=True),
                          r=[B(f"sqX{hp}"), B("ones"), B("pS3")], w=[B("pS3")],
                          l=f"qn{s}_{h}")
                        V(lambda e, rb=rb, cb=cb: e.tensor_scalar(
                            self.Qra[rb + 32:rb + 33, cb, :], self.S4[0:1, 1536:2048],
                            self.mxk[0:1, 4:5], -0.5, ALU.add, ALU.mult),
                          r=[B("pS3"), B("mxk"), B("Qra")], w=[B("Qra")], l=f"eqa{s}_{h}")
                    aug_q.append(aug)
                    if len(aug_q) > 1:
                        aug_q.pop(0)()
            flush_aug()
            emit_reloads(s + 1)

            if mstop == "q1":
                continue
            for j in range(4):
                def avs(kt, first, last):
                    ep = kt % 2
                    for m in range(4):
                        h = 4 * j + m
                        ypt = (self.PAB, self.PAB, self.Y2, self.Y2)[m]
                        yb = 512 * (m % 2)
                        ybn = ("pA0", "pA1", "pY0", "pY1")[m]
                        T(lambda e, m=m, kt=kt, h=h, ypt=ypt, yb=yb, ep=ep,
                          first=first, last=last: e.matmul(
                              ypt[0:33, yb:yb + 512],
                              self.ap(self.kvt, [128, self.NKT, HEADS * 33], 0, 128,
                                      kt * HEADS * 33 + 33 * h, [[1, 33]]),
                              self.expT[:, ep, m, :], start=first, stop=last),
                          r=[B(f"ex{ep}{m}"), B("kvt")] + ([B(ybn)] if first else []),
                          w=[B(ybn)], l=f"a{s}_{j}_{kt}_{m}")
                for kt in range(L):
                    mskd = (kt >= 8 * s)
                    ep = kt % 2
                    for m in range(4):
                        h = 4 * j + m
                        so = 512 * m
                        psn = f"pS{m}"
                        T(lambda e, m=m, j=j, kt=kt, so=so: e.matmul(
                            self.S4[0:128, so:so + 512],
                            self.kvpack[32 * m:32 * m + 32, j, kt * KT:(kt + 1) * KT],
                            self.Qnope[32 * m:32 * m + 32, j, :],
                            start=True, stop=False, tile_position=(32 * m, 0)),
                          r=[B("kvp"), B("Qnope"), B(psn)], w=[B(psn)], l=f"n{s}_{j}_{kt}_{m}")
                        rb = 64 * (m % 2)
                        cb = 2 * j + m // 2
                        T(lambda e, rb=rb, cb=cb, kt=kt, so=so: e.matmul(
                            self.S4[0:128, so:so + 512],
                            self.krass[rb:rb + 64, kt * KT:(kt + 1) * KT],
                            self.Qra[rb:rb + 64, cb, :],
                            start=False, stop=True, tile_position=(rb, 0)),
                          r=[B("krass"), B("Qra")], w=[B(psn)], l=f"r{s}_{j}_{kt}_{m}")
                        if m % 2 == 1:
                            S_(lambda e, m=m, kt=kt, sl_off=sl_off, ep=ep: e.activation(
                                self.ap(self.expT, [128, 2, 4, QB], 0, 128,
                                        (ep * 4 + m - 1) * QB, [[1, 2 * QB]]),
                                self.S4[0:128, (m - 1) * 512:(m + 1) * 512], AF.Exp,
                                bias=self.biasKsl[:, sl_off + kt:sl_off + kt + 1], scale=0.125),
                               r=[B(f"pS{m-1}"), B(psn), B("biasKsl"),
                                  B(f"ex{ep}{m-1}"), B(f"ex{ep}{m}")],
                               w=[B(f"ex{ep}{m-1}"), B(f"ex{ep}{m}")],
                               l=f"x{s}_{j}_{kt}_{m}")
                    if mskd:
                        md = kt - 8 * s
                        V(lambda e, md=md, ep=ep: e.tensor_tensor(
                            self.ap(self.expT, [128, 2, 4, QB], 0, 128, ep * 4 * QB,
                                    [[1, 4 * QB]]),
                            self.ap(self.expT, [128, 2, 4, QB], 0, 128, ep * 4 * QB,
                                    [[1, 4 * QB]]),
                            self.ap(self.masks_s, [128, 8, QB], 0, 128, md * QB,
                                    [[0, 4], [1, QB]]),
                            ALU.mult),
                          r=[B("masks")] + [B(f"ex{ep}{m}") for m in range(4)],
                          w=[B(f"ex{ep}{m}") for m in range(4)], l=f"m{s}_{j}_{kt}")
                    if kt > 0:
                        avs(kt - 1, kt - 1 == 0, False)
                avs(L - 1, L == 1, True)
                ypts = (self.PAB, self.PAB, self.Y2, self.Y2)
                ybs = (0, 512, 0, 512)
                ybns = ("pA0", "pA1", "pY0", "pY1")
                for m in range(4):
                    V(lambda e, m=m: e.tensor_copy(
                        self.rDc[32 * m:32 * m + 1, :], ypts[m][32:33, ybs[m]:ybs[m] + 512]),
                      r=[B(ybns[m]), B("rDc")], w=[B("rDc")], l=f"rd{s}_{j}_{m}")
                T(lambda e: e.matmul(self.S4[0:128, 1536:2048], self.sel32[:, :],
                                     self.rDc[:, :], start=True, stop=True),
                  r=[B("rDc"), B("sel32"), B("pS3")], w=[B("pS3")], l=f"bc{s}_{j}")
                V(lambda e: e.reciprocal(self.Bbc4[:, :], self.S4[0:128, 1536:2048]),
                  r=[B("pS3"), B("Bbc")], w=[B("Bbc")], l=f"ebc{s}_{j}")
                for m in range(4):
                    V(lambda e, m=m, j=j: e.tensor_tensor(
                        self.yasm[32 * m:32 * m + 32, j, :],
                        ypts[m][0:32, ybs[m]:ybs[m] + 512],
                        self.Bbc4[32 * m:32 * m + 32, :], ALU.mult),
                      r=[B(ybns[m]), B("Bbc"), B("yasm")], w=[B("yasm")],
                      l=f"ny{s}_{j}_{m}")

            if mstop == "att1":
                continue
            for s_ in range(4):
                t0_ = 2 * s * QB + s_ * 128
                t1_ = (2 * s + 1) * QB + s_ * 128
                DS(lambda e, t0_=t0_: e.dma_start(
                    self.x1su[:, :], self.x1_sp[t0_:t0_ + 128, :]),
                   r=[B("x1sp"), B("x1su")], w=[B("x1su")], l=f"lx1A{s}_{s_}")
                DS(lambda e, t1_=t1_: e.dma_start(
                    self.x1s2[:, :], self.x1_sp[t1_:t1_ + 128, :]),
                   r=[B("x1sp"), B("x1s2")], w=[B("x1s2")], l=f"lx1B{s}_{s_}")
                V(lambda e: e.tensor_tensor(self.x1su[:, :], self.x1su[:, :],
                                            self.x1s2[:, :], ALU.add),
                  r=[B("x1s2"), B("x1su")], w=[B("x1su")], l=f"sx1{s}_{s_}")
                for j in range(4):
                    for half in range(2):
                        T(lambda e, s_=s_, j=j, half=half: e.matmul(
                            self.PAB[0:128, half * 512:(half + 1) * 512],
                            self.yasm[:, j, s_ * 128:(s_ + 1) * 128],
                            self.WoE[:, j, half * 512:(half + 1) * 512],
                            start=(j == 0), stop=(j == 3)),
                          r=[B("yasm"), B("WoE")] + ([B(f"pA{half}")] if j == 0 else []),
                          w=[B(f"pA{half}")], l=f"mo{s}_{s_}_{j}_{half}")
                V(lambda e: e.scalar_tensor_tensor(
                    self.vscr[:, :], self.PAB[0:128, 0:1024], 1.0, self.x1su[:, :],
                    ALU.mult, ALU.add),
                  r=[B("pA0"), B("pA1"), B("x1su"), B("vscr")], w=[B("vscr")], l=f"y1{s}_{s_}")
                V(lambda e: e.tensor_tensor(self.xto[:, :], self.vscr[:, :],
                                            self.x1su[:, :], ALU.add),
                  r=[B("vscr"), B("x1su"), B("xto")], w=[B("xto")], l=f"g{s}_{s_}")
                DS(lambda e, s=s, s_=s_: e.dma_start(
                    self.g_sp[(s * 4 + s_) * 128:(s * 4 + s_ + 1) * 128, :], self.xto[:, :]),
                   r=[B("xto")], w=[B("gsp")], l=f"spg{s}_{s_}")
                V(lambda e: e.scalar_tensor_tensor(self.sqscr[:, :], self.vscr[:, :], 1.0,
                                                   self.vscr[:, :], ALU.mult, ALU.mult,
                                                   accum_out=self.stc[:, 9:10]),
                  r=[B("vscr"), B("sqscr"), B("stc")], w=[B("sqscr"), B("stc")], l=f"y2{s}_{s_}")
                V(lambda e: e.tensor_scalar(self.stc[:, 10:11], self.stc[:, 9:10], 1.0 / D,
                                            None, ALU.mult),
                  r=[B("stc")], w=[B("stc")], l=f"ms{s}_{s_}")
                S_(lambda e: e.activation(self.stc[:, 11:12], self.stc[:, 10:11], AF.Ln,
                                          bias=EPS),
                   r=[B("stc"), B("econst")], w=[B("stc")], l=f"l2{s}_{s_}")
                S_(lambda e: e.activation(self.stc[:, 11:12], self.stc[:, 11:12], AF.Exp,
                                          scale=-0.5),
                   r=[B("stc")], w=[B("stc")], l=f"r2{s}_{s_}")
                xp = s_ % 2
                V(lambda e, xp=xp: e.tensor_scalar(self.xnb[:, xp, :], self.vscr[:, :],
                                                   self.stc[:, 11:12], None, ALU.mult),
                  r=[B("vscr"), B("stc"), B(f"xnb{xp}")], w=[B(f"xnb{xp}")], l=f"x2{s}_{s_}")
                so_ = s * 4 + s_
                DG(lambda e, so_=so_, xp=xp: e.dma_start(
                    self.xn2_sp[so_ * 128:(so_ + 1) * 128, :], self.xnb[:, xp, :]),
                   r=[B(f"xnb{xp}")], w=[B("xn2sp")], l=f"sp2{s}_{s_}")
            sl_off += L

    def phase_moe(self):
        B, T, V, S_, G, DS, DG = self.B, self.T, self.V, self.S_, self.G, self.DS, self.DG
        i = self.inp
        NOS = self.NOS
        def emit_rebuild(ss):
            xp = ss % 4
            DG(lambda e, ss=ss, xp=xp: e.dma_start(
                self.xe[:, xp, :], self.xn2_sp[ss * 128:(ss + 1) * 128, :]),
               r=[B("xn2sp"), B(f"xe{xp}")], w=[B(f"xe{xp}")], l=f"lx2{ss}")
            DS(lambda e, ss=ss, xp=xp: e.dma_start_transpose(
                self.ap(self.xn2f, [128, self.NOS, NCH, 128], 0, 128, ss * NCH * 128,
                        [[128, NCH], [1, 128]]),
                self.xe[:, xp, :]),
               r=[B(f"xe{xp}"), B("xn2f")], w=[B("xn2f")], l=f"tx2{ss}")
            for c in range(NCH):
                V(lambda e, ss=ss, c=c: e.tensor_scalar(
                    self.xn2f[:, ss, c, :], self.xn2f[:, ss, c, :],
                    self.wln2_c[:, c:c + 1], None, ALU.mult),
                  r=[B("xn2f"), B("w_ln2")], w=[B("xn2f")], l=f"w2{ss}_{c}")

        def emit_gate(ss):
            for c in range(NCH):
                T(lambda e, ss=ss, c=c: e.matmul(
                    self.S4[0:128, 1536:1536 + EE],
                    self.ap(self.xn2f, [128, NOS, NCH, 128], 0, 128, (ss * NCH + c) * 128,
                            [[1, 128]]),
                    self.Wgate_s[:, c, :], start=(c == 0), stop=(c == NCH - 1)),
                  r=[B("xn2f"), B("Wgate")] + ([B("pS3")] if c == 0 else []),
                  w=[B("pS3")], l=f"gm{ss}_{c}")
            gp = self.gpr
            V(lambda e: e.tensor_tensor(gp[:, 0, :], self.S4[0:128, 1536:1536 + EE],
                                        self.ebias_b[:, :], ALU.add),
              r=[B("pS3"), B("ebias"), B("gp")], w=[B("gp")], l=f"gb{ss}")
            V(lambda e: e.reduce_max(self.gcol[:, 0:1], gp[:, 0, :], axis=AX.X, negate=True),
              r=[B("gp"), B("gc")], w=[B("gc")], l=f"gx{ss}")
            S_(lambda e: e.activation(gp[:, 1, :], gp[:, 0, :], AF.Exp, bias=self.gcol[:, 0:1]),
               r=[B("gp"), B("gc")], w=[B("gp")], l=f"ge{ss}")
            V(lambda e: e.reduce_max(self.gcol[:, 1:2], gp[:, 1, :], axis=AX.X),
              r=[B("gp"), B("gc")], w=[B("gc")], l=f"m1{ss}")
            V(lambda e: e.tensor_scalar(gp[:, 2, :], gp[:, 1, :], self.gcol[:, 1:2], None,
                                        ALU.is_ge),
              r=[B("gp"), B("gc")], w=[B("gp")], l=f"q1{ss}")
            V(lambda e: e.tensor_tensor(gp[:, 3, :], gp[:, 1, :], gp[:, 2, :], ALU.mult),
              r=[B("gp")], w=[B("gp")], l=f"tt{ss}")
            V(lambda e: e.tensor_tensor(gp[:, 3, :], gp[:, 1, :], gp[:, 3, :], ALU.subtract),
              r=[B("gp")], w=[B("gp")], l=f"e2{ss}")
            V(lambda e: e.reduce_max(self.gcol[:, 2:3], gp[:, 3, :], axis=AX.X),
              r=[B("gp"), B("gc")], w=[B("gc")], l=f"m2{ss}")
            V(lambda e: e.tensor_scalar(gp[:, 3, :], gp[:, 3, :], self.gcol[:, 2:3], None,
                                        ALU.is_ge),
              r=[B("gp"), B("gc")], w=[B("gp")], l=f"q2{ss}")
            V(lambda e: e.tensor_tensor(gp[:, 2, :], gp[:, 2, :], gp[:, 3, :], ALU.add),
              r=[B("gp")], w=[B("gp")], l=f"qs{ss}")
            V(lambda e: e.tensor_tensor(gp[:, 2, :], gp[:, 1, :], gp[:, 2, :], ALU.mult),
              r=[B("gp")], w=[B("gp")], l=f"nu{ss}")
            V(lambda e: e.tensor_tensor(self.gcol[:, 3:4], self.gcol[:, 1:2],
                                        self.gcol[:, 2:3], ALU.add),
              r=[B("gc")], w=[B("gc")], l=f"de{ss}")
            V(lambda e: e.reciprocal(self.gcol[:, 4:5], self.gcol[:, 3:4]),
              r=[B("gc")], w=[B("gc")], l=f"rr{ss}")
            V(lambda e, ss=ss: e.tensor_scalar(self.comb_t[:, ss, :], gp[:, 2, :],
                                               self.gcol[:, 4:5], None, ALU.mult),
              r=[B("gp"), B("gc"), B("cmb")], w=[B("cmb")], l=f"cm{ss}")
            T(lambda e, ss=ss: e.transpose(self.Y2[0:8, 0:128], self.comb_t[:, ss, :],
                                           self.iden_f[:]),
              r=[B("cmb"), B("iden"), B("pY0")], w=[B("pY0")], l=f"ct{ss}")
            V(lambda e: e.tensor_copy(self.comb_st[:, :], self.Y2[0:8, 0:128]),
              r=[B("pY0"), B("cst")], w=[B("cst")], l=f"ec{ss}")
            DS(lambda e, ss=ss: e.dma_start(self.comb_dram[:, ss * 128:(ss + 1) * 128],
                                            self.comb_st[:, :]),
               r=[B("cst")], w=[B("cfd")], l=f"wcf{ss}")

        def emit_expert_blk(ei, eidx, pp, blk):
            if eidx is not None:
                DS(lambda e, eidx=eidx, blk=blk: e.dma_start(
                    self.comb_row[:, :],
                    self.comb_dram[eidx, blk * 512:(blk + 1) * 512][None, :]),
                   r=[B("cfd"), B("crow")], w=[B("crow")], l=f"lcf{ei}_{blk}")
                T(lambda e: e.matmul(
                    self.Y2[0:128, 0:512], self.ones_f[0:1, :],
                    self.comb_row[:, :], start=True, stop=True),
                  r=[B("crow"), B("onesf"), B("pY0")], w=[B("pY0")], l=f"be{ei}_{blk}")
                V(lambda e: e.tensor_copy(self.Be[:, :], self.Y2[0:128, 0:512]),
                  r=[B("pY0"), B("Be")], w=[B("Be")], l=f"eb{ei}_{blk}")
            for ic in range(4):
                gso = 1024 * (ic % 2)
                gb = f"pS{2 * (ic % 2)}"
                ub = f"pS{2 * (ic % 2) + 1}"
                for c in range(NCH):
                    T(lambda e, ic=ic, c=c, pp=pp, blk=blk, gso=gso: e.matmul(
                        self.S4[0:128, gso:gso + 512],
                        self.ap(self.Wg_s, [128, 2, NCH, II], 0, 128,
                                (pp * NCH + c) * II + ic * 128, [[1, 128]]),
                        self.ap(self.xn2f, [128, self.NOS, NCH, 128], 0, 128,
                                blk * 4 * NCH * 128 + c * 128, [[NCH * 128, 4], [1, 128]]),
                        start=(c == 0), stop=(c == NCH - 1)),
                      r=[B(f"wg{pp}"), B("xn2f")] + ([B(gb)] if c == 0 else []),
                      w=[B(gb)], l=f"hg{ei}_{blk}_{ic}_{c}")
                S_(lambda e, ic=ic, gso=gso, gb=gb: e.activation(
                    self.sg[:, ic, :], self.S4[0:128, gso:gso + 512], AF.Silu),
                   r=[B(gb), B("sg")], w=[B("sg")], l=f"si{ei}_{blk}_{ic}")
                for c in range(NCH):
                    T(lambda e, ic=ic, c=c, pp=pp, blk=blk, gso=gso: e.matmul(
                        self.S4[0:128, gso + 512:gso + 1024],
                        self.ap(self.Wu_s, [128, 2, NCH, II], 0, 128,
                                (pp * NCH + c) * II + ic * 128, [[1, 128]]),
                        self.ap(self.xn2f, [128, self.NOS, NCH, 128], 0, 128,
                                blk * 4 * NCH * 128 + c * 128, [[NCH * 128, 4], [1, 128]]),
                        start=(c == 0), stop=(c == NCH - 1)),
                      r=[B(f"wu{pp}"), B("xn2f")] + ([B(ub)] if c == 0 else []),
                      w=[B(ub)], l=f"hu{ei}_{blk}_{ic}_{c}")
                if eidx is None:
                    V(lambda e, ic=ic, gso=gso, ub=ub: e.tensor_copy(
                        self.hu[:, ic, :], self.S4[0:128, gso + 512:gso + 1024]),
                      r=[B(ub), B("hu")], w=[B("hu")], l=f"eu{ei}_{blk}_{ic}")
                else:
                    V(lambda e, ic=ic, gso=gso, ub=ub: e.tensor_tensor(
                        self.hu[:, ic, :], self.S4[0:128, gso + 512:gso + 1024],
                        self.Be[:, :], ALU.mult),
                      r=[B(ub), B("Be"), B("hu")], w=[B("hu")], l=f"eu{ei}_{blk}_{ic}")
                V(lambda e, ic=ic: e.tensor_tensor(self.gu[:, ic, :], self.sg[:, ic, :],
                                                   self.hu[:, ic, :], ALU.mult),
                  r=[B("sg"), B("hu"), B("gu")], w=[B("gu")], l=f"gu{ei}_{blk}_{ic}")
            for s4 in range(4):
                sub = blk * 4 + s4
                dps = self.PAB if s4 % 2 == 0 else self.Y2
                db0 = "pA0" if s4 % 2 == 0 else "pY0"
                db1 = "pA1" if s4 % 2 == 0 else "pY1"
                for ic in range(4):
                    for half in range(2):
                        T(lambda e, ic=ic, s4=s4, half=half, pp=pp, dps=dps: e.matmul(
                            dps[0:128, half * 512:(half + 1) * 512],
                            self.gu[:, ic, s4 * 128:(s4 + 1) * 128],
                            self.ap(self.Wd_s, [128, 2, 4, D], 0, 128,
                                    (pp * 4 + ic) * D + half * 512, [[1, 512]]),
                            start=(ic == 0), stop=(ic == 3)),
                          r=[B("gu"), B(f"wd{pp}")] + ([B(db0 if half == 0 else db1)] if ic == 0 else []),
                          w=[B(db0 if half == 0 else db1)], l=f"wd{ei}_{blk}_{s4}_{ic}_{half}")
                V(lambda e, sub=sub, dps=dps: e.tensor_tensor(self.acc[:, sub, :],
                                                              dps[0:128, 0:1024],
                                                              self.acc[:, sub, :], ALU.add),
                  r=[B(db0), B(db1), B("acc")], w=[B("acc")], l=f"ac{ei}_{blk}_{s4}")

        def emit_wload(ei, pp, wg_src, wu_src, wd_src, sl):
            DG(lambda e, wg_src=wg_src, sl=sl, pp=pp: e.dma_start(
                self.Wg_s[:, pp, :, :], wg_src[sl].rearrange("(c p) i -> p c i", p=128)),
               r=[B(f"wg{pp}")], w=[B(f"wg{pp}")], l=f"lg{ei}")
            DG(lambda e, wu_src=wu_src, sl=sl, pp=pp: e.dma_start(
                self.Wu_s[:, pp, :, :], wu_src[sl].rearrange("(c p) i -> p c i", p=128)),
               r=[B(f"wu{pp}")], w=[B(f"wu{pp}")], l=f"lu{ei}")
            DG(lambda e, wd_src=wd_src, sl=sl, pp=pp: e.dma_start(
                self.Wd_s[:, pp, :, :], wd_src[sl].rearrange("(c p) d -> p c d", p=128)),
               r=[B(f"wd{pp}")], w=[B(f"wd{pp}")], l=f"ld{ei}")

        NBLK = self.OWN_T // QB
        emit_wload(0, 0, i["sWg"], i["sWu"], i["sWd"], 0)
        for blk in range(NBLK):
            DS(lambda e, blk=blk: e.dma_start(
                self.acc[:, blk * 4:(blk + 1) * 4, :],
                self.g_sp[blk * 512:(blk + 1) * 512].rearrange("(k p) d -> p k d", p=128)),
               r=[B("gsp"), B("acc")], w=[B("acc")], l=f"ldG{blk}")
            for s4 in range(4):
                ss = blk * 4 + s4
                emit_rebuild(ss)
                emit_gate(ss)
            emit_expert_blk(0, None, 0, blk)
        for ex in range(EE):
            ei = ex + 1
            pp = ei % 2
            emit_wload(ei, pp, i["Wg"], i["Wu"], i["Wd"], ex)
            for blk in range(NBLK):
                emit_expert_blk(ei, ex, pp, blk)

    def phase_final(self):
        B, T, V, S_, DS = self.B, self.T, self.V, self.S_, self.DS
        st = self.stc
        for sub in range(self.NOS):
            V(lambda e, sub=sub: e.reduce_sum(st[:, 8:9], self.acc[:, sub, :], axis=AX.X),
              r=[B("acc"), B("stc")], w=[B("stc")], l=f"fs{sub}")
            V(lambda e, sub=sub: e.scalar_tensor_tensor(
                self.sqscr[:, :], self.acc[:, sub, :], 1.0, self.acc[:, sub, :],
                ALU.mult, ALU.mult, accum_out=st[:, 9:10]),
              r=[B("acc"), B("sqscr"), B("stc")], w=[B("sqscr"), B("stc")], l=f"f2{sub}")
            V(lambda e: e.tensor_scalar(st[:, 10:11], st[:, 8:9], 1.0 / D, None, ALU.mult),
              r=[B("stc")], w=[B("stc")], l=f"fm{sub}")
            V(lambda e: e.tensor_tensor(st[:, 11:12], st[:, 10:11], st[:, 10:11], ALU.mult),
              r=[B("stc")], w=[B("stc")], l=f"f3{sub}")
            V(lambda e: e.scalar_tensor_tensor(st[:, 11:12], st[:, 9:10], 1.0 / D,
                                               st[:, 11:12], ALU.mult, ALU.subtract),
              r=[B("stc")], w=[B("stc")], l=f"fv{sub}")
            S_(lambda e: e.activation(st[:, 12:13], st[:, 11:12], AF.Ln, bias=EPS),
               r=[B("stc"), B("econst")], w=[B("stc")], l=f"fl{sub}")
            S_(lambda e: e.activation(st[:, 12:13], st[:, 12:13], AF.Exp, scale=-0.5),
               r=[B("stc")], w=[B("stc")], l=f"fr{sub}")
            V(lambda e: e.scalar_tensor_tensor(st[:, 15:16], st[:, 10:11], -1.0,
                                               st[:, 12:13], ALU.mult, ALU.mult),
              r=[B("stc")], w=[B("stc")], l=f"fn{sub}")
            S_(lambda e, sub=sub: e.activation(self.xto[:, :], self.acc[:, sub, :], AF.Identity,
                                               bias=st[:, 15:16], scale=st[:, 12:13]),
               r=[B("acc"), B("stc"), B("xto")], w=[B("xto")], l=f"fo{sub}")
            DS(lambda e, sub=sub: e.dma_start(self.out[sub * 128:(sub + 1) * 128, :],
                                              self.xto[:, :]),
               r=[B("xto")], w=[B("outd")], l=f"o{sub}")
        self.p.barrier(list(self._bufs.values()))

    def build(self, stop_after=None):
        self.declare_io()
        self.alloc_persist()
        self.nc.const_aps.register(F32, EPS, self.econst[:, 0:1])
        self.alloc_outer()
        self.xn1f_oa = self.sb("xn1f_oa", [128, 4, NCH, 128], BF16)
        self.phase_prep()
        if stop_after != "prep":
            self.phase_outer()
            if self.debug:
                for nm, bufn, tile, shape, dt in (
                        ("qA", "qA", self.qA, [128, self.S], BF16),
                        ("kA", "kA", self.kA, [128, self.S], BF16),
                        ("vt", "vt", self.vt, [128, self.NKT, 129], BF16),
                        ("krass", "krass", self.krass, [128, self.S], BF16),
                        ("kvpack", "kvp", self.kvpack, [128, 4, self.S], BF16),
                        ("kvt", "kvt", self.kvt, [128, self.NKT, HEADS * 33], BF16)):
                    t = self.nc.dram_tensor("dbg_" + nm, shape, dt,
                                            kind="ExternalOutput").ap()
                    self.p.add("sync", lambda e, t=t, tile=tile: e.dma_start(t[:], tile[:]),
                               reads=(self.B(bufn),), writes=(self.B("dbg"),),
                               label=f"dump_{nm}", dma=True)
        if stop_after in (None, "moe") or (stop_after or "").startswith("mla"):
            self.free_arena(set(self.OUTER_NAMES + ["xn1f_oa"]))
            self.alloc_mla()
            ms = stop_after[4:] if (stop_after or "").startswith("mla_") else None
            self.phase_mla(mstop=ms)
        if stop_after in (None, "moe"):
            self.free_arena(set(self.MLA_NAMES))
            self.alloc_moe()
            self.phase_moe()
        if stop_after is None:
            self.phase_final()
        else:
            self.p.barrier(list(self._bufs.values()))
            self.p.add("vector", lambda e: e.memset(self.xto[:, :], 0.0),
                       reads=(), writes=(self.B("xto"),), label="dumz")
            for sub in range(self.NOS):
                self.p.add("sync", lambda e, sub=sub: e.dma_start(
                    self.out[sub * 128:(sub + 1) * 128, :], self.xto[:, :]),
                    reads=(self.B("xto"),), writes=(self.B("outd"),),
                    label=f"do{sub}", dma=True)
            self.p.barrier([self.B("outd")])
        self.p.emit_all(self.nc)
        for name, cm in reversed(self._arena):
            cm.__exit__(None, None, None)
        return self.nc


def _dims_n(dims):
    n = 1
    for d in dims:
        n *= d[1]
    return n



def host_tables(S, half):
    NQB = S // QB
    NSLOT = NQB // 2
    Ls = [8 * (s + 1) for s in range(NSLOT)]
    A, Bm = outer_tables(S)
    own_qbs = [2 * s + half for s in range(NSLOT)]
    pos = np.concatenate([np.arange(qb * QB, (qb + 1) * QB) for qb in own_qbs])
    Aq, Bq = mla_tables(pos, 4)
    Ak, Bk = mla_tables(np.arange(S), 1)
    orow = np.zeros((1, NQB), np.float32)
    for qb in own_qbs:
        orow[0, qb] = 1.0
    padrow = np.zeros((1, sum(Ls)), np.float32)
    off = 0
    for s in range(NSLOT):
        lim = 4 * (2 * s + half) + 4
        for j in range(Ls[s]):
            if j >= lim:
                padrow[0, off + j] = NEG
        off += Ls[s]
    masks = np.ones((128, 8, QB), np.float32)
    f = np.arange(QB)[None, :]
    pp = np.arange(128)[:, None]
    for j in range(8):
        if half == 0 and j < 4:
            masks[:, j, :] = (f - pp - 128 * j >= 0)
        if half == 1 and j >= 4:
            masks[:, j, :] = (f - pp - 128 * (j - 4) >= 0)
    return {
        "Aout": A, "Bout": Bm, "Aq": Aq, "Bq": Bq, "Ak": Ak, "Bk": Bk,
        "iden": np.eye(128, dtype=np.float32), "orow": orow, "padrow": padrow,
        "masks": masks.astype(BF),
    }


_nc_cache = {}


def kernel(L0_attn_x, L0_attn_wq, L0_attn_wk, L0_attn_wv, L0_attn_wo,
           L0_ffn_w_ln1, L0_ffn_Wq, L0_ffn_Wkva, L0_ffn_Wkvb, L0_ffn_Wo,
           L0_ffn_w_ln2, L0_ffn_Wgate, L0_ffn_expert_bias, L0_ffn_Wg,
           L0_ffn_Wu, L0_ffn_Wd, L0_ffn_sWg, L0_ffn_sWu, L0_ffn_sWd):
    x = np.ascontiguousarray(np.asarray(L0_attn_x, dtype=np.float32))
    nb, S, _ = x.shape
    weights = {
        "wq": L0_attn_wq, "wk": L0_attn_wk, "wv": L0_attn_wv, "wo": L0_attn_wo,
        "w_ln1": L0_ffn_w_ln1, "Wq": L0_ffn_Wq, "Wkva": L0_ffn_Wkva,
        "Wkvb": L0_ffn_Wkvb, "Wo": L0_ffn_Wo, "w_ln2": L0_ffn_w_ln2,
        "Wgate": L0_ffn_Wgate, "ebias": L0_ffn_expert_bias, "Wg": L0_ffn_Wg,
        "Wu": L0_ffn_Wu, "Wd": L0_ffn_Wd, "sWg": L0_ffn_sWg, "sWu": L0_ffn_sWu,
        "sWd": L0_ffn_sWd,
    }
    weights = {k: np.ascontiguousarray(np.asarray(v, dtype=np.float32))
               for k, v in weights.items()}
    if S not in _nc_cache:
        _nc_cache[S] = Builder(S).build()
    nc = _nc_cache[S]
    tabs = [host_tables(S, h) for h in range(2)]
    n_cores = 2 * nb
    in_maps = []
    for b in range(nb):
        for h in range(2):
            m = dict(weights)
            m["x"] = x[b]
            m.update(tabs[h])
            in_maps.append(m)
    res = run_bass_kernel_spmd(nc, in_maps, core_ids=list(range(n_cores)))
    out = np.empty((nb, S, D), np.float32)
    NSLOT = (S // QB) // 2
    for b in range(nb):
        for h in range(2):
            o = res.results[2 * b + h]["out"]
            for s in range(NSLOT):
                qb = 2 * s + h
                out[b, qb * QB:(qb + 1) * QB, :] = o[s * QB:(s + 1) * QB, :]
    return out


# version 15
# speedup vs baseline: 1.2686x; 1.0270x over previous
import math
import numpy as np
import ml_dtypes
import concourse.bass as bass
import concourse.mybir as mybir
from concourse.bass_utils import run_bass_kernel_spmd

F32 = mybir.dt.float32
BF16 = mybir.dt.bfloat16
AF = mybir.ActivationFunctionType
ALU = mybir.AluOpType
AX = mybir.AxisListType
BF = ml_dtypes.bfloat16

D = 1024
NCH = 8
HEADS = 16
RLAT = 128
KR = 32
EE = 8
II = 512
OHD = 128
QB = 512
KT = 128
EPS = 1e-5
NEG = -1e30

ENGS = ("tensor", "vector", "scalar", "gpsimd", "sync")


class _Item:
    __slots__ = ("idx", "eng", "emit", "deps", "target", "label", "amt",
                 "semname", "use")

    def __init__(self, idx, eng, emit, label, amt):
        self.idx, self.eng, self.emit, self.label, self.amt = idx, eng, emit, label, amt
        self.deps = []
        self.target = None


class Buf:
    __slots__ = ("name", "last_write", "readers")

    def __init__(self, name):
        self.name = name
        self.last_write = None
        self.readers = []


class Prog:
    def __init__(self):
        self.items = []
        self.fence_items = None
        self.fenced_engines = set()

    EXCL = frozenset(("pA0", "pA1", "pS0", "pS1", "pS2", "pS3", "pY0", "pY1"))

    def add(self, eng, emit, reads=(), writes=(), label="", dma=False):
        it = _Item(len(self.items), eng, emit, label, 16 if dma else 1)
        deps = []
        if self.fence_items is not None and eng not in self.fenced_engines:
            deps.extend(self.fence_items)
            self.fenced_engines.add(eng)
        excl = [b for b in reads if b.name in self.EXCL]
        reads = [b for b in reads if b.name not in self.EXCL]
        writes = tuple(writes) + tuple(excl)
        for b in reads:
            if b.last_write is not None:
                deps.append(b.last_write)
            b.readers.append(it)
        for b in writes:
            if b.last_write is not None:
                deps.append(b.last_write)
            deps.extend(b.readers)
            b.readers = []
            b.last_write = it
        seen = set()
        for d in deps:
            if d is it or d.idx in seen:
                continue
            seen.add(d.idx)
            if d.eng == eng:
                if d.amt == 1 and it.amt == 1 and eng == "tensor":
                    continue
                if d.amt == 1 and it.amt == 16:
                    continue
            it.deps.append(d)
        self.items.append(it)
        return it

    def barrier(self, bufs):
        fence = []
        fb = Buf("_fence")
        for e in ENGS:
            fence.append(self.add(e, lambda h: h.nop(), reads=(), writes=tuple(bufs) + (fb,),
                                  label=f"fence_{e}", dma=False))
        self.fence_items = fence
        self.fenced_engines = set()

    NPOOL = 10

    def emit_all(self, nc):
        dma_cnt = {}
        for it in self.items:
            if it.amt == 16:
                q = it.eng
                idx = dma_cnt.get(q, 0)
                dma_cnt[q] = idx + 1
                it.semname = f"dpool_{q}_{idx % self.NPOOL}"
                it.use = idx // self.NPOOL
                it.target = 16 * (it.use + 1)
            else:
                it.semname = f"sem_{it.eng}"
                it.use = None
        needed = set()
        for it in self.items:
            for d in it.deps:
                needed.add(d.idx)
        running = {}
        for it in self.items:
            if it.amt == 1:
                if it.idx in needed:
                    running[it.eng] = running.get(it.eng, 0) + 1
                    it.target = running[it.eng]
                else:
                    it.target = None
        engs = [e for e in ENGS if any(it.eng == e for it in self.items)]
        semnames = sorted(set(it.semname for it in self.items))
        sem_cms = {n: nc.semaphore(n) for n in semnames}
        sems = {n: cm.__enter__() for n, cm in sem_cms.items()}
        blk_cm = nc.Block()
        block = blk_cm.__enter__()
        by_eng = {e: [it for it in self.items if it.eng == e] for e in engs}

        def mk(eng_items):
            def body(h):
                seen = {}
                for it in eng_items:
                    wmax = {}
                    for d in it.deps:
                        assert d.idx < it.idx and d.target is not None, (d.label, it.label)
                        wmax[d.semname] = max(wmax.get(d.semname, 0), d.target)
                    if it.amt == 16 and it.use > 0:
                        v = 16 * it.use
                        if wmax.get(it.semname, 0) < v:
                            wmax[it.semname] = v
                    for sn in sorted(wmax):
                        if seen.get(sn, -1) >= wmax[sn]:
                            continue
                        h.wait_ge(sems[sn], wmax[sn])
                        seen[sn] = wmax[sn]
                    ins = it.emit(h)
                    if it.target is not None:
                        ins.then_inc(sems[it.semname], it.amt)
            return body

        dec = {"tensor": block.tensor, "vector": block.vector, "scalar": block.scalar,
               "gpsimd": block.gpsimd, "sync": block.sync}
        for e in engs:
            dec[e](mk(by_eng[e]))
        blk_cm.__exit__(None, None, None)
        for n in semnames:
            sem_cms[n].__exit__(None, None, None)


def fe(shape):
    n = 1
    for s in shape[1:]:
        n *= s
    return n


def outer_tables(S):
    j = np.arange(64)
    invf = (10000.0 ** (-(2.0 * j) / 128.0))[:, None]
    ang = (np.arange(S)[None, :] - 1) * invf
    c = np.cos(ang)
    s = np.sin(ang)
    c[:, 0] = 1.0
    s[:, 0] = 0.0
    return (np.concatenate([c, c], 0).astype(np.float32),
            np.concatenate([-s, s], 0).astype(np.float32))


def mla_tables(positions, nblk):
    j = np.arange(16)
    freqs = (100000.0 ** (-(2.0 * j) / 32.0))[:, None]
    ang = positions[None, :].astype(np.float64) * freqs
    c = np.cos(ang)
    s = np.sin(ang)
    blkA = np.concatenate([c, c], 0)
    blkB = np.concatenate([-s, s], 0)
    A = np.concatenate([blkA] * nblk, 0).astype(np.float32)
    Bm = np.concatenate([blkB] * nblk, 0).astype(np.float32)
    return A, Bm


class Builder:
    def __init__(self, S, debug=False):
        self.S = S
        self.debug = debug
        self.NQB = S // QB
        self.NKT = S // KT
        self.NSLOT = self.NQB // 2
        self.OWN_T = self.NSLOT * QB
        self.NOS = self.OWN_T // 128
        self.Ls = [8 * (s + 1) for s in range(self.NSLOT)]
        self.nc = bass.Bass("TRN2", target_bir_lowering=False)
        self.p = Prog()
        self._bufs = {}
        self._arena = []

    def T(self, emit, r=(), w=(), l=""):
        return self.p.add("tensor", emit, r, w, l)

    def V(self, emit, r=(), w=(), l=""):
        return self.p.add("vector", emit, r, w, l)

    def S_(self, emit, r=(), w=(), l=""):
        return self.p.add("scalar", emit, r, w, l)

    def G(self, emit, r=(), w=(), l=""):
        return self.p.add("gpsimd", emit, r, w, l)

    def DS(self, emit, r=(), w=(), l=""):
        return self.p.add("sync", emit, r, w, l, dma=True)

    def DG(self, emit, r=(), w=(), l=""):
        return self.p.add("gpsimd", emit, r, w, l, dma=True)

    def B(self, name):
        if name not in self._bufs:
            self._bufs[name] = Buf(name)
        return self._bufs[name]

    def sb(self, name, shape, dtype):
        cm = self.nc.sbuf_tensor(name, shape, dtype)
        t = cm.__enter__()
        self._arena.append((name, cm))
        return t

    def free_arena(self, names, bufnames=None):
        self.p.barrier(list(self._bufs.values()))
        keep = []
        for name, cm in reversed(self._arena):
            if name in names:
                cm.__exit__(None, None, None)
            else:
                keep.append((name, cm))
        self._arena = list(reversed(keep))

    def ap(self, tile, shape, part0, nparts, free_off, dims):
        return bass.AP(tile, part0 * fe(shape) + free_off, [[fe(shape), nparts]] + list(dims))

    def declare_io(self):
        nc = self.nc
        S = self.S
        d = {}
        def di(name, shape, dtype=F32):
            d[name] = nc.dram_tensor(name, shape, dtype, kind="ExternalInput").ap()
        di("x", [S, D]); di("wq", [OHD, D]); di("wk", [OHD, D]); di("wv", [OHD, D])
        di("wo", [D, OHD]); di("w_ln1", [D]); di("Wq", [D, D]); di("Wkva", [D, RLAT + KR])
        di("Wkvb", [RLAT, HEADS * 32]); di("Wo", [HEADS * 64, D]); di("w_ln2", [D])
        di("Wgate", [D, EE]); di("ebias", [EE]); di("Wg", [EE, D, II]); di("Wu", [EE, D, II])
        di("Wd", [EE, II, D]); di("sWg", [1, D, II]); di("sWu", [1, D, II]); di("sWd", [1, II, D])
        di("Aout", [128, S]); di("Bout", [128, S])
        di("Aq", [128, self.OWN_T]); di("Bq", [128, self.OWN_T])
        di("Ak", [32, S]); di("Bk", [32, S]); di("iden", [128, 128])
        di("orow", [1, self.NQB]); di("padrow", [1, sum(self.Ls)])
        di("masks", [128, 8, QB], BF16)
        self.inp = d
        self.out = nc.dram_tensor("out", [self.OWN_T, D], F32, kind="ExternalOutput").ap()
        k = "ExternalOutput" if self.debug else "Internal"
        self.x1_sp = nc.dram_tensor("x1_sp", [S, D], F32, kind=k).ap()
        self.xn1_sp = nc.dram_tensor("xn1_sp", [S, D], BF16, kind=k).ap()
        self.g_sp = nc.dram_tensor("g_sp", [self.OWN_T, D], F32, kind=k).ap()
        self.comb_dram = nc.dram_tensor("comb_dram", [EE, self.OWN_T], F32, kind=k).ap()
        self.xn2_sp = nc.dram_tensor("xn2_sp", [self.OWN_T, D], BF16, kind=k).ap()

    def alloc_persist(self):
        sb = self.sb
        S = self.S
        self.biasKsl = sb("biasKsl", [128, sum(self.Ls)], F32)
        self.wln1_c = sb("wln1_c", [128, NCH], F32)
        self.wln2_c = sb("wln2_c", [128, NCH], F32)
        self.iden_f = sb("iden_f", [128, 128], F32)
        self.ones_b = sb("ones_b", [128, 2], BF16)
        self.ones_f = sb("ones_f", [128, 128], F32)
        self.negh = sb("negh", [128, 1], BF16)
        self.econst = sb("econst", [128, 1], F32)
        self.ocol = sb("ocol", [128, self.NQB], F32)
        self.ebias_b = sb("ebias_b", [128, EE], F32)
        self.Wgate_s = sb("Wgate_s", [128, NCH, EE], BF16)
        self.vscr = sb("vscr", [128, D], F32)
        self.sqscr = sb("sqscr", [128, D], BF16)
        self.stc = sb("stc", [128, 24], F32)
        self.xto = sb("xto", [128, D], F32)
        self.xnb = sb("xnb", [128, 2, D], BF16)
        self.sqX = sb("sqX", [128, 2, 512], BF16)
        self.nkrow = sb("nkrow", [1, 512], F32)
        self.mxk = sb("mxk", [1, 8], F32)
        self.t1s = sb("t1s", [128, QB], F32)
        self.t2s = sb("t2s", [128, QB], F32)
        self.kvpack = sb("kvpack", [128, 4, S], BF16)
        self.kvt = sb("kvt", [128, self.NKT, HEADS * 33], BF16)
        self.krass = sb("krass", [128, S], BF16)
        self.PAB = self.nc.psum_tensor("PAB", [128, 1024], F32).__enter__()
        self.S4 = self.nc.psum_tensor("S4", [128, 2048], F32).__enter__()
        self.Y2 = self.nc.psum_tensor("Y2", [128, 1024], F32).__enter__()

    def alloc_outer(self):
        sb = self.sb
        self.vdum = sb("vdum", [128, D], BF16)
        S = self.S
        self.WkvaA = sb("WkvaA", [128, NCH, 192], BF16)
        self.WkvbT = sb("WkvbT", [128, 4, 128], BF16)
        self.Wkvb_n = sb("Wkvb_n", [128, HEADS * 32], BF16)
        self.wqX = sb("wqX", [128, NCH, 128], BF16)
        self.wqXs = sb("wqXs", [128, NCH, 128], BF16)
        self.wkX = sb("wkX", [128, NCH, 128], BF16)
        self.wkXs = sb("wkXs", [128, NCH, 128], BF16)
        self.wvT = sb("wvT", [128, NCH, 128], BF16)
        self.woT = sb("woT", [128, NCH, 128], BF16)
        self.qA = sb("qA", [128, S], BF16)
        self.kA = sb("kA", [128, S], BF16)
        self.vscr2 = sb("vscr2", [128, D], F32)
        self.biasG = sb("biasG", [128, 1], F32)
        self.vt = sb("vt", [128, self.NKT, 129], BF16)
        self.x0f = sb("x0f", [128, 2, NCH, 512], BF16)
        self.x0tb = sb("x0tb", [128, 2, D], BF16)
        self.x0tf = sb("x0tf", [128, 4, D], F32)
        self.AB_blk = sb("AB_blk", [128, 2, 512], F32)
        self.ABk_blk = sb("ABk_blk", [32, 2, 512], F32)
        self.wld = sb("wld", [128, D], F32)
        self.wld2 = sb("wld2", [128, 128], F32)
        self.wrow = sb("wrow", [8, 128], F32)
        self.lat = sb("lat", [128, 512], BF16)
        self.expO = sb("expO", [128, 4, 512], BF16)

    OUTER_NAMES = ["vdum", "WkvaA", "WkvbT", "Wkvb_n",
                   "wqX", "wqXs", "wkX", "wkXs", "wvT", "woT", "qA", "kA",
                   "vscr2", "biasG", "vt", "x0f", "x0tb", "x0tf", "AB_blk",
                   "ABk_blk", "wld", "wld2", "wrow", "lat", "expO"]
    OUTER_BUFS = ["wq0", "wq1", "wk0", "wk1", "wvT", "woT", "qA", "qB", "kA", "kB",
                  "vt", "biasKo", "x0f0", "x0f1", "x0tb0", "x0tb1", "x0tf0", "x0tf1",
                  "ABb", "ABk", "wld", "wld2", "wrow", "lat",
                  "psXq", "psXqs", "psXk", "psXks", "psv", "psQn", "psKc", "psKc2",
                  "psS0", "psS512", "psY1", "psY2", "psA", "exp0", "exp1",
                  "t1", "t2", "sqX", "nkr", "vscr", "sqscr", "stc", "xto", "xsc",
                  "xnb0", "xnb1", "xnsc", "x1sp", "xn1sp", "krass", "kvp", "kvt",
                  "biasKm", "ocol"]

    def alloc_mla(self):
        sb = self.sb
        self.WqA = sb("WqA", [128, NCH, HEADS, 96], BF16)
        self.WoE = sb("WoE", [128, 4, D], BF16)
        self.Qnope = sb("Qnope", [128, 4, QB], BF16)
        self.Qra = sb("Qra", [128, 8, QB], BF16)
        self.expT = sb("expT", [128, 2, 4, QB], BF16)
        self.yasm = sb("yasm", [128, 4, QB], BF16)
        self.rDr = sb("rDr", [1, 4, QB], F32)
        self.sel32 = sb("sel32", [128, 128], F32)
        self.rDc = sb("rDc", [128, QB], F32)
        self.Bbc4 = sb("Bbc4", [128, QB], F32)
        self.xn1f = sb("xn1f", [128, 2, 4, NCH, 128], BF16)
        self.x1s2 = sb("x1s2", [128, D], F32)
        self.x1su = sb("x1su", [128, D], F32)
        self.xns2 = sb("xns2", [128, D], BF16)
        self.xnsu = sb("xnsu", [128, 2, D], BF16)
        self.masks_s = sb("masks_s", [128, 8, QB], BF16)
        self.ABq_sl = sb("ABq_sl", [128, 2, QB], F32)
        self.wldM = self.x1su

    MLA_NAMES = ["WqA", "WoE", "Qnope", "Qra", "expT", "yasm", "rDr", "sel32",
                 "rDc", "Bbc4",
                 "xn1f", "x1s2", "x1su", "xns2", "xnsu", "masks_s", "ABq_sl",
                 "kvpack", "kvt", "krass"]
    MLA_BUFS = ["WqA", "WoE", "Qnope", "Qra", "ex0", "ex1", "ex2", "ex3", "yasm",
                "rD", "Bbc", "xn1f", "x1qb", "x1qb2", "xnqb", "xnqb2", "masks",
                "ABq", "wldM", "psQ0", "psQ1", "psS0", "psS512", "psS1024",
                "psS1536", "psYm0", "psYm1", "psA", "psKc", "gsp", "kvp", "kvt",
                "krass", "biasKsl", "t1", "t2", "sqX", "vscr", "sqscr", "stc",
                "xto", "xnb0", "xnb1", "xn2f"]

    def alloc_moe(self):
        sb = self.sb
        self.acc = sb("acc", [128, self.NOS, D], F32)
        self.Wg_s = sb("Wg_s", [128, 2, NCH, II], BF16)
        self.Wu_s = sb("Wu_s", [128, 2, NCH, II], BF16)
        self.Wd_s = sb("Wd_s", [128, 2, 4, D], BF16)
        self.comb_t = sb("comb_t", [128, self.NOS, 8], F32)
        self.comb_st = sb("comb_st", [8, 128], F32)
        self.comb_row = sb("comb_row", [1, 512], F32)
        self.gpr = sb("gpr", [128, 4, 8], F32)
        self.gcol = sb("gcol", [128, 8], F32)
        self.sg = sb("sg", [128, 4, 512], BF16)
        self.hu = sb("hu", [128, 4, 512], BF16)
        self.gu = sb("gu", [128, 4, 512], BF16)
        self.Be = sb("Be", [128, 512], F32)
        self.xn2f = sb("xn2f", [128, self.NOS, NCH, 128], BF16)
        self.xe = sb("xe", [128, 4, D], BF16)

    def phase_prep(self):
        B, T, V, S_, G, DS, DG = self.B, self.T, self.V, self.S_, self.G, self.DS, self.DG
        i = self.inp
        DS(lambda e: e.dma_start(self.iden_f[:], i["iden"][:]), w=[B("iden")])
        V(lambda e: e.memset(self.ones_b[:], 1.0), w=[B("ones")])
        V(lambda e: e.memset(self.ones_f[:], 1.0), w=[B("onesf")])
        V(lambda e: e.memset(self.negh[:], -0.5), w=[B("negh")])
        V(lambda e: e.memset(self.econst[:], EPS), w=[B("econst")])
        V(lambda e: e.memset(self.mxk[:], 0.0), w=[B("mxk")])
        V(lambda e: e.memset(
            self.ap(self.kvt, [128, self.NKT, HEADS * 33], 0, 128, 32,
                    [[33, HEADS * self.NKT]]), 1.0), w=[B("kvt")])
        V(lambda e: e.memset(
            self.ap(self.vt, [128, self.NKT, 129], 0, 128, 64,
                    [[129, self.NKT]]), 1.0), w=[B("vt")])
        V(lambda e: e.memset(self.krass[32:64, :], 0.0), w=[B("krass")])
        V(lambda e: e.memset(self.krass[32:33, :], 1.0), r=[B("krass")], w=[B("krass")])
        V(lambda e: e.memset(self.krass[96:128, :], 0.0), r=[B("krass")], w=[B("krass")])
        V(lambda e: e.memset(self.krass[96:97, :], 1.0), r=[B("krass")], w=[B("krass")])

        for nm, dst, bn in (("w_ln1", self.wln1_c, "w_ln1"), ("w_ln2", self.wln2_c, "w_ln2")):
            DS(lambda e, nm=nm: e.dma_start(self.wrow[:], i[nm].rearrange("(c p) -> c p", p=128)),
               r=[B("wrow")], w=[B("wrow")], l=f"ld{nm}")
            T(lambda e: e.transpose(self.PAB[0:128, 0:8], self.wrow[:], self.iden_f[0:8, 0:8]),
              r=[B("wrow"), B("iden"), B("pA0")], w=[B("pA0")], l=f"t{nm}")
            V(lambda e, dst=dst: e.tensor_copy(dst[:, 0:NCH], self.PAB[0:128, 0:8]),
              r=[B("pA0")], w=[B(bn)], l=f"ev{nm}")
        DS(lambda e: e.dma_start(self.wrow[0:1, 0:EE], i["ebias"][None, :]),
           r=[B("wrow")], w=[B("wrow")], l="ldeb")
        T(lambda e: e.matmul(self.PAB[0:128, 256:256 + EE], self.ones_f[0:1, :],
                             self.wrow[0:1, 0:EE], start=True, stop=True),
          r=[B("wrow"), B("onesf"), B("pA0")], w=[B("pA0")], l="mmEb")
        V(lambda e: e.tensor_copy(self.ebias_b[:], self.PAB[0:128, 256:256 + EE]),
          r=[B("pA0")], w=[B("ebias")], l="evEb")
        DS(lambda e: e.dma_start(self.wrow[0:1, 0:self.NQB], i["orow"][:]),
           r=[B("wrow")], w=[B("wrow")], l="ldor")
        T(lambda e: e.matmul(self.PAB[0:128, 256:256 + self.NQB], self.ones_f[0:1, :],
                             self.wrow[0:1, 0:self.NQB], start=True, stop=True),
          r=[B("wrow"), B("onesf"), B("pA0")], w=[B("pA0")], l="mmOr")
        V(lambda e: e.tensor_copy(self.ocol[:], self.PAB[0:128, 256:256 + self.NQB]),
          r=[B("pA0")], w=[B("ocol")], l="evOr")

        def outer_w(name, dstX, dstXs, tag):
            wview = i[name].rearrange("(j two) d -> j two d", two=2)
            for swap in (0, 1):
                dst = dstXs if swap else dstX
                DS(lambda e, swap=swap, wview=wview: e.dma_start(self.wld[0:64, :], wview[:, swap, :]),
                   r=[B("wld")], w=[B("wld")], l=f"{tag}a{swap}")
                DS(lambda e, swap=swap, wview=wview: e.dma_start(self.wld[64:128, :], wview[:, 1 - swap, :]),
                   r=[B("wld")], w=[B("wld")], l=f"{tag}b{swap}")
                for c in range(NCH):
                    T(lambda e, c=c: e.transpose(self.PAB[0:128, 0:128],
                                                 self.wld[:, c * 128:(c + 1) * 128], self.iden_f[:]),
                      r=[B("wld"), B("iden"), B("pA0")], w=[B("pA0")], l=f"t{tag}{swap}{c}")
                    V(lambda e, c=c, dst=dst, tag=tag, swap=swap: e.tensor_copy(
                        dst[:, c, :], self.PAB[0:128, 0:128]),
                      r=[B("pA0")], w=[B(f"{tag}{swap}")], l=f"e{tag}{swap}{c}")
        outer_w("wq", self.wqX, self.wqXs, "wq")
        outer_w("wk", self.wkX, self.wkXs, "wk")
        DS(lambda e: e.dma_start(self.wld[:, :], i["wv"][:]), r=[B("wld")], w=[B("wld")], l="ldwv")
        for c in range(NCH):
            T(lambda e, c=c: e.transpose(self.PAB[0:128, 0:128],
                                         self.wld[:, c * 128:(c + 1) * 128], self.iden_f[:]),
              r=[B("wld"), B("iden"), B("pA0")], w=[B("pA0")], l=f"twv{c}")
            V(lambda e, c=c: e.tensor_copy(self.wvT[:, c, :], self.PAB[0:128, 0:128]),
              r=[B("pA0")], w=[B("wvT")], l=f"evwv{c}")
        for c in range(NCH):
            DS(lambda e, c=c: e.dma_start(self.wld2[:, :], i["wo"][c * 128:(c + 1) * 128, :]),
               r=[B("wld2")], w=[B("wld2")], l=f"ldwo{c}")
            T(lambda e: e.transpose(self.PAB[0:128, 128:256], self.wld2[:, :], self.iden_f[:]),
              r=[B("wld2"), B("iden"), B("pA0")], w=[B("pA0")], l=f"two{c}")
            V(lambda e, c=c: e.tensor_copy(self.woT[:, c, :], self.PAB[0:128, 128:256]),
              r=[B("pA0")], w=[B("woT")], l=f"evwo{c}")
        for c in range(NCH):
            DS(lambda e, c=c: e.dma_start(self.wld[:, 0:RLAT + KR], i["Wkva"][c * 128:(c + 1) * 128, :]),
               r=[B("wld"), B("WkvaA")], w=[B("wld")], l=f"ldkva{c}")
            for (o_d, o_s, dims) in ((0, 0, [[1, 128]]), (128, 128, [[1, 2], [2, 16]]),
                                     (160, 128 + 33, [[2, 16]]), (176, 128 + 32, [[2, 16]])):
                V(lambda e, c=c, o_d=o_d, o_s=o_s, dims=dims: e.tensor_scalar(
                    self.WkvaA[:, c, o_d:o_d + _dims_n(dims)],
                    self.ap(self.wld, [128, D], 0, 128, o_s, dims),
                    self.wln1_c[:, c:c + 1], None, ALU.mult),
                  r=[B("wld"), B("w_ln1")], w=[B("WkvaA")], l=f"gkva{c}_{o_d}")
        DG(lambda e: e.dma_start(self.Wkvb_n[:], i["Wkvb"][:]), w=[B("Wkvb_n")], l="ldKvbN")
        DS(lambda e: e.dma_start(self.wld[:, 0:512], i["Wkvb"][:]),
           r=[B("wld"), B("WkvaA")], w=[B("wld")], l="ldKvbF")
        for c in range(4):
            T(lambda e, c=c: e.transpose(self.PAB[0:128, 0:128],
                                         self.wld[:, c * 128:(c + 1) * 128], self.iden_f[:]),
              r=[B("wld"), B("iden"), B("pA0")], w=[B("pA0")], l=f"tkvb{c}")
            V(lambda e, c=c: e.tensor_copy(self.WkvbT[:, c, :], self.PAB[0:128, 0:128]),
              r=[B("pA0")], w=[B("WkvbT")], l=f"ekvb{c}")
        DG(lambda e: e.dma_start(self.Wgate_s[:], i["Wgate"].rearrange("(c p) e -> p c e", p=128)),
           w=[B("Wgate")], l="ldWgate")

    def phase_outer(self):
        self.V(lambda e: e.memset(self.mxk[0:1, 4:5], 0.0),
               r=[self.B("mxk")], w=[self.B("mxk")], l="zmxm")
        B, T, V, S_, G, DS, DG = self.B, self.T, self.V, self.S_, self.G, self.DS, self.DG
        i = self.inp
        for blk in range(self.NQB):
            pp = blk % 2
            t0 = blk * QB
            DS(lambda e, t0=t0: e.dma_start(self.AB_blk[:, 0, :], i["Aout"][:, t0:t0 + QB]),
               r=[B("ABb")], w=[B("ABb")], l=f"ldA{blk}")
            DS(lambda e, t0=t0: e.dma_start(self.AB_blk[:, 1, :], i["Bout"][:, t0:t0 + QB]),
               r=[B("ABb")], w=[B("ABb")], l=f"ldB{blk}")
            for s_ in range(4):
                ts_ = t0 + s_ * 128
                DG(lambda e, ts_=ts_, pp=pp: e.dma_start(self.x0tb[:, pp, :], i["x"][ts_:ts_ + 128, :]),
                   r=[B(f"x0tb{pp}")], w=[B(f"x0tb{pp}")], l=f"ldx0{blk}_{s_}")
                DS(lambda e, pp=pp, s_=s_: e.dma_start_transpose(
                    self.ap(self.x0f, [128, 2, NCH, 512], 0, 128, pp * NCH * 512 + s_ * 128,
                            [[512, NCH], [1, 128]]),
                    self.x0tb[:, pp, :]),
                   r=[B(f"x0tb{pp}"), B(f"x0f{pp}")], w=[B(f"x0f{pp}")], l=f"tx0{blk}_{s_}")

            def proj(off, wgt, wbuf, tag):
                for c in range(NCH):
                    T(lambda e, c=c, off=off, wgt=wgt, pp=pp: e.matmul(
                        self.S4[0:128, off:off + 512], wgt[:, c, :],
                        self.ap(self.x0f, [128, 2, NCH, 512], 0, 128, pp * NCH * 512 + c * 512,
                                [[1, 512]]),
                        start=(c == 0), stop=(c == NCH - 1)),
                      r=[B(f"x0f{pp}"), B(wbuf)] + ([B(tag)] if c == 0 else []),
                      w=[B(tag)], l=f"{tag}{blk}_{c}")
            proj(0, self.wqX, "wq0", "pS0")
            proj(512, self.wqXs, "wq1", "pS1")
            proj(1024, self.wkX, "wk0", "pS2")
            proj(1536, self.wkXs, "wk1", "pS3")
            for s_ in range(4):
                for c in range(NCH):
                    T(lambda e, s_=s_, c=c, pp=pp: e.matmul(
                        self.Y2[0:128, s_ * 128:(s_ + 1) * 128],
                        self.ap(self.x0f, [128, 2, NCH, 512], 0, 128,
                                pp * NCH * 512 + c * 512 + s_ * 128, [[1, 128]]),
                        self.wvT[:, c, :], start=(c == 0), stop=(c == NCH - 1)),
                      r=[B(f"x0f{pp}"), B("wvT")] + ([B("pY0")] if c == 0 else []),
                      w=[B("pY0")], l=f"v{blk}_{s_}_{c}")
                V(lambda e, s_=s_, blk=blk: e.tensor_copy(
                    self.ap(self.vt, [128, self.NKT, 129], 0, 128, (blk * 4 + s_) * 129,
                            [[65, 2], [1, 64]]),
                    self.Y2[0:128, s_ * 128:(s_ + 1) * 128]),
                  r=[B("pY0"), B("vt")], w=[B("vt")], l=f"ev{blk}_{s_}")

            def rope(psA, psAs, dstA, dstB, bA, bAs, tag):
                V(lambda e, psA=psA: e.tensor_tensor(
                    self.t1s[:, :], self.S4[0:128, psA:psA + 512], self.AB_blk[:, 0, :],
                    ALU.mult),
                  r=[B(bA), B("ABb"), B("t1")], w=[B("t1")], l=f"r1{tag}{blk}")
                V(lambda e, psAs=psAs: e.tensor_tensor(
                    self.t2s[:, :], self.S4[0:128, psAs:psAs + 512], self.AB_blk[:, 1, :],
                    ALU.mult),
                  r=[B(bAs), B("ABb"), B("t2")], w=[B("t2")], l=f"r2{tag}{blk}")
                V(lambda e, dstA=dstA, t0=t0, tag=tag: e.tensor_tensor(
                    dstA[0:64, t0:t0 + 512], self.t1s[0:64, :], self.t2s[0:64, :], ALU.add),
                  r=[B("t1"), B("t2"), B(f"{tag}A")], w=[B(f"{tag}A")], l=f"r3{tag}{blk}")
                V(lambda e, dstA=dstA, t0=t0, tag=tag: e.tensor_tensor(
                    dstA[64:128, t0:t0 + 512], self.t1s[64:128, :], self.t2s[64:128, :], ALU.add),
                  r=[B("t1"), B("t2"), B(f"{tag}A")], w=[B(f"{tag}A")], l=f"r4{tag}{blk}")
            rope(0, 512, self.qA, None, "pS0", "pS1", "q")
            rope(1024, 1536, self.kA, None, "pS2", "pS3", "k")

            S_(lambda e: e.activation(self.sqX[:, 0, :], self.S4[0:128, 0:512], AF.Square),
               r=[B("pS0"), B("sqX")], w=[B("sqX")], l=f"sqQ{blk}")
            T(lambda e: e.matmul(self.PAB[0:1, 0:512], self.ones_b[:, 0:1], self.sqX[:, 0, :],
                                 start=True, stop=True),
              r=[B("sqX"), B("ones"), B("pA0")], w=[B("pA0")], l=f"mmQn{blk}")
            V(lambda e: e.reduce_max(self.mxk[0:1, 6:7], self.PAB[0:1, 0:512], axis=AX.X),
              r=[B("pA0"), B("mxk")], w=[B("mxk")], l=f"qmx{blk}")
            V(lambda e: e.tensor_tensor(self.mxk[0:1, 3:4], self.mxk[0:1, 3:4],
                                        self.mxk[0:1, 6:7], ALU.max),
              r=[B("mxk")], w=[B("mxk")], l=f"qmr{blk}")
            S_(lambda e: e.activation(self.sqX[:, 0, :], self.S4[0:128, 1024:1536], AF.Square),
               r=[B("pS2"), B("sqX")], w=[B("sqX")], l=f"sqK{blk}")
            T(lambda e: e.matmul(self.PAB[0:1, 0:512], self.ones_b[:, 0:1], self.sqX[:, 0, :],
                                 start=True, stop=True),
              r=[B("sqX"), B("ones"), B("pA0")], w=[B("pA0")], l=f"mmKn{blk}")
            V(lambda e: e.reduce_max(self.mxk[0:1, 1:2], self.PAB[0:1, 0:512], axis=AX.X),
              r=[B("pA0"), B("mxk")], w=[B("mxk")], l=f"kmx{blk}")
            V(lambda e: e.tensor_tensor(self.mxk[0:1, 0:1], self.mxk[0:1, 0:1],
                                        self.mxk[0:1, 1:2], ALU.max),
              r=[B("mxk")], w=[B("mxk")], l=f"kmr{blk}")

        V(lambda e: e.tensor_tensor(self.mxk[0:1, 2:3], self.mxk[0:1, 0:1],
                                    self.mxk[0:1, 3:4], ALU.add),
          r=[B("mxk")], w=[B("mxk")], l="gb1")
        V(lambda e: e.tensor_scalar(self.mxk[0:1, 2:3], self.mxk[0:1, 2:3],
                                    -0.5 / math.sqrt(OHD), None, ALU.mult),
          r=[B("mxk")], w=[B("mxk")], l="gb2")
        T(lambda e: e.matmul(self.PAB[0:128, 512:513], self.ones_f[0:1, :],
                             self.mxk[0:1, 2:3], start=True, stop=True),
          r=[B("mxk"), B("onesf"), B("pA1")], w=[B("pA1")], l="gb3")
        V(lambda e: e.tensor_copy(self.biasG[:, :], self.PAB[0:128, 512:513]),
          r=[B("pA1"), B("bG")], w=[B("bG")], l="gb4")

        for qb in range(self.NQB):
            t0 = qb * QB
            def oav(kp2):
                for kt in (2 * kp2, 2 * kp2 + 1):
                    eb = f"expO{kt % 4}"
                    T(lambda e, kt=kt: e.matmul(
                        self.Y2[0:65, 0:512],
                        self.ap(self.vt, [128, self.NKT, 129], 0, 128, kt * 129, [[1, 65]]),
                        self.expO[:, kt % 4, :], start=(kt == 0), stop=(kt == self.NKT - 1)),
                      r=[B(eb), B("vt")] + ([B("pY0")] if kt == 0 else []),
                      w=[B("pY0")], l=f"a1_{qb}_{kt}")
                    T(lambda e, kt=kt: e.matmul(
                        self.Y2[0:64, 512:1024],
                        self.ap(self.vt, [128, self.NKT, 129], 0, 128, kt * 129 + 65, [[1, 64]]),
                        self.expO[:, kt % 4, :], start=(kt == 0), stop=(kt == self.NKT - 1)),
                      r=[B(eb), B("vt")] + ([B("pY1")] if kt == 0 else []),
                      w=[B("pY1")], l=f"a2_{qb}_{kt}")

            for kp in range(self.NKT // 2):
                for kt in (2 * kp, 2 * kp + 1):
                    so = 512 * (kt % 4)
                    pbank = f"pS{kt % 4}"
                    T(lambda e, kt=kt, t0=t0, so=so: e.matmul(
                        self.S4[0:128, so:so + 512], self.kA[:, kt * KT:(kt + 1) * KT],
                        self.qA[:, t0:t0 + 512], start=True, stop=True),
                      r=[B("qA"), B("kA"), B(pbank)], w=[B(pbank)], l=f"s1_{qb}_{kt}")
                b0 = (2 * kp) % 4
                S_(lambda e, b0=b0: e.activation(
                    self.ap(self.expO, [128, 4, 512], 0, 128, b0 * 512, [[1, 1024]]),
                    self.S4[0:128, b0 * 512:(b0 + 2) * 512], AF.Exp,
                    bias=self.biasG[:, :], scale=1.0 / math.sqrt(OHD)),
                   r=[B(f"pS{b0}"), B(f"pS{b0 + 1}"), B("bG"),
                      B(f"expO{b0}"), B(f"expO{b0 + 1}")],
                   w=[B(f"expO{b0}"), B(f"expO{b0 + 1}")], l=f"e_{qb}_{kp}")
                if kp > 0:
                    oav(kp - 1)
            oav(self.NKT // 2 - 1)
            V(lambda e: e.tensor_copy(self.sqX[0:64, 0, :], self.Y2[0:64, 0:512]),
              r=[B("pY0"), B("sqX")], w=[B("sqX")], l=f"ey1_{qb}")
            V(lambda e: e.tensor_copy(self.sqX[64:128, 0, :], self.Y2[0:64, 512:1024]),
              r=[B("pY1"), B("sqX")], w=[B("sqX")], l=f"ey2_{qb}")
            V(lambda e: e.tensor_copy(self.nkrow[:], self.Y2[64:65, 0:512]),
              r=[B("pY0"), B("nkr")], w=[B("nkr")], l=f"rD_{qb}")
            for s_ in range(4):
                T(lambda e, s_=s_: e.matmul(
                    self.PAB[0:128, 512 + s_:513 + s_],
                    self.nkrow[0:1, s_ * 128:(s_ + 1) * 128], self.ones_f[0:1, 0:1],
                    start=True, stop=True),
                  r=[B("nkr"), B("onesf")] + ([B("pA1")] if s_ == 0 else []),
                  w=[B("pA1")], l=f"rdc_{qb}_{s_}")
            V(lambda e: e.reciprocal(self.stc[:, 0:4], self.PAB[0:128, 512:516]),
              r=[B("pA1"), B("stc")], w=[B("stc")], l=f"erdc_{qb}")

            if qb == 0:
                for g0 in (0, 1):
                    DS(lambda e, g0=g0: e.dma_start(self.x0tf[:, g0 % 4, :],
                                                    i["x"][g0 * 128:(g0 + 1) * 128, :]),
                       r=[B(f"x0tf{g0 % 4}")], w=[B(f"x0tf{g0 % 4}")], l=f"ldxf{g0}")
            for s_ in range(4):
                sub = qb * 4 + s_
                xp = sub % 4
                gnx = sub + 2
                if gnx < self.NQB * 4:
                    DS(lambda e, gnx=gnx: e.dma_start(self.x0tf[:, gnx % 4, :],
                                                      i["x"][gnx * 128:(gnx + 1) * 128, :]),
                       r=[B(f"x0tf{gnx % 4}")], w=[B(f"x0tf{gnx % 4}")], l=f"ldxf{gnx}")
                wps = self.PAB
                wb0 = "pA0"
                wb1 = "pA1"
                for half in range(2):
                    T(lambda e, s_=s_, half=half, wps=wps: e.matmul(
                        wps[0:128, half * 512:(half + 1) * 512],
                        self.sqX[:, 0, s_ * 128:(s_ + 1) * 128],
                        self.ap(self.woT, [128, NCH, 128], 0, 128, half * 512, [[1, 512]]),
                        start=True, stop=True),
                      r=[B("sqX"), B("woT"), B(wb0 if half == 0 else wb1)],
                      w=[B(wb0 if half == 0 else wb1)], l=f"wo_{sub}_{half}")
                xp2 = s_ % 2
                vsx = self.vscr if xp2 == 0 else self.vscr2
                vbx = "vscr" if xp2 == 0 else "vscr2"
                ca = 8 if xp2 == 0 else 16
                V(lambda e, s_=s_, wps=wps, vsx=vsx, ca=ca, xp=xp: e.scalar_tensor_tensor(
                    vsx[:, :], wps[0:128, 0:1024], self.stc[:, s_:s_ + 1],
                    self.x0tf[:, xp, :], ALU.mult, ALU.add,
                    accum_out=self.stc[:, ca:ca + 1]),
                  r=[B(wb0), B(wb1), B("stc"), B(f"x0tf{xp}"), B(vbx), B(f"stc{xp2}")],
                  w=[B(vbx), B(f"stc{xp2}"), B(f"x0tf{xp}")], l=f"vres_{sub}")
                self._ln1(qb, sub)
            self._mla_kv(qb)

    def _ln1(self, qb, sub):
        B, T, V, S_, DS, DG = self.B, self.T, self.V, self.S_, self.DS, self.DG
        st = self.stc
        xp = sub % 2
        cb_ = 8 if xp == 0 else 16
        sb_ = f"stc{xp}"
        vs = self.vscr if xp == 0 else self.vscr2
        vb = "vscr" if xp == 0 else "vscr2"
        c0, c1, c2, c3, c4, c5 = cb_, cb_ + 1, cb_ + 2, cb_ + 3, cb_ + 4, cb_ + 5
        c6, c7 = cb_ + 6, cb_ + 7
        V(lambda e: e.scalar_tensor_tensor(self.vdum[:, :], vs[:, :], 1.0,
                                           vs[:, :], ALU.mult, ALU.mult,
                                           accum_out=st[:, c1:c1 + 1]),
          r=[B(vb), B("vdum"), B(sb_)], w=[B("vdum"), B(sb_)], l=f"v2_{sub}")
        V(lambda e: e.tensor_scalar(st[:, c2:c2 + 1], st[:, c0:c0 + 1], 1.0 / D, None, ALU.mult),
          r=[B(sb_)], w=[B(sb_)], l=f"mu_{sub}")
        V(lambda e: e.tensor_tensor(st[:, c3:c3 + 1], st[:, c2:c2 + 1], st[:, c2:c2 + 1], ALU.mult),
          r=[B(sb_)], w=[B(sb_)], l=f"mu2_{sub}")
        V(lambda e: e.scalar_tensor_tensor(st[:, c3:c3 + 1], st[:, c1:c1 + 1], 1.0 / D,
                                           st[:, c3:c3 + 1], ALU.mult, ALU.subtract),
          r=[B(sb_)], w=[B(sb_)], l=f"var_{sub}")
        S_(lambda e: e.activation(st[:, c4:c4 + 1], st[:, c3:c3 + 1], AF.Ln, bias=EPS),
           r=[B(sb_), B("econst")], w=[B(sb_)], l=f"ln_{sub}")
        S_(lambda e: e.activation(st[:, c4:c4 + 1], st[:, c4:c4 + 1], AF.Exp, scale=-0.5),
           r=[B(sb_)], w=[B(sb_)], l=f"rstd_{sub}")
        V(lambda e: e.tensor_scalar(st[:, c5:c5 + 1], st[:, c3:c3 + 1], EPS, None, ALU.add),
          r=[B(sb_)], w=[B(sb_)], l=f"ve_{sub}")
        V(lambda e: e.reciprocal(st[:, c5:c5 + 1], st[:, c5:c5 + 1]),
          r=[B(sb_)], w=[B(sb_)], l=f"vr_{sub}")
        V(lambda e: e.tensor_tensor(st[:, c5:c5 + 1], st[:, c3:c3 + 1], st[:, c5:c5 + 1], ALU.mult),
          r=[B(sb_)], w=[B(sb_)], l=f"m2_{sub}")
        S_(lambda e: e.activation(st[:, c6:c6 + 1], st[:, c5:c5 + 1], AF.Ln, bias=EPS),
           r=[B(sb_), B("econst")], w=[B(sb_)], l=f"lnr_{sub}")
        S_(lambda e: e.activation(st[:, c6:c6 + 1], st[:, c6:c6 + 1], AF.Exp, scale=-0.5),
           r=[B(sb_)], w=[B(sb_)], l=f"rr_{sub}")
        V(lambda e: e.scalar_tensor_tensor(st[:, c7:c7 + 1], st[:, c2:c2 + 1], -1.0,
                                           st[:, c4:c4 + 1], ALU.mult, ALU.mult),
          r=[B(sb_)], w=[B(sb_)], l=f"nm_{sub}")
        S_(lambda e: e.activation(self.xto[:, :], vs[:, :], AF.Identity,
                                  bias=st[:, c7:c7 + 1], scale=st[:, c4:c4 + 1]),
           r=[B(vb), B(sb_), B("xto")], w=[B("xto")], l=f"x1_{sub}")
        V(lambda e, xp=xp: e.tensor_scalar(self.xnb[:, xp, :], self.xto[:, :],
                                           st[:, c6:c6 + 1], None, ALU.mult),
          r=[B("xto"), B(sb_), B(f"xnb{xp}")], w=[B(f"xnb{xp}")], l=f"xn1_{sub}")
        V(lambda e, qb=qb, vs=vs: e.tensor_scalar(vs[:, :], self.xto[:, :],
                                                  self.ocol[:, qb:qb + 1], None, ALU.mult),
          r=[B("xto"), B("ocol"), B(vb)], w=[B(vb)], l=f"scx_{sub}")
        DG(lambda e, sub=sub, vs=vs: e.dma_start(self.x1_sp[sub * 128:(sub + 1) * 128, :], vs[:, :]),
           r=[B(vb)], w=[B("x1sp")], l=f"spx_{sub}")
        V(lambda e, qb=qb, xp=xp: e.tensor_scalar(self.sqscr[:, 0:D], self.xnb[:, xp, :],
                                                  self.ocol[:, qb:qb + 1], None, ALU.mult),
          r=[B(f"xnb{xp}"), B("ocol"), B("sqscr")], w=[B("sqscr")], l=f"scn_{sub}")
        DG(lambda e, sub=sub: e.dma_start(self.xn1_sp[sub * 128:(sub + 1) * 128, :],
                                          self.sqscr[:, 0:D]),
           r=[B("sqscr")], w=[B("xn1sp")], l=f"spn_{sub}")
        s_ = sub % 4
        DS(lambda e, s_=s_, xp=xp: e.dma_start_transpose(
            self.ap(self.xn1f_oa, [128, 4, NCH, 128], 0, 128, s_ * NCH * 128,
                    [[128, NCH], [1, 128]]),
            self.xnb[:, xp, :]),
           r=[B(f"xnb{xp}"), B("xn1f")], w=[B("xn1f")], l=f"txn_{sub}")

    def _mla_kv(self, qb):
        B, T, V, S_, DS = self.B, self.T, self.V, self.S_, self.DS
        i = self.inp
        t0 = qb * QB
        pp = qb % 2
        DS(lambda e, t0=t0: e.dma_start(self.ABk_blk[:, 0, :], i["Ak"][:, t0:t0 + QB]),
           r=[B("ABk")], w=[B("ABk")], l=f"ldAk{qb}")
        DS(lambda e, t0=t0: e.dma_start(self.ABk_blk[:, 1, :], i["Bk"][:, t0:t0 + QB]),
           r=[B("ABk")], w=[B("ABk")], l=f"ldBk{qb}")
        for c in range(NCH):
            rhs = lambda c=c: self.ap(self.xn1f_oa, [128, 4, NCH, 128], 0, 128, c * 128,
                                      [[NCH * 128, 4], [1, 128]])
            T(lambda e, c=c, rhs=rhs: e.matmul(
                self.PAB[0:128, 0:512],
                self.ap(self.WkvaA, [128, NCH, 192], 0, 128, c * 192, [[1, 128]]),
                rhs(), start=(c == 0), stop=(c == NCH - 1)),
              r=[B("xn1f"), B("WkvaA")] + ([B("pA0")] if c == 0 else []),
              w=[B("pA0")], l=f"lat{qb}_{c}")
            T(lambda e, c=c, rhs=rhs: e.matmul(
                self.PAB[0:32, 512:1024],
                self.ap(self.WkvaA, [128, NCH, 192], 0, 128, c * 192 + 128, [[1, 32]]),
                rhs(), start=(c == 0), stop=(c == NCH - 1)),
              r=[B("xn1f"), B("WkvaA")] + ([B("pA1")] if c == 0 else []),
              w=[B("pA1")], l=f"krX{qb}_{c}")
        for c in range(NCH):
            rhs = lambda c=c: self.ap(self.xn1f_oa, [128, 4, NCH, 128], 0, 128, c * 128,
                                      [[NCH * 128, 4], [1, 128]])
            T(lambda e, c=c, rhs=rhs: e.matmul(
                self.PAB[64:96, 512:1024],
                self.ap(self.WkvaA, [128, NCH, 192], 0, 128, c * 192 + 160, [[1, 32]]),
                rhs(), start=(c == 0), stop=(c == NCH - 1)),
              r=[B("xn1f"), B("WkvaA")], w=[B("pA1")], l=f"krXs{qb}_{c}")
        V(lambda e: e.tensor_copy(self.lat[:, :], self.PAB[0:128, 0:512]),
          r=[B("pA0"), B("lat")], w=[B("lat")], l=f"elat{qb}")
        V(lambda e: e.tensor_tensor(self.t1s[0:32, :], self.PAB[0:32, 512:1024],
                                    self.ABk_blk[:, 0, :], ALU.mult),
          r=[B("pA1"), B("ABk"), B("t1")], w=[B("t1")], l=f"kr1{qb}")
        V(lambda e: e.tensor_tensor(self.t2s[0:32, :], self.PAB[64:96, 512:1024],
                                    self.ABk_blk[:, 1, :], ALU.mult),
          r=[B("pA1"), B("ABk"), B("t2")], w=[B("t2")], l=f"kr2{qb}")
        V(lambda e, t0=t0: e.tensor_tensor(self.krass[0:32, t0:t0 + 512],
                                           self.t1s[0:32, :], self.t2s[0:32, :], ALU.add),
          r=[B("t1"), B("t2"), B("krass")], w=[B("krass")], l=f"kr3{qb}")
        V(lambda e, t0=t0: e.tensor_copy(self.krass[64:96, t0:t0 + 512],
                                         self.krass[0:32, t0:t0 + 512]),
          r=[B("krass")], w=[B("krass")], l=f"kr4{qb}")
        for c in range(4):
            yo = 512 * (c % 2)
            yb = f"pY{c % 2}"
            T(lambda e, c=c, yo=yo: e.matmul(self.Y2[0:128, yo:yo + 512],
                                             self.WkvbT[:, c, :],
                                             self.lat[:, :], start=True, stop=True),
              r=[B("lat"), B("WkvbT"), B(yb)], w=[B(yb)], l=f"kvf{qb}_{c}")
            V(lambda e, c=c, t0=t0, yo=yo: e.tensor_copy(self.kvpack[:, c, t0:t0 + 512],
                                                         self.Y2[0:128, yo:yo + 512]),
              r=[B(yb), B("kvp")], w=[B("kvp")], l=f"ekvf{qb}_{c}")
        for s_ in range(4):
            kt = qb * 4 + s_
            yo = 512 * (s_ % 2)
            yb = f"pY{s_ % 2}"
            T(lambda e, s_=s_, yo=yo: e.matmul(self.Y2[0:128, yo:yo + 512],
                                               self.lat[:, s_ * 128:(s_ + 1) * 128],
                                               self.Wkvb_n[:, :], start=True, stop=True),
              r=[B("lat"), B("Wkvb_n"), B(yb)], w=[B(yb)], l=f"kvt{kt}")
            V(lambda e, kt=kt, yo=yo: e.tensor_copy(
                self.ap(self.kvt, [128, self.NKT, HEADS * 33], 0, 128, kt * HEADS * 33,
                        [[33, 16], [1, 32]]),
                self.Y2[0:128, yo:yo + 512]),
              r=[B(yb), B("kvt")], w=[B("kvt")], l=f"ekvt{kt}")


        for c in range(4):
            S_(lambda e, c=c, t0=t0: e.activation(
                self.sqX[:, 0, :], self.kvpack[:, c, t0:t0 + 512], AF.Square),
               r=[B("kvp"), B("sqX0")], w=[B("sqX0")], l=f"mks{qb}_{c}")
            T(lambda e, c=c: e.matmul(self.S4[0:1, 0:512], self.ones_b[:, 0:1],
                                      self.sqX[:, 0, :], start=(c == 0), stop=False),
              r=[B("sqX0"), B("ones")] + ([B("pS0")] if c == 0 else []),
              w=[B("pS0")], l=f"mkm{qb}_{c}")
        S_(lambda e, t0=t0: e.activation(self.sqX[0:32, 0, :],
                                         self.krass[0:32, t0:t0 + 512], AF.Square),
           r=[B("krass"), B("sqX0")], w=[B("sqX0")], l=f"mksr{qb}")
        T(lambda e: e.matmul(self.S4[0:1, 0:512], self.ones_b[0:32, 0:1],
                             self.sqX[0:32, 0, :], start=False, stop=True),
          r=[B("sqX0"), B("ones")], w=[B("pS0")], l=f"mkmr{qb}")
        V(lambda e: e.reduce_max(self.mxk[0:1, 5:6], self.S4[0:1, 0:512], axis=AX.X),
          r=[B("pS0"), B("mxk")], w=[B("mxk")], l=f"mkx{qb}")
        V(lambda e: e.tensor_tensor(self.mxk[0:1, 4:5], self.mxk[0:1, 4:5],
                                    self.mxk[0:1, 5:6], ALU.max),
          r=[B("mxk")], w=[B("mxk")], l=f"mkr2{qb}")

    def phase_mla(self, mstop=None):
        B, T, V, S_, G, DS, DG = self.B, self.T, self.V, self.S_, self.G, self.DS, self.DG
        i = self.inp
        for c in range(NCH):
            wt = self.x1su if c % 2 == 0 else self.x1s2
            wtb = "x1su" if c % 2 == 0 else "x1s2"
            DS(lambda e, c=c, wt=wt: e.dma_start(wt[:, :], i["Wq"][c * 128:(c + 1) * 128, :]),
               r=[B(wtb), B("WqA")], w=[B(wtb)], l=f"ldWq{c}")
            for (o_d, o_s, dims, n_) in (
                    (0, 32, [[64, HEADS], [1, 2], [2, 16]], 32),
                    (32, 0, [[64, HEADS], [1, 32]], 32),
                    (64, 33, [[64, HEADS], [2, 16]], 16),
                    (80, 32, [[64, HEADS], [2, 16]], 16)):
                V(lambda e, c=c, o_d=o_d, o_s=o_s, dims=dims, n_=n_, wt=wt: e.tensor_scalar(
                    self.ap(self.WqA, [128, NCH, HEADS, 96], 0, 128,
                            c * HEADS * 96 + o_d, [[96, HEADS], [1, n_]]),
                    self.ap(wt, [128, D], 0, 128, o_s, dims),
                    self.wln1_c[:, c:c + 1], None, ALU.mult),
                  r=[B(wtb), B("w_ln1")], w=[B("WqA")], l=f"g{c}_{o_d}")
        for j in range(4):
            DG(lambda e, j=j: e.dma_start(
                self.WoE[:, j, :],
                bass.AP(i["Wo"].tensor, (256 * j) * D, [[64 * D, 4], [D, 32], [1, D]])),
               w=[B("WoE")], l=f"ldWoE{j}")
        DS(lambda e: e.dma_start(self.masks_s[:], i["masks"][:]), w=[B("masks")], l="ldmask")
        DS(lambda e: e.dma_start(self.nkrow[0:1, 0:sum(self.Ls)], i["padrow"][:]),
           r=[B("nkr")], w=[B("nkr")], l="ldpad")
        T(lambda e: e.matmul(self.PAB[0:128, 0:sum(self.Ls)], self.ones_f[0:1, :],
                             self.nkrow[0:1, 0:sum(self.Ls)], start=True, stop=True),
          r=[B("nkr"), B("onesf"), B("pA0")], w=[B("pA0")], l="mmpad")
        V(lambda e: e.tensor_copy(self.biasKsl[:, :], self.PAB[0:128, 0:sum(self.Ls)]),
          r=[B("pA0"), B("biasKsl")], w=[B("biasKsl")], l="bsl")
        V(lambda e: e.memset(self.Qra[:, :, :], 0.0), r=[B("Qra")], w=[B("Qra")], l="zQra")
        V(lambda e: e.memset(self.sel32[:, :], 0.0), r=[B("sel32")], w=[B("sel32")], l="zsel")
        for m_ in range(4):
            V(lambda e, m_=m_: e.memset(
                self.sel32[32 * m_:32 * m_ + 1, 32 * m_:32 * m_ + 32], 1.0),
              r=[B("sel32")], w=[B("sel32")], l=f"sel{m_}")
        V(lambda e: e.memset(self.rDc[:, :], 0.0), r=[B("rDc")], w=[B("rDc")], l="zrdc")
        if mstop == "prep":
            return
        def emit_reloads(sn):
            if sn >= self.NSLOT:
                return
            pb = sn % 2
            q0n = sn * QB
            for s_ in range(4):
                xp = s_ % 2
                t0_ = 2 * sn * QB + s_ * 128
                t1_ = (2 * sn + 1) * QB + s_ * 128
                DG(lambda e, t0_=t0_, xp=xp: e.dma_start(
                    self.xnsu[:, xp, :], self.xn1_sp[t0_:t0_ + 128, :]),
                   r=[B("xn1sp"), B(f"xnsu{xp}")], w=[B(f"xnsu{xp}")], l=f"lxnA{sn}_{s_}")
                DG(lambda e, t1_=t1_: e.dma_start(
                    self.xns2[:, :], self.xn1_sp[t1_:t1_ + 128, :]),
                   r=[B("xn1sp"), B("xns2")], w=[B("xns2")], l=f"lxnB{sn}_{s_}")
                V(lambda e, xp=xp: e.tensor_tensor(self.xnsu[:, xp, :], self.xnsu[:, xp, :],
                                                   self.xns2[:, :], ALU.add),
                  r=[B("xns2"), B(f"xnsu{xp}")], w=[B(f"xnsu{xp}")], l=f"sxn{sn}_{s_}")
                DS(lambda e, s_=s_, xp=xp, pb=pb: e.dma_start_transpose(
                    self.ap(self.xn1f, [128, 2, 4, NCH, 128], 0, 128,
                            (pb * 4 + s_) * NCH * 128, [[128, NCH], [1, 128]]),
                    self.xnsu[:, xp, :]),
                   r=[B(f"xnsu{xp}"), B(f"xn1fM{pb}")], w=[B(f"xn1fM{pb}")],
                   l=f"txq{sn}_{s_}")
            DS(lambda e, q0n=q0n: e.dma_start(self.ABq_sl[:, 0, :], i["Aq"][:, q0n:q0n + QB]),
               r=[B("ABq")], w=[B("ABq")], l=f"ldAq{sn}")
            DS(lambda e, q0n=q0n: e.dma_start(self.ABq_sl[:, 1, :], i["Bq"][:, q0n:q0n + QB]),
               r=[B("ABq")], w=[B("ABq")], l=f"ldBq{sn}")

        emit_reloads(0)
        sl_off = 0
        for s in range(self.NSLOT):
            if mstop in ("rel1", "q1", "att1", "slot1") and s > 0:
                break
            L = self.Ls[s]
            pb = s % 2

            if mstop == "rel1":
                continue
            import os
            qheads = int(os.environ.get("K_QHEADS", HEADS))
            qskip = os.environ.get("K_QSKIP", "")
            aug_q = []

            def flush_aug():
                while aug_q:
                    fn = aug_q.pop(0)
                    fn()

            for h in range(qheads):
                hr = h % 4
                qps = (self.PAB, self.PAB, self.S4, self.S4)[hr]
                po = (0, 512, 512, 1024)[hr]
                qbn = ("pA0", "pA1", "pS1", "pS2")[hr]
                hp = h % 2
                for c in range(NCH):
                    T(lambda e, c=c, h=h, po=po, qps=qps, pb=pb: e.matmul(
                        qps[0:96, po:po + 512],
                        self.ap(self.WqA, [128, NCH, HEADS, 96], 0, 128, (c * HEADS + h) * 96,
                                [[1, 96]]),
                        self.ap(self.xn1f, [128, 2, 4, NCH, 128], 0, 128,
                                pb * 4 * NCH * 128 + c * 128,
                                [[NCH * 128, 4], [1, 128]]),
                        start=(c == 0), stop=(c == NCH - 1)),
                      r=[B(f"xn1fM{pb}"), B("WqA")] + ([B(qbn)] if c == 0 else []),
                      w=[B(qbn)], l=f"q{s}_{h}_{c}")
                m = h % 4
                j = h // 4
                if "n" not in qskip:
                    V(lambda e, m=m, j=j, po=po, qps=qps: e.tensor_copy(
                        self.Qnope[32 * m:32 * m + 32, j, :], qps[32:64, po:po + 512]),
                      r=[B(qbn), B("Qnope")], w=[B("Qnope")], l=f"eqn{s}_{h}")
                rb = 64 * (h % 2)
                cb = h // 2
                if "r" not in qskip:
                    V(lambda e, po=po, qps=qps: e.tensor_tensor(
                        self.t1s[0:32, :], qps[0:32, po:po + 512], self.ABq_sl[0:32, 0, :],
                        ALU.mult),
                      r=[B(qbn), B("ABq"), B("t1")], w=[B("t1")], l=f"qr1{s}_{h}")
                    V(lambda e, po=po, qps=qps: e.tensor_tensor(
                        self.t2s[0:32, :], qps[64:96, po:po + 512], self.ABq_sl[0:32, 1, :],
                        ALU.mult),
                      r=[B(qbn), B("ABq"), B("t2")], w=[B("t2")], l=f"qr2{s}_{h}")
                    V(lambda e, rb=rb, cb=cb: e.tensor_tensor(
                        self.Qra[rb:rb + 32, cb, :], self.t1s[0:32, :], self.t2s[0:32, :], ALU.add),
                      r=[B("t1"), B("t2"), B("Qra")], w=[B("Qra")], l=f"qr3{s}_{h}")
                if "a" not in qskip:
                    S_(lambda e, po=po, hp=hp, qps=qps: e.activation(
                        self.sqX[0:64, hp, :], qps[0:64, po:po + 512], AF.Square),
                       r=[B(qbn), B(f"sqX{hp}")], w=[B(f"sqX{hp}")], l=f"qs{s}_{h}")

                    def aug(hp=hp, rb=rb, cb=cb, s=s, h=h):
                        T(lambda e, hp=hp: e.matmul(
                            self.S4[0:1, 1536:2048], self.ones_b[0:64, 0:1],
                            self.sqX[0:64, hp, :], start=True, stop=True),
                          r=[B(f"sqX{hp}"), B("ones"), B("pS3")], w=[B("pS3")],
                          l=f"qn{s}_{h}")
                        V(lambda e, rb=rb, cb=cb: e.tensor_scalar(
                            self.Qra[rb + 32:rb + 33, cb, :], self.S4[0:1, 1536:2048],
                            self.mxk[0:1, 4:5], -0.5, ALU.add, ALU.mult),
                          r=[B("pS3"), B("mxk"), B("Qra")], w=[B("Qra")], l=f"eqa{s}_{h}")
                    aug_q.append(aug)
                    if len(aug_q) > 1:
                        aug_q.pop(0)()
            flush_aug()
            emit_reloads(s + 1)

            if mstop == "q1":
                continue
            for j in range(4):
                def avs(kt, first, last):
                    ep = kt % 2
                    for m in range(4):
                        h = 4 * j + m
                        ypt = (self.PAB, self.PAB, self.Y2, self.Y2)[m]
                        yb = 512 * (m % 2)
                        ybn = ("pA0", "pA1", "pY0", "pY1")[m]
                        T(lambda e, m=m, kt=kt, h=h, ypt=ypt, yb=yb, ep=ep,
                          first=first, last=last: e.matmul(
                              ypt[0:33, yb:yb + 512],
                              self.ap(self.kvt, [128, self.NKT, HEADS * 33], 0, 128,
                                      kt * HEADS * 33 + 33 * h, [[1, 33]]),
                              self.expT[:, ep, m, :], start=first, stop=last),
                          r=[B(f"ex{ep}{m}"), B("kvt")] + ([B(ybn)] if first else []),
                          w=[B(ybn)], l=f"a{s}_{j}_{kt}_{m}")
                for kt in range(L):
                    mskd = (kt >= 8 * s)
                    ep = kt % 2
                    for m in range(4):
                        h = 4 * j + m
                        so = 512 * m
                        psn = f"pS{m}"
                        T(lambda e, m=m, j=j, kt=kt, so=so: e.matmul(
                            self.S4[0:128, so:so + 512],
                            self.kvpack[32 * m:32 * m + 32, j, kt * KT:(kt + 1) * KT],
                            self.Qnope[32 * m:32 * m + 32, j, :],
                            start=True, stop=False, tile_position=(32 * m, 0)),
                          r=[B("kvp"), B("Qnope"), B(psn)], w=[B(psn)], l=f"n{s}_{j}_{kt}_{m}")
                        rb = 64 * (m % 2)
                        cb = 2 * j + m // 2
                        T(lambda e, rb=rb, cb=cb, kt=kt, so=so: e.matmul(
                            self.S4[0:128, so:so + 512],
                            self.krass[rb:rb + 64, kt * KT:(kt + 1) * KT],
                            self.Qra[rb:rb + 64, cb, :],
                            start=False, stop=True, tile_position=(rb, 0)),
                          r=[B("krass"), B("Qra")], w=[B(psn)], l=f"r{s}_{j}_{kt}_{m}")
                        if m % 2 == 1:
                            S_(lambda e, m=m, kt=kt, sl_off=sl_off, ep=ep: e.activation(
                                self.ap(self.expT, [128, 2, 4, QB], 0, 128,
                                        (ep * 4 + m - 1) * QB, [[1, 2 * QB]]),
                                self.S4[0:128, (m - 1) * 512:(m + 1) * 512], AF.Exp,
                                bias=self.biasKsl[:, sl_off + kt:sl_off + kt + 1], scale=0.125),
                               r=[B(f"pS{m-1}"), B(psn), B("biasKsl"),
                                  B(f"ex{ep}{m-1}"), B(f"ex{ep}{m}")],
                               w=[B(f"ex{ep}{m-1}"), B(f"ex{ep}{m}")],
                               l=f"x{s}_{j}_{kt}_{m}")
                    if mskd:
                        md = kt - 8 * s
                        V(lambda e, md=md, ep=ep: e.tensor_tensor(
                            self.ap(self.expT, [128, 2, 4, QB], 0, 128, ep * 4 * QB,
                                    [[1, 4 * QB]]),
                            self.ap(self.expT, [128, 2, 4, QB], 0, 128, ep * 4 * QB,
                                    [[1, 4 * QB]]),
                            self.ap(self.masks_s, [128, 8, QB], 0, 128, md * QB,
                                    [[0, 4], [1, QB]]),
                            ALU.mult),
                          r=[B("masks")] + [B(f"ex{ep}{m}") for m in range(4)],
                          w=[B(f"ex{ep}{m}") for m in range(4)], l=f"m{s}_{j}_{kt}")
                    if kt > 0:
                        avs(kt - 1, kt - 1 == 0, False)
                avs(L - 1, L == 1, True)
                ypts = (self.PAB, self.PAB, self.Y2, self.Y2)
                ybs = (0, 512, 0, 512)
                ybns = ("pA0", "pA1", "pY0", "pY1")
                for m in range(4):
                    V(lambda e, m=m: e.tensor_copy(
                        self.rDc[32 * m:32 * m + 1, :], ypts[m][32:33, ybs[m]:ybs[m] + 512]),
                      r=[B(ybns[m]), B("rDc")], w=[B("rDc")], l=f"rd{s}_{j}_{m}")
                T(lambda e: e.matmul(self.S4[0:128, 1536:2048], self.sel32[:, :],
                                     self.rDc[:, :], start=True, stop=True),
                  r=[B("rDc"), B("sel32"), B("pS3")], w=[B("pS3")], l=f"bc{s}_{j}")
                V(lambda e: e.reciprocal(self.Bbc4[:, :], self.S4[0:128, 1536:2048]),
                  r=[B("pS3"), B("Bbc")], w=[B("Bbc")], l=f"ebc{s}_{j}")
                for m in range(4):
                    V(lambda e, m=m, j=j: e.tensor_tensor(
                        self.yasm[32 * m:32 * m + 32, j, :],
                        ypts[m][0:32, ybs[m]:ybs[m] + 512],
                        self.Bbc4[32 * m:32 * m + 32, :], ALU.mult),
                      r=[B(ybns[m]), B("Bbc"), B("yasm")], w=[B("yasm")],
                      l=f"ny{s}_{j}_{m}")

            if mstop == "att1":
                continue
            for s_ in range(4):
                t0_ = 2 * s * QB + s_ * 128
                t1_ = (2 * s + 1) * QB + s_ * 128
                DS(lambda e, t0_=t0_: e.dma_start(
                    self.x1su[:, :], self.x1_sp[t0_:t0_ + 128, :]),
                   r=[B("x1sp"), B("x1su")], w=[B("x1su")], l=f"lx1A{s}_{s_}")
                DS(lambda e, t1_=t1_: e.dma_start(
                    self.x1s2[:, :], self.x1_sp[t1_:t1_ + 128, :]),
                   r=[B("x1sp"), B("x1s2")], w=[B("x1s2")], l=f"lx1B{s}_{s_}")
                V(lambda e: e.tensor_tensor(self.x1su[:, :], self.x1su[:, :],
                                            self.x1s2[:, :], ALU.add),
                  r=[B("x1s2"), B("x1su")], w=[B("x1su")], l=f"sx1{s}_{s_}")
                for j in range(4):
                    for half in range(2):
                        T(lambda e, s_=s_, j=j, half=half: e.matmul(
                            self.PAB[0:128, half * 512:(half + 1) * 512],
                            self.yasm[:, j, s_ * 128:(s_ + 1) * 128],
                            self.WoE[:, j, half * 512:(half + 1) * 512],
                            start=(j == 0), stop=(j == 3)),
                          r=[B("yasm"), B("WoE")] + ([B(f"pA{half}")] if j == 0 else []),
                          w=[B(f"pA{half}")], l=f"mo{s}_{s_}_{j}_{half}")
                V(lambda e: e.scalar_tensor_tensor(
                    self.vscr[:, :], self.PAB[0:128, 0:1024], 1.0, self.x1su[:, :],
                    ALU.mult, ALU.add),
                  r=[B("pA0"), B("pA1"), B("x1su"), B("vscr")], w=[B("vscr")], l=f"y1{s}_{s_}")
                V(lambda e: e.tensor_tensor(self.xto[:, :], self.vscr[:, :],
                                            self.x1su[:, :], ALU.add),
                  r=[B("vscr"), B("x1su"), B("xto")], w=[B("xto")], l=f"g{s}_{s_}")
                DS(lambda e, s=s, s_=s_: e.dma_start(
                    self.g_sp[(s * 4 + s_) * 128:(s * 4 + s_ + 1) * 128, :], self.xto[:, :]),
                   r=[B("xto")], w=[B("gsp")], l=f"spg{s}_{s_}")
                V(lambda e: e.scalar_tensor_tensor(self.sqscr[:, :], self.vscr[:, :], 1.0,
                                                   self.vscr[:, :], ALU.mult, ALU.mult,
                                                   accum_out=self.stc[:, 9:10]),
                  r=[B("vscr"), B("sqscr"), B("stc")], w=[B("sqscr"), B("stc")], l=f"y2{s}_{s_}")
                V(lambda e: e.tensor_scalar(self.stc[:, 10:11], self.stc[:, 9:10], 1.0 / D,
                                            None, ALU.mult),
                  r=[B("stc")], w=[B("stc")], l=f"ms{s}_{s_}")
                S_(lambda e: e.activation(self.stc[:, 11:12], self.stc[:, 10:11], AF.Ln,
                                          bias=EPS),
                   r=[B("stc"), B("econst")], w=[B("stc")], l=f"l2{s}_{s_}")
                S_(lambda e: e.activation(self.stc[:, 11:12], self.stc[:, 11:12], AF.Exp,
                                          scale=-0.5),
                   r=[B("stc")], w=[B("stc")], l=f"r2{s}_{s_}")
                xp = s_ % 2
                V(lambda e, xp=xp: e.tensor_scalar(self.xnb[:, xp, :], self.vscr[:, :],
                                                   self.stc[:, 11:12], None, ALU.mult),
                  r=[B("vscr"), B("stc"), B(f"xnb{xp}")], w=[B(f"xnb{xp}")], l=f"x2{s}_{s_}")
                so_ = s * 4 + s_
                DG(lambda e, so_=so_, xp=xp: e.dma_start(
                    self.xn2_sp[so_ * 128:(so_ + 1) * 128, :], self.xnb[:, xp, :]),
                   r=[B(f"xnb{xp}")], w=[B("xn2sp")], l=f"sp2{s}_{s_}")
            sl_off += L

    def phase_moe(self):
        B, T, V, S_, G, DS, DG = self.B, self.T, self.V, self.S_, self.G, self.DS, self.DG
        i = self.inp
        NOS = self.NOS
        def emit_rebuild(ss):
            xp = ss % 4
            DG(lambda e, ss=ss, xp=xp: e.dma_start(
                self.xe[:, xp, :], self.xn2_sp[ss * 128:(ss + 1) * 128, :]),
               r=[B("xn2sp"), B(f"xe{xp}")], w=[B(f"xe{xp}")], l=f"lx2{ss}")
            DS(lambda e, ss=ss, xp=xp: e.dma_start_transpose(
                self.ap(self.xn2f, [128, self.NOS, NCH, 128], 0, 128, ss * NCH * 128,
                        [[128, NCH], [1, 128]]),
                self.xe[:, xp, :]),
               r=[B(f"xe{xp}"), B("xn2f")], w=[B("xn2f")], l=f"tx2{ss}")
            for c in range(NCH):
                V(lambda e, ss=ss, c=c: e.tensor_scalar(
                    self.xn2f[:, ss, c, :], self.xn2f[:, ss, c, :],
                    self.wln2_c[:, c:c + 1], None, ALU.mult),
                  r=[B("xn2f"), B("w_ln2")], w=[B("xn2f")], l=f"w2{ss}_{c}")

        def emit_gate(ss):
            for c in range(NCH):
                T(lambda e, ss=ss, c=c: e.matmul(
                    self.S4[0:128, 1536:1536 + EE],
                    self.ap(self.xn2f, [128, NOS, NCH, 128], 0, 128, (ss * NCH + c) * 128,
                            [[1, 128]]),
                    self.Wgate_s[:, c, :], start=(c == 0), stop=(c == NCH - 1)),
                  r=[B("xn2f"), B("Wgate")] + ([B("pS3")] if c == 0 else []),
                  w=[B("pS3")], l=f"gm{ss}_{c}")
            gp = self.gpr
            V(lambda e: e.tensor_tensor(gp[:, 0, :], self.S4[0:128, 1536:1536 + EE],
                                        self.ebias_b[:, :], ALU.add),
              r=[B("pS3"), B("ebias"), B("gp")], w=[B("gp")], l=f"gb{ss}")
            V(lambda e: e.reduce_max(self.gcol[:, 0:1], gp[:, 0, :], axis=AX.X, negate=True),
              r=[B("gp"), B("gc")], w=[B("gc")], l=f"gx{ss}")
            S_(lambda e: e.activation(gp[:, 1, :], gp[:, 0, :], AF.Exp, bias=self.gcol[:, 0:1]),
               r=[B("gp"), B("gc")], w=[B("gp")], l=f"ge{ss}")
            V(lambda e: e.reduce_max(self.gcol[:, 1:2], gp[:, 1, :], axis=AX.X),
              r=[B("gp"), B("gc")], w=[B("gc")], l=f"m1{ss}")
            V(lambda e: e.tensor_scalar(gp[:, 2, :], gp[:, 1, :], self.gcol[:, 1:2], None,
                                        ALU.is_ge),
              r=[B("gp"), B("gc")], w=[B("gp")], l=f"q1{ss}")
            V(lambda e: e.tensor_tensor(gp[:, 3, :], gp[:, 1, :], gp[:, 2, :], ALU.mult),
              r=[B("gp")], w=[B("gp")], l=f"tt{ss}")
            V(lambda e: e.tensor_tensor(gp[:, 3, :], gp[:, 1, :], gp[:, 3, :], ALU.subtract),
              r=[B("gp")], w=[B("gp")], l=f"e2{ss}")
            V(lambda e: e.reduce_max(self.gcol[:, 2:3], gp[:, 3, :], axis=AX.X),
              r=[B("gp"), B("gc")], w=[B("gc")], l=f"m2{ss}")
            V(lambda e: e.tensor_scalar(gp[:, 3, :], gp[:, 3, :], self.gcol[:, 2:3], None,
                                        ALU.is_ge),
              r=[B("gp"), B("gc")], w=[B("gp")], l=f"q2{ss}")
            V(lambda e: e.tensor_tensor(gp[:, 2, :], gp[:, 2, :], gp[:, 3, :], ALU.add),
              r=[B("gp")], w=[B("gp")], l=f"qs{ss}")
            V(lambda e: e.tensor_tensor(gp[:, 2, :], gp[:, 1, :], gp[:, 2, :], ALU.mult),
              r=[B("gp")], w=[B("gp")], l=f"nu{ss}")
            V(lambda e: e.tensor_tensor(self.gcol[:, 3:4], self.gcol[:, 1:2],
                                        self.gcol[:, 2:3], ALU.add),
              r=[B("gc")], w=[B("gc")], l=f"de{ss}")
            V(lambda e: e.reciprocal(self.gcol[:, 4:5], self.gcol[:, 3:4]),
              r=[B("gc")], w=[B("gc")], l=f"rr{ss}")
            V(lambda e, ss=ss: e.tensor_scalar(self.comb_t[:, ss, :], gp[:, 2, :],
                                               self.gcol[:, 4:5], None, ALU.mult),
              r=[B("gp"), B("gc"), B("cmb")], w=[B("cmb")], l=f"cm{ss}")
            T(lambda e, ss=ss: e.transpose(self.Y2[0:8, 0:128], self.comb_t[:, ss, :],
                                           self.iden_f[:]),
              r=[B("cmb"), B("iden"), B("pY0")], w=[B("pY0")], l=f"ct{ss}")
            V(lambda e: e.tensor_copy(self.comb_st[:, :], self.Y2[0:8, 0:128]),
              r=[B("pY0"), B("cst")], w=[B("cst")], l=f"ec{ss}")
            DS(lambda e, ss=ss: e.dma_start(self.comb_dram[:, ss * 128:(ss + 1) * 128],
                                            self.comb_st[:, :]),
               r=[B("cst")], w=[B("cfd")], l=f"wcf{ss}")

        def emit_expert_blk(ei, eidx, pp, blk):
            if eidx is not None:
                DS(lambda e, eidx=eidx, blk=blk: e.dma_start(
                    self.comb_row[:, :],
                    self.comb_dram[eidx, blk * 512:(blk + 1) * 512][None, :]),
                   r=[B("cfd"), B("crow")], w=[B("crow")], l=f"lcf{ei}_{blk}")
                T(lambda e: e.matmul(
                    self.Y2[0:128, 0:512], self.ones_f[0:1, :],
                    self.comb_row[:, :], start=True, stop=True),
                  r=[B("crow"), B("onesf"), B("pY0")], w=[B("pY0")], l=f"be{ei}_{blk}")
                V(lambda e: e.tensor_copy(self.Be[:, :], self.Y2[0:128, 0:512]),
                  r=[B("pY0"), B("Be")], w=[B("Be")], l=f"eb{ei}_{blk}")
            for ic in range(4):
                gso = 1024 * (ic % 2)
                gb = f"pS{2 * (ic % 2)}"
                ub = f"pS{2 * (ic % 2) + 1}"
                for c in range(NCH):
                    T(lambda e, ic=ic, c=c, pp=pp, blk=blk, gso=gso: e.matmul(
                        self.S4[0:128, gso:gso + 512],
                        self.ap(self.Wg_s, [128, 2, NCH, II], 0, 128,
                                (pp * NCH + c) * II + ic * 128, [[1, 128]]),
                        self.ap(self.xn2f, [128, self.NOS, NCH, 128], 0, 128,
                                blk * 4 * NCH * 128 + c * 128, [[NCH * 128, 4], [1, 128]]),
                        start=(c == 0), stop=(c == NCH - 1)),
                      r=[B(f"wg{pp}"), B("xn2f")] + ([B(gb)] if c == 0 else []),
                      w=[B(gb)], l=f"hg{ei}_{blk}_{ic}_{c}")
                S_(lambda e, ic=ic, gso=gso, gb=gb: e.activation(
                    self.sg[:, ic, :], self.S4[0:128, gso:gso + 512], AF.Silu),
                   r=[B(gb), B("sg")], w=[B("sg")], l=f"si{ei}_{blk}_{ic}")
                for c in range(NCH):
                    T(lambda e, ic=ic, c=c, pp=pp, blk=blk, gso=gso: e.matmul(
                        self.S4[0:128, gso + 512:gso + 1024],
                        self.ap(self.Wu_s, [128, 2, NCH, II], 0, 128,
                                (pp * NCH + c) * II + ic * 128, [[1, 128]]),
                        self.ap(self.xn2f, [128, self.NOS, NCH, 128], 0, 128,
                                blk * 4 * NCH * 128 + c * 128, [[NCH * 128, 4], [1, 128]]),
                        start=(c == 0), stop=(c == NCH - 1)),
                      r=[B(f"wu{pp}"), B("xn2f")] + ([B(ub)] if c == 0 else []),
                      w=[B(ub)], l=f"hu{ei}_{blk}_{ic}_{c}")
                if eidx is None:
                    V(lambda e, ic=ic, gso=gso, ub=ub: e.tensor_copy(
                        self.hu[:, ic, :], self.S4[0:128, gso + 512:gso + 1024]),
                      r=[B(ub), B("hu")], w=[B("hu")], l=f"eu{ei}_{blk}_{ic}")
                else:
                    V(lambda e, ic=ic, gso=gso, ub=ub: e.tensor_tensor(
                        self.hu[:, ic, :], self.S4[0:128, gso + 512:gso + 1024],
                        self.Be[:, :], ALU.mult),
                      r=[B(ub), B("Be"), B("hu")], w=[B("hu")], l=f"eu{ei}_{blk}_{ic}")
                V(lambda e, ic=ic: e.tensor_tensor(self.gu[:, ic, :], self.sg[:, ic, :],
                                                   self.hu[:, ic, :], ALU.mult),
                  r=[B("sg"), B("hu"), B("gu")], w=[B("gu")], l=f"gu{ei}_{blk}_{ic}")
            for s4 in range(4):
                sub = blk * 4 + s4
                dps = self.PAB if s4 % 2 == 0 else self.Y2
                db0 = "pA0" if s4 % 2 == 0 else "pY0"
                db1 = "pA1" if s4 % 2 == 0 else "pY1"
                for ic in range(4):
                    for half in range(2):
                        T(lambda e, ic=ic, s4=s4, half=half, pp=pp, dps=dps: e.matmul(
                            dps[0:128, half * 512:(half + 1) * 512],
                            self.gu[:, ic, s4 * 128:(s4 + 1) * 128],
                            self.ap(self.Wd_s, [128, 2, 4, D], 0, 128,
                                    (pp * 4 + ic) * D + half * 512, [[1, 512]]),
                            start=(ic == 0), stop=(ic == 3)),
                          r=[B("gu"), B(f"wd{pp}")] + ([B(db0 if half == 0 else db1)] if ic == 0 else []),
                          w=[B(db0 if half == 0 else db1)], l=f"wd{ei}_{blk}_{s4}_{ic}_{half}")
                V(lambda e, sub=sub, dps=dps: e.tensor_tensor(self.acc[:, sub, :],
                                                              dps[0:128, 0:1024],
                                                              self.acc[:, sub, :], ALU.add),
                  r=[B(db0), B(db1), B("acc")], w=[B("acc")], l=f"ac{ei}_{blk}_{s4}")

        def emit_wload(ei, pp, wg_src, wu_src, wd_src, sl):
            DG(lambda e, wg_src=wg_src, sl=sl, pp=pp: e.dma_start(
                self.Wg_s[:, pp, :, :], wg_src[sl].rearrange("(c p) i -> p c i", p=128)),
               r=[B(f"wg{pp}")], w=[B(f"wg{pp}")], l=f"lg{ei}")
            DG(lambda e, wu_src=wu_src, sl=sl, pp=pp: e.dma_start(
                self.Wu_s[:, pp, :, :], wu_src[sl].rearrange("(c p) i -> p c i", p=128)),
               r=[B(f"wu{pp}")], w=[B(f"wu{pp}")], l=f"lu{ei}")
            DG(lambda e, wd_src=wd_src, sl=sl, pp=pp: e.dma_start(
                self.Wd_s[:, pp, :, :], wd_src[sl].rearrange("(c p) d -> p c d", p=128)),
               r=[B(f"wd{pp}")], w=[B(f"wd{pp}")], l=f"ld{ei}")

        NBLK = self.OWN_T // QB
        emit_wload(0, 0, i["sWg"], i["sWu"], i["sWd"], 0)
        for blk in range(NBLK):
            DS(lambda e, blk=blk: e.dma_start(
                self.acc[:, blk * 4:(blk + 1) * 4, :],
                self.g_sp[blk * 512:(blk + 1) * 512].rearrange("(k p) d -> p k d", p=128)),
               r=[B("gsp"), B("acc")], w=[B("acc")], l=f"ldG{blk}")
            for s4 in range(4):
                ss = blk * 4 + s4
                emit_rebuild(ss)
                emit_gate(ss)
            emit_expert_blk(0, None, 0, blk)
        for ex in range(EE):
            ei = ex + 1
            pp = ei % 2
            emit_wload(ei, pp, i["Wg"], i["Wu"], i["Wd"], ex)
            for blk in range(NBLK):
                emit_expert_blk(ei, ex, pp, blk)

    def phase_final(self):
        B, T, V, S_, DS = self.B, self.T, self.V, self.S_, self.DS
        st = self.stc
        for sub in range(self.NOS):
            V(lambda e, sub=sub: e.reduce_sum(st[:, 8:9], self.acc[:, sub, :], axis=AX.X),
              r=[B("acc"), B("stc")], w=[B("stc")], l=f"fs{sub}")
            V(lambda e, sub=sub: e.scalar_tensor_tensor(
                self.sqscr[:, :], self.acc[:, sub, :], 1.0, self.acc[:, sub, :],
                ALU.mult, ALU.mult, accum_out=st[:, 9:10]),
              r=[B("acc"), B("sqscr"), B("stc")], w=[B("sqscr"), B("stc")], l=f"f2{sub}")
            V(lambda e: e.tensor_scalar(st[:, 10:11], st[:, 8:9], 1.0 / D, None, ALU.mult),
              r=[B("stc")], w=[B("stc")], l=f"fm{sub}")
            V(lambda e: e.tensor_tensor(st[:, 11:12], st[:, 10:11], st[:, 10:11], ALU.mult),
              r=[B("stc")], w=[B("stc")], l=f"f3{sub}")
            V(lambda e: e.scalar_tensor_tensor(st[:, 11:12], st[:, 9:10], 1.0 / D,
                                               st[:, 11:12], ALU.mult, ALU.subtract),
              r=[B("stc")], w=[B("stc")], l=f"fv{sub}")
            S_(lambda e: e.activation(st[:, 12:13], st[:, 11:12], AF.Ln, bias=EPS),
               r=[B("stc"), B("econst")], w=[B("stc")], l=f"fl{sub}")
            S_(lambda e: e.activation(st[:, 12:13], st[:, 12:13], AF.Exp, scale=-0.5),
               r=[B("stc")], w=[B("stc")], l=f"fr{sub}")
            V(lambda e: e.scalar_tensor_tensor(st[:, 15:16], st[:, 10:11], -1.0,
                                               st[:, 12:13], ALU.mult, ALU.mult),
              r=[B("stc")], w=[B("stc")], l=f"fn{sub}")
            S_(lambda e, sub=sub: e.activation(self.xto[:, :], self.acc[:, sub, :], AF.Identity,
                                               bias=st[:, 15:16], scale=st[:, 12:13]),
               r=[B("acc"), B("stc"), B("xto")], w=[B("xto")], l=f"fo{sub}")
            DS(lambda e, sub=sub: e.dma_start(self.out[sub * 128:(sub + 1) * 128, :],
                                              self.xto[:, :]),
               r=[B("xto")], w=[B("outd")], l=f"o{sub}")
        self.p.barrier(list(self._bufs.values()))

    def build(self, stop_after=None):
        self.declare_io()
        self.alloc_persist()
        self.nc.const_aps.register(F32, EPS, self.econst[:, 0:1])
        self.alloc_outer()
        self.xn1f_oa = self.sb("xn1f_oa", [128, 4, NCH, 128], BF16)
        self.phase_prep()
        if stop_after != "prep":
            self.phase_outer()
            if self.debug:
                for nm, bufn, tile, shape, dt in (
                        ("qA", "qA", self.qA, [128, self.S], BF16),
                        ("kA", "kA", self.kA, [128, self.S], BF16),
                        ("vt", "vt", self.vt, [128, self.NKT, 129], BF16),
                        ("krass", "krass", self.krass, [128, self.S], BF16),
                        ("kvpack", "kvp", self.kvpack, [128, 4, self.S], BF16),
                        ("kvt", "kvt", self.kvt, [128, self.NKT, HEADS * 33], BF16)):
                    t = self.nc.dram_tensor("dbg_" + nm, shape, dt,
                                            kind="ExternalOutput").ap()
                    self.p.add("sync", lambda e, t=t, tile=tile: e.dma_start(t[:], tile[:]),
                               reads=(self.B(bufn),), writes=(self.B("dbg"),),
                               label=f"dump_{nm}", dma=True)
        if stop_after in (None, "moe") or (stop_after or "").startswith("mla"):
            self.free_arena(set(self.OUTER_NAMES + ["xn1f_oa"]))
            self.alloc_mla()
            ms = stop_after[4:] if (stop_after or "").startswith("mla_") else None
            self.phase_mla(mstop=ms)
        if stop_after in (None, "moe"):
            self.free_arena(set(self.MLA_NAMES))
            self.alloc_moe()
            self.phase_moe()
        if stop_after is None:
            self.phase_final()
        else:
            self.p.barrier(list(self._bufs.values()))
            self.p.add("vector", lambda e: e.memset(self.xto[:, :], 0.0),
                       reads=(), writes=(self.B("xto"),), label="dumz")
            for sub in range(self.NOS):
                self.p.add("sync", lambda e, sub=sub: e.dma_start(
                    self.out[sub * 128:(sub + 1) * 128, :], self.xto[:, :]),
                    reads=(self.B("xto"),), writes=(self.B("outd"),),
                    label=f"do{sub}", dma=True)
            self.p.barrier([self.B("outd")])
        self.p.emit_all(self.nc)
        for name, cm in reversed(self._arena):
            cm.__exit__(None, None, None)
        return self.nc


def _dims_n(dims):
    n = 1
    for d in dims:
        n *= d[1]
    return n



def host_tables(S, half):
    NQB = S // QB
    NSLOT = NQB // 2
    Ls = [8 * (s + 1) for s in range(NSLOT)]
    A, Bm = outer_tables(S)
    own_qbs = [2 * s + half for s in range(NSLOT)]
    pos = np.concatenate([np.arange(qb * QB, (qb + 1) * QB) for qb in own_qbs])
    Aq, Bq = mla_tables(pos, 4)
    Ak, Bk = mla_tables(np.arange(S), 1)
    orow = np.zeros((1, NQB), np.float32)
    for qb in own_qbs:
        orow[0, qb] = 1.0
    padrow = np.zeros((1, sum(Ls)), np.float32)
    off = 0
    for s in range(NSLOT):
        lim = 4 * (2 * s + half) + 4
        for j in range(Ls[s]):
            if j >= lim:
                padrow[0, off + j] = NEG
        off += Ls[s]
    masks = np.ones((128, 8, QB), np.float32)
    f = np.arange(QB)[None, :]
    pp = np.arange(128)[:, None]
    for j in range(8):
        if half == 0 and j < 4:
            masks[:, j, :] = (f - pp - 128 * j >= 0)
        if half == 1 and j >= 4:
            masks[:, j, :] = (f - pp - 128 * (j - 4) >= 0)
    return {
        "Aout": A, "Bout": Bm, "Aq": Aq, "Bq": Bq, "Ak": Ak, "Bk": Bk,
        "iden": np.eye(128, dtype=np.float32), "orow": orow, "padrow": padrow,
        "masks": masks.astype(BF),
    }


_nc_cache = {}


def kernel(L0_attn_x, L0_attn_wq, L0_attn_wk, L0_attn_wv, L0_attn_wo,
           L0_ffn_w_ln1, L0_ffn_Wq, L0_ffn_Wkva, L0_ffn_Wkvb, L0_ffn_Wo,
           L0_ffn_w_ln2, L0_ffn_Wgate, L0_ffn_expert_bias, L0_ffn_Wg,
           L0_ffn_Wu, L0_ffn_Wd, L0_ffn_sWg, L0_ffn_sWu, L0_ffn_sWd):
    x = np.ascontiguousarray(np.asarray(L0_attn_x, dtype=np.float32))
    nb, S, _ = x.shape
    weights = {
        "wq": L0_attn_wq, "wk": L0_attn_wk, "wv": L0_attn_wv, "wo": L0_attn_wo,
        "w_ln1": L0_ffn_w_ln1, "Wq": L0_ffn_Wq, "Wkva": L0_ffn_Wkva,
        "Wkvb": L0_ffn_Wkvb, "Wo": L0_ffn_Wo, "w_ln2": L0_ffn_w_ln2,
        "Wgate": L0_ffn_Wgate, "ebias": L0_ffn_expert_bias, "Wg": L0_ffn_Wg,
        "Wu": L0_ffn_Wu, "Wd": L0_ffn_Wd, "sWg": L0_ffn_sWg, "sWu": L0_ffn_sWu,
        "sWd": L0_ffn_sWd,
    }
    weights = {k: np.ascontiguousarray(np.asarray(v, dtype=np.float32))
               for k, v in weights.items()}
    if S not in _nc_cache:
        _nc_cache[S] = Builder(S).build()
    nc = _nc_cache[S]
    tabs = [host_tables(S, h) for h in range(2)]
    n_cores = 2 * nb
    in_maps = []
    for b in range(nb):
        for h in range(2):
            m = dict(weights)
            m["x"] = x[b]
            m.update(tabs[h])
            in_maps.append(m)
    res = run_bass_kernel_spmd(nc, in_maps, core_ids=list(range(n_cores)))
    out = np.empty((nb, S, D), np.float32)
    NSLOT = (S // QB) // 2
    for b in range(nb):
        for h in range(2):
            o = res.results[2 * b + h]["out"]
            for s in range(NSLOT):
                qb = 2 * s + h
                out[b, qb * QB:(qb + 1) * QB, :] = o[s * QB:(s + 1) * QB, :]
    return out
